# version 13
# baseline (speedup 1.0000x reference)
import contextlib
import math
import numpy as np
import concourse.bass as bass
import concourse.mybir as mybir
from concourse.bass_utils import run_bass_kernel_spmd

class _Op:
    __slots__ = ("fn", "deps", "inc", "inck", "tokv", "sem", "tag")

    def __init__(self, fn, deps, inc, inck, tokv, sem, tag):
        self.fn = fn
        self.deps = deps
        self.inc = inc
        self.inck = inck
        self.tokv = tokv
        self.sem = sem
        self.tag = tag


class Prog:
    ENG = ("pe", "dv", "ac", "gp", "sy")
    CHAIN = {"dv", "ac", "gp"}
    DMASEM = {"sy": "sy_dma", "gp": "gp_dma"}

    def __init__(self, nc, es):
        self.nc = nc
        self.es = es
        self.ops = {e: [] for e in self.ENG}
        self.last_compute = {e: None for e in self.ENG}
        self.sems = {}
        self.base_deps = []
        self._ccn = 0
        self.last_cc = None
        for name in ("pe_s", "dv_s", "ac_s", "gp_s", "sy_dma", "gp_dma"):
            self.sems[name] = [es.enter_context(nc.semaphore(name)), 0]

    def barrier(self):
        return [(n, s[1]) for n, s in self.sems.items()
                if s[1] > 0 and not n.startswith("cc")]

    def cc(self, fn, deps=(), tag=""):
        name = f"cc{self._ccn}"
        self._ccn += 1
        self.sems[name] = [self.es.enter_context(self.nc.semaphore(name)), 1]
        mydeps = [d for d in deps if d is not None] + list(self.base_deps)
        o = _Op(fn, mydeps, True, None, 1, name, tag)
        self.ops["gp"].append(o)
        self.last_cc = (name, 1)
        return (name, 1)

    def op(self, eng, fn, deps=(), tok=False, tag=""):
        sem = eng + "_s"
        inc = bool(tok) or (eng in self.CHAIN)
        mydeps = [d for d in deps if d is not None] + list(self.base_deps)
        if eng in self.CHAIN and self.last_compute[eng] is not None:
            mydeps.append(self.last_compute[eng])
        tokv = None
        if inc:
            self.sems[sem][1] += 1
            tokv = self.sems[sem][1]
        o = _Op(fn, mydeps, inc, 1, tokv, sem, tag)
        self.ops[eng].append(o)
        token = (sem, tokv) if inc else None
        if eng in self.CHAIN:
            self.last_compute[eng] = token
        return token

    def dma_on(self, eng, fn, sem_name, deps=(), tag=""):
        if sem_name not in self.sems:
            self.sems[sem_name] = [self.es.enter_context(self.nc.semaphore(sem_name)), 0]
        self.sems[sem_name][1] += 16
        tokv = self.sems[sem_name][1]
        mydeps = [d for d in deps if d is not None] + list(self.base_deps)
        o = _Op(fn, mydeps, True, 16, tokv, sem_name, tag)
        self.ops[eng].append(o)
        return (sem_name, tokv)

    def dma(self, eng, fn, deps=(), tag=""):
        sem = self.DMASEM[eng]
        self.sems[sem][1] += 16
        tokv = self.sems[sem][1]
        mydeps = [d for d in deps if d is not None] + list(self.base_deps)
        o = _Op(fn, mydeps, True, 16, tokv, sem, tag)
        self.ops[eng].append(o)
        return (sem, tokv)

    def emit(self, block):
        builders = {"pe": block.tensor, "dv": block.vector, "ac": block.scalar,
                    "gp": block.gpsimd, "sy": block.sync}
        semh = {n: s[0] for n, s in self.sems.items()}
        for eng in self.ENG:
            ops = self.ops[eng]
            if not ops:
                continue

            def body(e, ops=ops):
                hw = {}
                for o in ops:
                    need = {}
                    for (sn, v) in o.deps:
                        if v and v > need.get(sn, 0):
                            need[sn] = v
                    for sn, v in sorted(need.items()):
                        if v > hw.get(sn, 0):
                            e.wait_ge(semh[sn], v)
                            hw[sn] = v
                    ins = o.fn(e)
                    if o.inc:
                        if o.inck is None:
                            ins.then_inc(semh[o.sem])
                        else:
                            ins.then_inc(semh[o.sem], o.inck)
            builders[eng](body)

    def tail_wait(self, block, tokens):
        semh = {n: s[0] for n, s in self.sems.items()}

        def body(e):
            need = {}
            for (sn, v) in tokens:
                if v and v > need.get(sn, 0):
                    need[sn] = v
            for sn, v in sorted(need.items()):
                e.wait_ge(semh[sn], v)
        block.sync(body)


F32 = mybir.dt.float32
BF16 = mybir.dt.bfloat16
I32 = mybir.dt.int32
AF = mybir.ActivationFunctionType
ALU = mybir.AluOpType
AX = mybir.AxisListType

D = 1024
TQ = 2048
TK = 4096
NH = 16
CAP = 4096
SCALE = 1.0 / math.sqrt(96.0)
NCHUNK = 4
CH2 = TQ // NCHUNK
YCH = CAP // NCHUNK


def build_fused(dbg=(), upto=99):
    nc = bass.Bass("TRN2", target_bir_lowering=False, debug=False)
    I = lambda n, s, dt=F32: nc.dram_tensor(n, s, dt, kind="ExternalInput").ap()
    O = lambda n, s, dt=F32: nc.dram_tensor(n, s, dt, kind="ExternalOutput").ap()
    N = lambda n, s, dt=BF16: nc.dram_tensor(n, s, dt).ap()

    x_kv = I("x_kv", [TK, D])
    c_colT = I("c_colT", [128, 8])
    wmod = I("wmod", [D, 6 * D])
    WQ, WK, WV, WO = (I(n, [D, D]) for n in ("wq", "wk", "wv", "wo"))
    WR = I("wr", [D, 512])
    BQ, BK, BV, BO = (I(n, [1, D]) for n in ("bq", "bk", "bv", "bo"))
    BR = I("br", [1, 512])
    wg_t = I("wg_t", [128, 64])
    idf32 = I("idf32", [128, 128])
    idbf = I("idbf", [128, 128], BF16)
    onesf32 = I("onesf32", [1, 128])
    onesbf = I("onesbf", [1, 128], BF16)
    ones4k_bf = I("ones4k_bf", [1, TK], BF16)
    ut_bf = I("ut_bf", [128, 128], BF16)
    all1_bf = I("all1_bf", [128, 128], BF16)
    eq_k = I("eq_k", [128, 8 * 16], BF16)
    eq_r = I("eq_r", [128, 4 * 16], BF16)
    gtokp1 = I("gtokp1", [128, 16])
    iotaEi = I("iotaEi", [128, 8])
    w1t = I("w1t", [32 * 128, D])
    w2m = I("w2m", [4096, D])
    b1m = I("b1m", [128, 32])
    b2r = I("b2r", [1, D])
    s16 = I("s16", [16, 16])
    exr0 = I("exr0", [16, 1])
    exr1 = I("exr1", [16, 1])
    exs0 = I("exs0", [8, 16])
    exs1 = I("exs1", [8, 16])
    iotad = I("iotad", [8, 1])
    iota128 = I("iota128", [128, 1])
    rkoh = I("rkoh", [16, 8])

    outo = O("outo", [TQ, D])
    x2_dram = N("x2_dram", [TQ, D], F32)
    lsort_loc = N("lsort_loc", [CAP, 1], F32)
    cnt_loc = N("cnt_loc", [1, 16], F32)
    h2ch = [N(f"h2ch{c}", [CH2, D]) for c in range(NCHUNK)]
    h2_all = N("h2_all", [8 * TQ, D])
    lsort_all = N("lsort_all", [8 * CAP, 1], F32)
    cnt_all_d = N("cnt_all", [8, 16], F32)
    ylc = [N(f"ylc{c}", [YCH, D]) for c in range(NCHUNK)]
    y_all = N("y_all", [8 * CAP, D])
    dbg_out = {}
    for name, shp, dt in [
        ("d_mod", [1, 6 * D], F32), ("d_xn", [TK, D], BF16),
        ("d_kt", [D, TK], BF16), ("d_qt", [D, TQ], BF16),
        ("d_rt", [512, TK], BF16), ("d_vp", [TK, 1040], BF16),
        ("d_normk", [16, TK], BF16), ("d_negm", [16, TQ], BF16),
        ("d_attn", [128, 8 * TQ], BF16), ("d_logits", [TQ, 8], F32),
        ("d_gab", [128, D], F32), ("d_modraw", [1, 6 * D], F32),
        ("d_wops", [TQ, D], F32), ("d_wos", [128, 8 * D], BF16),
    ]:
        if name in dbg:
            dbg_out[name] = nc.dram_tensor(name, shp, dt, kind="ExternalOutput").ap()

    xn_dram = N("xn_dram", [TK, D])
    kt_dram = N("kt_dram", [D, TK])
    rt_dram = N("rt_dram", [512, TK])
    qt_dram = N("qt_dram", [D, TQ])
    vp_dram = N("vp_dram", [TK, 1040])
    negm_dram = N("negm_dram", [16, TQ])

    es = contextlib.ExitStack()
    sb = lambda n, s, dt=F32: es.enter_context(nc.sbuf_tensor(n, s, dt))
    P = Prog(nc, es)
    ST = []

    id_f = sb("id_f", [128, 128])
    ones_f = sb("ones_f", [1, 128])
    iotaE = sb("iotaE", [128, 8])
    rt_cols = sb("rt_cols", [128, 16 * 8])
    gbB = sb("gbB", [128, D])
    phA = contextlib.ExitStack()
    sa = lambda n, s, dt=F32: phA.enter_context(nc.sbuf_tensor(n, s, dt))
    id_b = sa("id_b", [128, 128], BF16)
    ones_b = sa("ones_b", [1, 128], BF16)
    ut_b = sa("ut_b", [128, 128], BF16)
    all1_b = sa("all1_b", [128, 128], BF16)
    eqk_b = sa("eqk_b", [128, 8 * 16], BF16)
    eqr_b = sa("eqr_b", [128, 4 * 16], BF16)
    gtok_f = sa("gtok_f", [128, 16])
    wg_b = sa("wg_b", [128, 64], BF16)
    caCol1 = sa("caCol1", [128, 8])
    saColb = sa("saColb", [128, 8], BF16)
    cb1B = sa("cb1B", [128, D])
    sbB = sa("sbB", [128, D])
    bvB = sa("bvB", [128, D])
    gabo_b = sa("gabo_b", [1, D], BF16)
    bqCol = sa("bqCol", [128, 8])
    bkCol = sa("bkCol", [128, 8])
    brCol = sa("brCol", [128, 4])
    normk = sa("normk", [16, TK], BF16)
    normq = sa("normq", [16, TQ], BF16)
    negm_s = sa("negm_s", [16, TQ], BF16)
    wo_s = sa("wo_s", [128, 8 * D], BF16)
    m0_s = sa("m0_s", [128, 16 * 8], BF16)
    m1_s = sa("m1_s", [128, 16 * 8], BF16)
    xbuf = sa("xbuf", [128, 2 * D])
    sqscr = sa("sqscr", [128, D], BF16)
    epscol = sa("epscol", [128, 1])
    c_s = sa("c_s", [128, 8])
    wgt_f = sa("wgt_f", [128, 64])
    xcbuf = sa("xcbuf", [128, 2 * D])
    xnstg = sa("xnstg", [128, 2 * D], BF16)
    stat = sa("stat", [128, 8])
    kmc = sa("kmc", [16, 2])
    phW = contextlib.ExitStack()
    wq_s = phW.enter_context(nc.sbuf_tensor("wq_s", [128, 8 * D], BF16))
    wk_s = phW.enter_context(nc.sbuf_tensor("wk_s", [128, 8 * D], BF16))
    wv_s = phW.enter_context(nc.sbuf_tensor("wv_s", [128, 8 * D], BF16))
    wr_s = phW.enter_context(nc.sbuf_tensor("wr_s", [128, 8 * 512], BF16))
    ph01 = contextlib.ExitStack()
    mod_row = ph01.enter_context(nc.sbuf_tensor("mod_row", [1, 6 * D], F32))
    brow = ph01.enter_context(nc.sbuf_tensor("brow", [1, 5 * D], F32))
    beff = ph01.enter_context(nc.sbuf_tensor("beff", [1, 2 * D], F32))
    gaB = ph01.enter_context(nc.sbuf_tensor("gaB", [128, D], F32))

    for a, b_ in [(id_f[:], idf32), (id_b[:], idbf), (ones_f[:], onesf32),
                  (ones_b[:], onesbf), (ut_b[:], ut_bf), (all1_b[:], all1_bf),
                  (eqk_b[:], eq_k), (eqr_b[:], eq_r), (gtok_f[:], gtokp1),
                  (iotaE[:], iotaEi),
                  (brow[0:1, 0:D], BQ), (brow[0:1, D:2 * D], BK),
                  (brow[0:1, 2 * D:3 * D], BV), (brow[0:1, 3 * D:4 * D], BO),
                  (brow[0:1, 4 * D:4 * D + 512], BR)]:
        P.dma("sy", lambda e, a=a, b_=b_: e.dma_start(out=a, in_=b_))
    tc1 = P.dma("sy", lambda e: e.dma_start(out=c_s[:], in_=c_colT))
    tc2 = P.dma("sy", lambda e: e.dma_start(out=wgt_f[:], in_=wg_t))
    P.op("dv", lambda e: e.tensor_copy(out=wg_b[:], in_=wgt_f[:]), deps=[tc2])

    ph0 = contextlib.ExitStack()
    wm_s = ph0.enter_context(nc.sbuf_tensor("wm_s", [128, 4096], BF16))
    mod_ps = ph0.enter_context(nc.psum_tensor("mod_ps", [1, 512], F32))
    tp_ps = ph0.enter_context(nc.psum_tensor("tp_ps", [128, 8], F32))
    bc_ps = ph0.enter_context(nc.psum_tensor("bc_ps", [128, 512], F32))
    sig = ph0.enter_context(nc.sbuf_tensor("sig", [128, 8], F32))
    silu_b = ph0.enter_context(nc.sbuf_tensor("silu_b", [128, 8], BF16))

    teps = P.op("dv", lambda e: e.memset(epscol[:], 1e-5))
    tsg = P.op("ac", lambda e: e.activation(out=sig[:], in_=c_s[:], func=AF.Exp,
                                            scale=-1.0), deps=[tc1])
    P.op("dv", lambda e: e.tensor_scalar_add(out=sig[:], in0=sig[:], scalar1=1.0),
         deps=[tsg])
    P.op("dv", lambda e: e.reciprocal(out=sig[:], in_=sig[:]))
    P.op("dv", lambda e: e.tensor_tensor(out=sig[:], in0=sig[:], in1=c_s[:], op=ALU.mult))
    tsilu = P.op("dv", lambda e: e.tensor_copy(out=silu_b[:], in_=sig[:]))

    mod_evac = []
    tm_prev = None
    for cb in range(12):
        tw = P.dma("gp", lambda e, cb=cb: e.dma_start(
            out=wm_s[:].rearrange("p (dc f) -> p dc f", dc=8),
            in_=wmod[:, cb * 512:(cb + 1) * 512].rearrange("(dc p) f -> p dc f", p=128)),
            deps=([tm_prev] if tm_prev else []))
        tm = None
        for dc in range(8):
            tm = P.op("pe", lambda e, dc=dc: e.matmul(
                mod_ps[:], silu_b[:, dc:dc + 1],
                wm_s[:, dc * 512:(dc + 1) * 512],
                start=(dc == 0), stop=(dc == 7)),
                deps=[tw, tsilu] + (mod_evac[-1:] if mod_evac else []), tok=(dc == 7))
        tm_prev = tm
        mod_evac.append(P.op("dv", lambda e, cb=cb: e.tensor_copy(
            out=mod_row[0:1, cb * 512:(cb + 1) * 512], in_=mod_ps[0:1, :]), deps=[tm]))

    tclamp = mod_evac[-1]
    for base in (2 * D, 5 * D):
        r = mod_row[0:1, base:base + D]
        P.op("dv", lambda e, r=r: e.tensor_scalar(
            out=beff[0:1, 0:D], in0=r, scalar1=0.0, scalar2=None, op0=ALU.is_ge))
        P.op("dv", lambda e: e.tensor_scalar(
            out=beff[0:1, 0:D], in0=beff[0:1, 0:D], scalar1=2.0, scalar2=-1.0,
            op0=ALU.mult, op1=ALU.add))
        P.op("dv", lambda e, r=r: e.tensor_tensor(
            out=beff[0:1, D:2 * D], in0=r, in1=beff[0:1, 0:D], op=ALU.mult))
        P.op("dv", lambda e: e.tensor_scalar_max(
            out=beff[0:1, D:2 * D], in0=beff[0:1, D:2 * D], scalar1=0.25))
        tclamp = P.op("dv", lambda e, r=r: e.tensor_tensor(
            out=r, in0=beff[0:1, 0:D], in1=beff[0:1, D:2 * D], op=ALU.mult))
    garow = mod_row[0:1, 2 * D:3 * D]
    gbrow = mod_row[0:1, 5 * D:6 * D]
    if "d_mod" in dbg_out:
        ST.append(P.dma("sy", lambda e: e.dma_start(out=dbg_out["d_mod"],
                                                    in_=mod_row[0:1, :]), deps=[tclamp]))

    tt = None
    for j in range(8):
        tt = P.op("pe", lambda e, j=j: e.transpose(
            tp_ps[:, j:j + 1], mod_row[0:1, D + j * 128:D + (j + 1) * 128], id_f[0:1, 0:1]),
            deps=[mod_evac[-1]], tok=(j == 7))
    tca = P.op("dv", lambda e: e.tensor_scalar_add(out=caCol1[:], in0=tp_ps[:], scalar1=1.0),
               deps=[tt])
    tt = None
    for j in range(8):
        tt = P.op("pe", lambda e, j=j: e.transpose(
            tp_ps[:, j:j + 1], mod_row[0:1, j * 128:(j + 1) * 128], id_f[0:1, 0:1]),
            deps=[tca], tok=(j == 7))
    tsa = P.op("dv", lambda e: e.tensor_copy(out=saColb[:], in_=tp_ps[:]), deps=[tt])

    tprev = tclamp
    for (dst, base, add1) in ((cb1B, 4 * D, True), (sbB, 3 * D, False),
                              (gaB, 2 * D, False), (gbB, 5 * D, False)):
        for half in range(2):
            tm = P.op("pe", lambda e, base=base, half=half: e.matmul(
                bc_ps[:], ones_f[:], mod_row[0:1, base + half * 512:base + (half + 1) * 512],
                start=True, stop=True), deps=[tprev, tclamp], tok=True)
            tprev = P.op("dv", lambda e, dst=dst, half=half, add1=add1:
                         (e.tensor_scalar_add(out=dst[:, half * 512:(half + 1) * 512],
                                              in0=bc_ps[:], scalar1=1.0) if add1 else
                          e.tensor_copy(out=dst[:, half * 512:(half + 1) * 512],
                                        in_=bc_ps[:])), deps=[tm])
    tgabo = P.op("dv", lambda e: e.tensor_tensor(
        out=gabo_b[0:1, :], in0=garow, in1=brow[0:1, 3 * D:4 * D], op=ALU.mult))
    if "d_gab" in dbg_out:
        ST.append(P.dma("sy", lambda e: e.dma_start(out=dbg_out["d_gab"], in_=gaB[:]),
                        deps=[tprev]))
    tiles_done = tprev
    ph0.close()
    if upto < 1:
        blk = es.enter_context(nc.Block())
        P.emit(blk)
        P.tail_wait(blk, ST)
        return nc, es

    ph1 = contextlib.ExitStack()
    bias_ps = ph1.enter_context(nc.psum_tensor("bias_ps", [1, 512], F32))
    tpb_ps = ph1.enter_context(nc.psum_tensor("tpb_ps", [128, 8], F32))
    bcb_ps = ph1.enter_context(nc.psum_tensor("bcb_ps", [128, 512], F32))

    wload = {}
    for (name, wsb, src) in (("q", wq_s, WQ), ("k", wk_s, WK), ("v", wv_s, WV),
                             ("r", wr_s, WR), ("o", wo_s, WO)):
        cw = wsb.shape[1] // 8
        wload[name] = [P.dma("gp", lambda e, wsb=wsb, src=src, dc=dc, cw=cw: e.dma_start(
            out=wsb[:, dc * cw:(dc + 1) * cw], in_=src[dc * 128:(dc + 1) * 128, :]))
            for dc in range(8)]

    bias_done = {}
    fold_done = []
    prev_bias_ps = None
    prev_tp = tsa
    prev_bc = tiles_done
    for (name, wsb, no, boff, col_dst) in (
            ("q", wq_s, 8, 0, bqCol), ("k", wk_s, 8, D, bkCol),
            ("r", wr_s, 4, 4 * D, brCol), ("v", wv_s, 8, 2 * D, None)):
        cw = no * 128
        ev = []
        nhalf = (no * 128) // 512
        for half in range(nhalf):
            tm = None
            for dc in range(8):
                tm = P.op("pe", lambda e, wsb=wsb, dc=dc, half=half, cw=cw: e.matmul(
                    bias_ps[:], saColb[:, dc:dc + 1],
                    wsb[:, dc * cw + half * 512:dc * cw + half * 512 + 512],
                    start=(dc == 0), stop=(dc == 7)),
                    deps=[wload[name][dc], tsa, prev_bias_ps], tok=(dc == 7))
            tev = P.op("dv", lambda e, boff=boff, half=half: e.tensor_tensor(
                out=beff[0:1, half * 512:(half + 1) * 512], in0=bias_ps[0:1, :],
                in1=brow[0:1, boff + half * 512:boff + (half + 1) * 512], op=ALU.add),
                deps=[tm])
            prev_bias_ps = tev
            ev.append(tev)
        if col_dst is not None:
            tt = None
            for j in range(no):
                tt = P.op("pe", lambda e, j=j: e.transpose(
                    tpb_ps[:, j:j + 1], beff[0:1, j * 128:(j + 1) * 128], id_f[0:1, 0:1]),
                    deps=ev + [prev_tp], tok=(j == no - 1))
            tcd = P.op("dv", lambda e, col_dst=col_dst, no=no: e.tensor_copy(
                out=col_dst[:, 0:no], in_=tpb_ps[:, 0:no]), deps=[tt])
            prev_tp = tcd
            bias_done[name] = tcd
        else:
            tb = None
            for half in range(2):
                tm = P.op("pe", lambda e, half=half: e.matmul(
                    bcb_ps[:], ones_f[:], beff[0:1, half * 512:(half + 1) * 512],
                    start=True, stop=True), deps=ev + [prev_bc], tok=True)
                tb = P.op("dv", lambda e, half=half: e.tensor_copy(
                    out=bvB[:, half * 512:(half + 1) * 512], in_=bcb_ps[:]), deps=[tm])
                prev_bc = tb
            bias_done[name] = tb
        for dc in range(8):
            fold_done.append(P.op("dv", lambda e, wsb=wsb, dc=dc, cw=cw: e.tensor_scalar(
                out=wsb[:, dc * cw:(dc + 1) * cw], in0=wsb[:, dc * cw:(dc + 1) * cw],
                scalar1=caCol1[:, dc:dc + 1], scalar2=None, op0=ALU.mult),
                deps=[bias_done[name], tca]))
    for dc in range(8):
        fold_done.append(P.op("dv", lambda e, dc=dc: e.tensor_tensor(
            out=wo_s[:, dc * D:(dc + 1) * D], in0=wo_s[:, dc * D:(dc + 1) * D],
            in1=gaB[:], op=ALU.mult), deps=[wload["o"][dc], tiles_done]))
    if "d_wos" in dbg_out:
        ST.append(P.dma("sy", lambda e: e.dma_start(out=dbg_out["d_wos"], in_=wo_s[:]),
                        deps=fold_done[-8:]))
    ph1.close()
    ph01.close()
    if upto < 2:
        blk = es.enter_context(nc.Block())
        P.emit(blk)
        P.tail_wait(blk, ST)
        return nc, es

    xn_done = []
    relL, relC, relN = {}, {}, {}
    for i in range(32):
        par = i % 2
        xb = xbuf[:, par * D:(par + 1) * D]
        xc = xcbuf[:, par * D:(par + 1) * D]
        xn = xnstg[:, par * D:(par + 1) * D]
        st = stat[:, par * 4:par * 4 + 4]
        tl = P.dma("sy", lambda e, i=i, xb=xb: e.dma_start(
            out=xb, in_=x_kv[i * 128:(i + 1) * 128, :]), deps=[relL.get(par)])
        P.op("dv", lambda e, xb=xb, st=st: e.reduce_sum(out=st[:, 0:1], in_=xb, axis=AX.X),
             deps=[tl])
        P.op("dv", lambda e, st=st: e.tensor_scalar(
            out=st[:, 1:2], in0=st[:, 0:1], scalar1=1.0 / D, scalar2=None, op0=ALU.mult))
        txc = P.op("dv", lambda e, xb=xb, xc=xc, st=st: e.tensor_scalar(
            out=xc, in0=xb, scalar1=st[:, 1:2], scalar2=None, op0=ALU.subtract),
            deps=[relC.get(par)])
        relL[par] = txc
        P.op("ac", lambda e, xc=xc, st=st: e.activation(
            out=sqscr[:], in_=xc, func=AF.Square, accum_out=st[:, 2:3]), deps=[txc])
        P.op("ac", lambda e, st=st: e.activation(
            out=st[:, 3:4], in_=st[:, 2:3], func=AF.Ln, scale=1.0 / D,
            bias=epscol[:, 0:1]), deps=[teps])
        trho = P.op("ac", lambda e, st=st: e.activation(
            out=st[:, 3:4], in_=st[:, 3:4], func=AF.Exp, scale=-0.5))
        txn = P.op("dv", lambda e, xc=xc, xn=xn, st=st: e.tensor_scalar(
            out=xn, in0=xc, scalar1=st[:, 3:4], scalar2=None, op0=ALU.mult),
            deps=[trho, relN.get(par)])
        relC[par] = txn
        ts = P.dma("sy", lambda e, i=i, xn=xn: e.dma_start(
            out=xn_dram[i * 128:(i + 1) * 128, :], in_=xn), deps=[txn])
        relN[par] = ts
        xn_done.append(ts)
    if "d_xn" in dbg_out:
        ST.append(P.dma("sy", lambda e: e.dma_start(out=dbg_out["d_xn"], in_=xn_dram),
                        deps=xn_done))
    if upto < 3:
        blk = es.enter_context(nc.Block())
        P.emit(blk)
        P.tail_wait(blk, ST)
        return nc, es

    ph3 = contextlib.ExitStack()
    xnT = ph3.enter_context(nc.sbuf_tensor("xnT", [128, 8 * TK], BF16))
    xnT_tok = [[P.dma("sy", lambda e, dcx=dcx, hf=hf: e.dma_start(
        out=xnT[:, dcx * TK + hf * 2048:dcx * TK + (hf + 1) * 2048],
        in_=xn_dram[hf * 2048:(hf + 1) * 2048, dcx * 128:(dcx + 1) * 128],
        transpose=True), deps=xn_done[hf * 16:hf * 16 + 16])
        for hf in range(2)] for dcx in range(8)]

    pj_ps = [ph3.enter_context(nc.psum_tensor(f"pj_ps{j}", [128, 512], F32)) for j in range(2)]
    nk_ps = ph3.enter_context(nc.psum_tensor("nk_ps", [16, 512], F32))
    nq_ps = ph3.enter_context(nc.psum_tensor("nq_ps", [16, 512], F32))
    vps = [[ph3.enter_context(nc.psum_tensor(f"v_ps{p}_{j}", [128, 512], F32))
            for j in range(2)] for p in range(2)]
    kstgs = ph3.enter_context(nc.sbuf_tensor("kstg", [128, 2 * 512], BF16))
    sqstg = ph3.enter_context(nc.sbuf_tensor("sqstg", [128, 2 * 512], BF16))
    vstg = ph3.enter_context(nc.sbuf_tensor("vstg", [128, 2 * 1040], BF16))

    relPP, relKS, relSQ = {}, {}, {}
    state = {"cnt": 0}
    kt_toks, qt_toks, v_toks = [], [], []
    nk_evac, nq_evac = [], []

    def one_proj(blkidx, wsb, cw, oc, dst, bcol, bcol_tok, eqt, nps, nst, nstop, nprev=None,
                 nps2=None, nst2=False, nstop2=False, nprev2=None):
        par = state["cnt"] % 2
        state["cnt"] += 1
        kst = kstgs[:, par * 512:(par + 1) * 512]
        sst = sqstg[:, par * 512:(par + 1) * 512]
        pp = pj_ps[par]
        tm = None
        for dc in range(8):
            tm = P.op("pe", lambda e, wsb=wsb, cw=cw, oc=oc, dc=dc, blkidx=blkidx, pp=pp: e.matmul(
                pp[:], wsb[:, dc * cw + oc * 128:dc * cw + (oc + 1) * 128],
                xnT[:, dc * TK + blkidx * 512:dc * TK + (blkidx + 1) * 512],
                start=(dc == 0), stop=(dc == 7)),
                deps=[xnT_tok[dc][blkidx // 4], relPP.get(par)] + fold_done,
                tok=(dc == 7))
        tev = P.op("dv", lambda e, pp=pp, kst=kst, bcol=bcol, oc=oc: e.tensor_scalar(
            out=kst, in0=pp[:], scalar1=bcol[:, oc:oc + 1], scalar2=None, op0=ALU.add),
            deps=[tm, bcol_tok, relKS.get(par)])
        relPP[par] = tev
        tsq = P.op("ac", lambda e, kst=kst, sst=sst: e.activation(
            out=sst, in_=kst, func=AF.Square), deps=[tev, relSQ.get(par)])
        tnm = P.op("pe", lambda e, eqt=eqt, oc=oc, sst=sst, nps=nps, nst=nst, nstop=nstop: e.matmul(
            nps[:], eqt[:, oc * 16:(oc + 1) * 16], sst, start=nst, stop=nstop,
            skip_group_check=True), deps=[tsq, nprev], tok=True)
        if nps2 is not None:
            tnm = P.op("pe", lambda e, eqt=eqt, oc=oc, sst=sst, nps2=nps2, nst2=nst2, nstop2=nstop2: e.matmul(
                nps2[:], eqt[:, oc * 16:(oc + 1) * 16], sst, start=nst2, stop=nstop2,
                skip_group_check=True), deps=[tsq, nprev2], tok=True)
        relSQ[par] = tnm
        tst = P.dma("sy", lambda e, dst=dst, oc=oc, blkidx=blkidx, kst=kst: e.dma_start(
            out=dst[oc * 128:(oc + 1) * 128, blkidx * 512:(blkidx + 1) * 512], in_=kst),
            deps=[tev])
        relKS[par] = tst
        return tst, tnm

    last_nq_r = [None]
    for blkidx in range(8):
        qside = blkidx < 4
        last_nk = None
        for oc in range(8):
            tst, last_nk = one_proj(blkidx, wk_s, D, oc, kt_dram, bkCol,
                                    bias_done["k"], eqk_b, nk_ps, oc == 0, False,
                                    nprev=(nk_evac[-1] if (oc == 0 and nk_evac) else None))
            kt_toks.append(tst)
        if qside:
            for oc in range(8):
                tst, _ = one_proj(blkidx, wq_s, D, oc, qt_dram, bqCol,
                                  bias_done["q"], eqk_b, nq_ps, oc == 0, False,
                                  nprev=(nq_evac[-1] if (oc == 0 and nq_evac) else None))
                qt_toks.append(tst)
        for oc in range(4):
            tst, last_nk = one_proj(blkidx, wr_s, 512, oc, rt_dram, brCol,
                                    bias_done["r"], eqr_b, nk_ps, False, oc == 3,
                                    nps2=(nq_ps if qside else None), nst2=False,
                                    nstop2=(oc == 3))
            kt_toks.append(tst)
            if qside:
                last_nq_r[0] = relSQ[(state["cnt"] - 1) % 2]
        nk_evac.append(P.op("dv", lambda e, blkidx=blkidx: e.tensor_copy(
            out=normk[:, blkidx * 512:(blkidx + 1) * 512], in_=nk_ps[:]), deps=[last_nk]))
        if qside:
            nq_evac.append(P.op("dv", lambda e, blkidx=blkidx: e.tensor_copy(
                out=normq[:, blkidx * 512:(blkidx + 1) * 512], in_=nq_ps[:]),
                deps=[last_nq_r[0]]))


    tmset = []
    relVV, relVS = {}, {}
    for tc in range(32):
        par = tc % 2
        vst = vstg[:, par * 1040:(par + 1) * 1040]
        if tc < 2:
            tmset.append(P.op("gp", lambda e, vst=vst: e.memset(vst, 1.0)))
        tms = []
        for half in range(2):
            tm = None
            for dc in range(8):
                tm = P.op("pe", lambda e, tc=tc, dc=dc, par=par, half=half: e.matmul(
                    vps[par][half][:], xnT[:, dc * TK + tc * 128:dc * TK + (tc + 1) * 128],
                    wv_s[:, dc * D + half * 512:dc * D + (half + 1) * 512],
                    start=(dc == 0), stop=(dc == 7)),
                    deps=[xnT_tok[dc][tc // 16], relVV.get((par, half))] + fold_done,
                    tok=(dc == 7))
            tms.append(tm)
        tev = None
        for half in range(2):
            tev = P.op("dv", lambda e, vst=vst, par=par, half=half: e.tensor_tensor(
                out=vst.rearrange("p (h j) -> p h j", j=65)[:, half * 8:(half + 1) * 8, 0:64],
                in0=vps[par][half][:].rearrange("p (h j) -> p h j", j=64),
                in1=bvB[:, half * 512:(half + 1) * 512].rearrange("p (h j) -> p h j", j=64),
                op=ALU.add),
                deps=[tms[half], bias_done["v"], relVS.get(par)] + (tmset if tc < 2 else []))
            relVV[(par, half)] = tev
        ts = P.dma("sy", lambda e, tc=tc, vst=vst: e.dma_start(
            out=vp_dram[tc * 128:(tc + 1) * 128, :], in_=vst), deps=[tev])
        relVS[par] = ts
        v_toks.append(ts)

    t = P.op("dv", lambda e: e.reduce_max(out=kmc[:, 0:1], in_=normk[:], axis=AX.X),
             deps=nk_evac)
    t = P.op("ac", lambda e: e.activation(out=kmc[:, 1:2], in_=kmc[:, 0:1], func=AF.Ln),
             deps=[t])
    t = P.op("ac", lambda e: e.activation(out=kmc[:, 0:1], in_=kmc[:, 1:2], func=AF.Exp,
                                          scale=0.5), deps=[t])
    t2 = P.op("ac", lambda e: e.activation(out=normq[:], in_=normq[:], func=AF.Ln),
              deps=nq_evac)
    t2 = P.op("ac", lambda e: e.activation(out=normq[:], in_=normq[:], func=AF.Exp,
                                           scale=0.5), deps=[t2])
    tneg = P.op("dv", lambda e: e.tensor_scalar(
        out=negm_s[:], in0=normq[:], scalar1=kmc[:, 0:1], scalar2=-1.0,
        op0=ALU.mult, op1=ALU.mult), deps=[t, t2])
    tnegst = P.dma("sy", lambda e: e.dma_start(out=negm_dram, in_=negm_s[:]), deps=[tneg])
    for nm, dr, tks in (("d_kt", kt_dram, kt_toks), ("d_rt", rt_dram, kt_toks),
                        ("d_qt", qt_dram, qt_toks), ("d_vp", vp_dram, v_toks),
                        ("d_normk", normk[:], nk_evac), ("d_negm", negm_s[:], [tneg])):
        if nm in dbg_out:
            ST.append(P.dma("sy", lambda e, nm=nm, dr=dr: e.dma_start(
                out=dbg_out[nm], in_=dr), deps=tks))
    if upto < 4:
        blk = es.enter_context(nc.Block())
        P.emit(blk)
        P.tail_wait(blk, ST)
        return nc, es

    ph3.close()
    phW.close()
    ph4a = contextlib.ExitStack()
    attnT = ph4a.enter_context(nc.sbuf_tensor("attnT", [128, 8 * TQ], BF16))
    ph4 = contextlib.ExitStack()
    kbuf = [ph4.enter_context(nc.sbuf_tensor(f"kbuf{j}", [128, TK], BF16)) for j in range(2)]
    qbuf = [ph4.enter_context(nc.sbuf_tensor(f"qbuf{j}", [128, TQ], BF16)) for j in range(2)]
    vbuf = [ph4.enter_context(nc.sbuf_tensor(f"vbuf{j}", [128, 32 * 260], BF16)) for j in range(2)]
    pstg = [ph4.enter_context(nc.sbuf_tensor(f"pstg{j}", [128, 1536], BF16)) for j in range(2)]
    rlb_s = ph4.enter_context(nc.sbuf_tensor("rlb_s", [64, 512], F32))
    rl_s = ph4.enter_context(nc.sbuf_tensor("rl_s", [1, 512], F32))
    s_ps = [ph4.enter_context(nc.psum_tensor(f"s_ps{j}", [128, 1536], F32)) for j in range(2)]
    av_ps = ph4.enter_context(nc.psum_tensor("av_ps", [65, 512], F32))
    rlb_ps = ph4.enter_context(nc.psum_tensor("rlb_ps", [64, 512], F32))

    GRPS = []
    k0 = 0
    while k0 < 32:
        GRPS.append((k0, min(k0 + 3, 32)))
        k0 += 3

    kq_load, v_load, head_rel, vg_rel = {}, {}, {}, {}
    ones_init = [P.dma("sy", lambda e, j=j: e.dma_start(
        out=kbuf[j][96:97, :], in_=ones4k_bf)) for j in range(2)]

    def load_head(h):
        par = h % 2
        d0 = [kt_toks[-1], qt_toks[-1], tnegst, head_rel.get(h - 2), ones_init[par]]
        toks = [
            P.dma("sy", lambda e, h=h, par=par: e.dma_start(
                out=kbuf[par][0:64, :], in_=kt_dram[h * 64:(h + 1) * 64, :]), deps=d0),
            P.dma("sy", lambda e, h=h, par=par: e.dma_start(
                out=kbuf[par][64:96, :], in_=rt_dram[h * 32:(h + 1) * 32, :]), deps=d0),
            P.dma("sy", lambda e, h=h, par=par: e.dma_start(
                out=qbuf[par][0:64, :], in_=qt_dram[h * 64:(h + 1) * 64, :]), deps=d0),
            P.dma("sy", lambda e, h=h, par=par: e.dma_start(
                out=qbuf[par][64:96, :], in_=rt_dram[h * 32:(h + 1) * 32, 0:TQ]), deps=d0),
            P.dma("sy", lambda e, h=h, par=par: e.dma_start(
                out=qbuf[par][96:97, :], in_=negm_dram[h:h + 1, :]), deps=d0),
        ]
        kq_load[h] = toks

    def load_vg(g):
        par = g % 2
        deps = [vg_rel.get(g - 2), v_toks[-1]]
        v_load[g] = [P.dma("sy", lambda e, g=g, par=par, piece=piece: e.dma_start(
            out=vbuf[par][:, piece * 8 * 260:(piece + 1) * 8 * 260].rearrange(
                "p (kc j) -> p kc j", j=260),
            in_=vp_dram.rearrange("(kc p) (h j) -> p kc h j", p=128, j=65)[
                :, piece * 8:(piece + 1) * 8, 4 * g:4 * g + 4, :].rearrange(
                "p kc h j -> p kc (h j)")), deps=deps) for piece in range(4)]

    load_head(0)
    load_head(1)
    load_vg(0)
    load_vg(1)
    sg_rel, pst_rel = {}, {}
    attn_evac = []
    for h in range(NH):
        par = h % 2
        g = h // 4
        gpar = g % 2
        for qb in range(4):
            av_last = None
            tS_l = {}

            def emit_qk(gi, h=h, qb=qb, par=par):
                g0, g1 = GRPS[gi]
                spar = gi % 2
                tS = None
                for kc in range(g0, g1):
                    tS = P.op("pe", lambda e, h=h, qb=qb, kc=kc, g0=g0, spar=spar, par=par: e.matmul(
                        s_ps[spar][:, (kc - g0) * 512:(kc - g0 + 1) * 512],
                        kbuf[par][0:97, kc * 128:(kc + 1) * 128],
                        qbuf[par][0:97, qb * 512:(qb + 1) * 512],
                        start=True, stop=True, skip_group_check=True),
                        deps=kq_load[h] + [sg_rel.get(spar)], tok=(kc == g1 - 1))
                tS_l[gi] = tS
            emit_qk(0)
            for gi, (g0, g1) in enumerate(GRPS):
                spar = gi % 2
                if gi + 1 < len(GRPS):
                    emit_qk(gi + 1)
                tE = P.op("ac", lambda e, spar=spar, n=(g1 - g0): e.activation(
                    out=pstg[spar][:, 0:n * 512], in_=s_ps[spar][:, 0:n * 512],
                    func=AF.Exp, scale=SCALE), deps=[tS_l[gi], pst_rel.get(spar)])
                sg_rel[spar] = tE
                for kc in range(g0, g1):
                    av_last = P.op("pe", lambda e, h=h, kc=kc, g0=g0, gpar=gpar, spar=spar: e.matmul(
                        av_ps[:],
                        vbuf[gpar][:, kc * 260 + (h % 4) * 65:kc * 260 + (h % 4) * 65 + 65],
                        pstg[spar][:, (kc - g0) * 512:(kc - g0 + 1) * 512],
                        start=(kc == 0), stop=(kc == 31), skip_group_check=True),
                        deps=[tE] + v_load[g] + (attn_evac[-1:] if attn_evac else []),
                        tok=(kc == 31))
                pst_rel[spar] = av_last
            trl = P.op("dv", lambda e: e.reciprocal(out=rl_s[0:1, :], in_=av_ps[64:65, :]),
                       deps=[av_last])
            tbm = P.op("pe", lambda e: e.matmul(rlb_ps[:], ones_f[0:1, 0:64], rl_s[0:1, :],
                                                start=True, stop=True, skip_group_check=True),
                       deps=[trl], tok=True)
            tbe = P.op("dv", lambda e: e.tensor_copy(out=rlb_s[:], in_=rlb_ps[:]), deps=[tbm])
            tev = P.op("dv", lambda e, h=h, qb=qb: e.tensor_tensor(
                out=attnT[(h % 2) * 64:(h % 2) * 64 + 64,
                          (h // 2) * TQ + qb * 512:(h // 2) * TQ + (qb + 1) * 512],
                in0=av_ps[0:64, :], in1=rlb_s[:], op=ALU.mult), deps=[tbe])
            attn_evac.append(tev)
        head_rel[h] = attn_evac[-1]
        if h % 4 == 3:
            vg_rel[g] = attn_evac[-1]
        if h + 2 < NH:
            load_head(h + 2)
        if h % 4 == 3 and g + 2 < 4:
            load_vg(g + 2)
    if "d_attn" in dbg_out:
        ST.append(P.dma("sy", lambda e: e.dma_start(out=dbg_out["d_attn"], in_=attnT[:]),
                        deps=attn_evac))
    if upto < 5:
        blk = es.enter_context(nc.Block())
        P.emit(blk)
        P.tail_wait(blk, ST)
        return nc, es

    ph4.close()
    ph6s = contextlib.ExitStack()
    x2_s = ph6s.enter_context(nc.sbuf_tensor("x2_s", [128, 2 * D], F32))
    h2_stg = ph6s.enter_context(nc.sbuf_tensor("h2_stg", [128, 2 * D], BF16))
    xn2buf = ph6s.enter_context(nc.sbuf_tensor("xn2buf", [128, 2 * D], F32))
    stat2 = ph6s.enter_context(nc.sbuf_tensor("stat2", [128, 8], F32))
    ph6 = contextlib.ExitStack()
    wo_ps = [[ph6.enter_context(nc.psum_tensor(f"wo_ps{p}_{j}", [128, 512], F32))
              for j in range(2)] for p in range(2)]
    x_rel = dict(relL)
    relWO = {}
    relH2 = {}
    relX2S = {}
    h2_toks = []
    x2_toks = []
    for tc2 in range(16):
        par = tc2 % 2
        xb = xbuf[:, par * D:(par + 1) * D]
        tl = P.dma("sy", lambda e, tc2=tc2, xb=xb: e.dma_start(
            out=xb, in_=x_kv[tc2 * 128:(tc2 + 1) * 128, :]), deps=[x_rel.get(par)])
        tdone = []
        for half in range(2):
            for ac_ in range(8):
                P.op("pe", lambda e, tc2=tc2, ac_=ac_, par=par, half=half: e.matmul(
                    wo_ps[par][half][:],
                    attnT[:, ac_ * TQ + tc2 * 128:ac_ * TQ + (tc2 + 1) * 128],
                    wo_s[:, ac_ * D + half * 512:ac_ * D + (half + 1) * 512],
                    start=(ac_ == 0), stop=False, skip_group_check=True),
                    deps=[attn_evac[-1], relWO.get((par, half))] + fold_done[-8:])
            tm = P.op("pe", lambda e, par=par, half=half: e.matmul(
                wo_ps[par][half][:], ones_b[0:1, 0:128],
                gabo_b[0:1, half * 512:(half + 1) * 512], start=False, stop=True,
                skip_group_check=True), deps=[tgabo], tok=True)
            tdone.append(tm)
        if "d_wops" in dbg_out:
            for half in range(2):
                tdbg = P.op("dv", lambda e, par=par, half=half: e.tensor_copy(
                    out=xn2buf[:, half * 512:(half + 1) * 512], in_=wo_ps[par][half][:]),
                    deps=[tdone[half]])
            ST.append(P.dma("sy", lambda e, tc2=tc2: e.dma_start(
                out=dbg_out["d_wops"][tc2 * 128:(tc2 + 1) * 128, :],
                in_=xn2buf[:, 0:D]), deps=[tdbg]))
        tx2 = None
        for half in range(2):
            tx2 = P.op("dv", lambda e, tc2=tc2, par=par, half=half, xb=xb: e.tensor_tensor(
                out=x2_s[:, par * D + half * 512:par * D + (half + 1) * 512],
                in0=wo_ps[par][half][:], in1=xb[:, half * 512:(half + 1) * 512], op=ALU.add),
                deps=[tdone[half], tl] + (relX2S.get(par) or []))
            relWO[(par, half)] = tx2
        x_rel[par] = tx2
        tso = P.dma("sy", lambda e, tc2=tc2, par=par: e.dma_start(
            out=x2_dram[tc2 * 128:(tc2 + 1) * 128, :], in_=x2_s[:, par * D:(par + 1) * D]),
            deps=[tx2])
        x2_toks.append(tso)
        x2t = x2_s[:, par * D:(par + 1) * D]
        xc2 = xn2buf[:, par * D:(par + 1) * D]
        st = stat2[:, par * 4:par * 4 + 4]
        P.op("dv", lambda e, x2t=x2t, st=st: e.reduce_sum(out=st[:, 0:1], in_=x2t, axis=AX.X),
             deps=[tx2])
        P.op("dv", lambda e, st=st: e.tensor_scalar(
            out=st[:, 1:2], in0=st[:, 0:1], scalar1=1.0 / D, scalar2=None, op0=ALU.mult))
        txc = P.op("dv", lambda e, x2t=x2t, xc2=xc2, st=st: e.tensor_scalar(
            out=xc2, in0=x2t, scalar1=st[:, 1:2], scalar2=None, op0=ALU.subtract))
        relX2S[par] = [txc, tso]
        P.op("ac", lambda e, xc2=xc2, st=st: e.activation(
            out=sqscr[:], in_=xc2, func=AF.Square, accum_out=st[:, 2:3]), deps=[txc])
        P.op("ac", lambda e, st=st: e.activation(
            out=st[:, 3:4], in_=st[:, 2:3], func=AF.Ln, scale=1.0 / D,
            bias=epscol[:, 0:1]), deps=[teps])
        trho = P.op("ac", lambda e, st=st: e.activation(
            out=st[:, 3:4], in_=st[:, 3:4], func=AF.Exp, scale=-0.5))
        P.op("dv", lambda e, xc2=xc2, st=st: e.tensor_scalar(
            out=xc2, in0=xc2, scalar1=st[:, 3:4], scalar2=None, op0=ALU.mult), deps=[trho])
        P.op("dv", lambda e, xc2=xc2: e.tensor_tensor(
            out=xc2, in0=xc2, in1=cb1B[:], op=ALU.mult), deps=[tiles_done])
        h2t = h2_stg[:, par * D:(par + 1) * D]
        th2 = P.op("dv", lambda e, xc2=xc2, h2t=h2t: e.tensor_tensor(
            out=h2t, in0=xc2, in1=sbB[:], op=ALU.add), deps=[relH2.get(par)])
        tpc = 16 // NCHUNK
        cchunk = tc2 // tpc
        ts1 = P.dma_on("sy", lambda e, tc2=tc2, h2t=h2t, cchunk=cchunk, tpc=tpc:
                       e.dma_start(
                           out=h2ch[cchunk][(tc2 % tpc) * 128:(tc2 % tpc + 1) * 128, :],
                           in_=h2t),
                       f"h2st{cchunk}", deps=[th2])
        relH2[par] = ts1
        h2_toks.append(ts1)
        if tc2 % tpc == tpc - 1:
            if cchunk == 0:
                RG = [list(range(8))]
                t_ag_h2_l = []
            t_ag_h2_l.append(P.cc(
                lambda e, c=cchunk: e.collective_compute(
                    "AllGather", ALU.bypass, replica_groups=RG,
                    ins=[h2ch[c].bitcast(F32).opt()],
                    outs=[h2_all[c * 8 * CH2:(c + 1) * 8 * CH2, :].bitcast(F32).opt()]),
                deps=h2_toks[-tpc:]))
    if upto < 6:
        blk = es.enter_context(nc.Block())
        P.emit(blk)
        P.tail_wait(blk, ST)
        return nc, es

    ph6.close()
    ph7 = contextlib.ExitStack()
    h2T = ph7.enter_context(nc.sbuf_tensor("h2T", [128, 8 * TQ], BF16))
    lgT_ps = ph7.enter_context(nc.psum_tensor("lgT_ps", [8, 512], F32))
    lg_ps = ph7.enter_context(nc.psum_tensor("lg_ps", [128, 8], BF16))
    b_ps = ph7.enter_context(nc.psum_tensor("b_ps", [128, 8], F32))
    a_ps = [ph7.enter_context(nc.psum_tensor(f"a_ps{j}", [128, 8], F32)) for j in range(2)]
    sg_ps = ph7.enter_context(nc.psum_tensor("sg_ps", [128, 16], F32))
    lgT_s = ph7.enter_context(nc.sbuf_tensor("lgT_s", [8, TQ], BF16))
    lgt = ph7.enter_context(nc.sbuf_tensor("lgt", [128, 16 * 8], F32))
    scr8 = ph7.enter_context(nc.sbuf_tensor("scr8", [128, 4 * 8], F32))
    colscr = ph7.enter_context(nc.sbuf_tensor("colscr", [128, 8], F32))
    arun = ph7.enter_context(nc.sbuf_tensor("arun", [128, 2 * 8], F32))
    cntrow = ph7.enter_context(nc.sbuf_tensor("cntrow", [1, 16], F32))
    segrow = ph7.enter_context(nc.sbuf_tensor("segrow", [1, 16], F32))
    segB = ph7.enter_context(nc.sbuf_tensor("segB", [128, 16], F32))
    ticol = ph7.enter_context(nc.sbuf_tensor("ticol", [128, 32], I32))

    h2T_tok = [[P.dma("sy", lambda e, dcx=dcx, c=c: e.dma_start(
        out=h2T[:, dcx * TQ + c * CH2:dcx * TQ + (c + 1) * CH2],
        in_=h2ch[c][:, dcx * 128:(dcx + 1) * 128], transpose=True),
        deps=h2_toks[c * (16 // NCHUNK):(c + 1) * (16 // NCHUNK)])
        for c in range(NCHUNK)] for dcx in range(8)]
    lgT_evac = []
    for qb in range(4):
        tm = None
        for dc in range(8):
            tm = P.op("pe", lambda e, qb=qb, dc=dc: e.matmul(
                lgT_ps[:], wg_b[:, dc * 8:(dc + 1) * 8],
                h2T[:, dc * TQ + qb * 512:dc * TQ + (qb + 1) * 512],
                start=(dc == 0), stop=(dc == 7)),
                deps=[h2T_tok[dc][qb * 512 // CH2]] + (lgT_evac[-1:] if lgT_evac else []),
                tok=(dc == 7))
        lgT_evac.append(P.op("dv", lambda e, qb=qb: e.tensor_copy(
            out=lgT_s[:, qb * 512:(qb + 1) * 512], in_=lgT_ps[:]), deps=[tm]))
    lg_toks = []
    prev_lg = None
    for tc3 in range(16):
        tt = P.op("pe", lambda e, tc3=tc3: e.transpose(
            lg_ps[:, 0:8], lgT_s[0:8, tc3 * 128:(tc3 + 1) * 128], id_b[0:8, 0:8]),
            deps=[lgT_evac[-1], prev_lg], tok=True)
        prev_lg = P.op("dv", lambda e, tc3=tc3: e.tensor_copy(
            out=lgt[:, tc3 * 8:(tc3 + 1) * 8], in_=lg_ps[:, 0:8]), deps=[tt])
        lg_toks.append(prev_lg)
    if "d_logits" in dbg_out:
        ST.append(P.dma("sy", lambda e: e.dma_start(
            out=dbg_out["d_logits"].rearrange("(t p) e -> p t e", p=128),
            in_=lgt[:].rearrange("p (t e) -> p t e", e=8)), deps=lg_toks))

    cnt_done = []
    tae_prev = {0: None, 1: None}
    for tc3 in range(16):
        lg = lgt[:, tc3 * 8:(tc3 + 1) * 8]
        P.op("dv", lambda e, lg=lg: e.reduce_max(out=colscr[:, 0:1], in_=lg, axis=AX.X),
             deps=[lg_toks[tc3]])
        P.op("dv", lambda e, lg=lg: e.tensor_scalar(
            out=scr8[:, 0:8], in0=lg, scalar1=colscr[:, 0:1], scalar2=None,
            op0=ALU.is_equal))
        P.op("dv", lambda e: e.tensor_tensor_scan(
            out=scr8[:, 8:16], data0=scr8[:, 0:8], data1=scr8[:, 0:8],
            initial=0.0, op0=ALU.add, op1=ALU.bypass))
        P.op("dv", lambda e: e.tensor_scalar(
            out=scr8[:, 8:16], in0=scr8[:, 8:16], scalar1=1.0, scalar2=None,
            op0=ALU.is_equal))
        tm0 = P.op("dv", lambda e, tc3=tc3: e.tensor_tensor(
            out=m0_s[:, tc3 * 8:(tc3 + 1) * 8], in0=scr8[:, 0:8], in1=scr8[:, 8:16],
            op=ALU.mult))
        P.op("dv", lambda e, tc3=tc3: e.tensor_scalar(
            out=scr8[:, 16:24], in0=m0_s[:, tc3 * 8:(tc3 + 1) * 8], scalar1=-1e30,
            scalar2=None, op0=ALU.mult))
        P.op("dv", lambda e, lg=lg: e.tensor_tensor(
            out=scr8[:, 16:24], in0=lg, in1=scr8[:, 16:24], op=ALU.add))
        P.op("dv", lambda e: e.reduce_max(out=colscr[:, 1:2], in_=scr8[:, 16:24], axis=AX.X))
        P.op("dv", lambda e: e.tensor_scalar(
            out=scr8[:, 24:32], in0=scr8[:, 16:24], scalar1=colscr[:, 1:2], scalar2=None,
            op0=ALU.is_equal))
        P.op("dv", lambda e: e.tensor_tensor_scan(
            out=scr8[:, 16:24], data0=scr8[:, 24:32], data1=scr8[:, 24:32],
            initial=0.0, op0=ALU.add, op1=ALU.bypass))
        P.op("dv", lambda e: e.tensor_scalar(
            out=scr8[:, 16:24], in0=scr8[:, 16:24], scalar1=1.0, scalar2=None,
            op0=ALU.is_equal))
        tm1 = P.op("dv", lambda e, tc3=tc3: e.tensor_tensor(
            out=m1_s[:, tc3 * 8:(tc3 + 1) * 8], in0=scr8[:, 24:32], in1=scr8[:, 16:24],
            op=ALU.mult))
        td01 = P.op("dv", lambda e: e.tensor_tensor(
            out=colscr[:, 2:3], in0=colscr[:, 1:2], in1=colscr[:, 0:1], op=ALU.subtract))
        P.op("dv", lambda e, tc3=tc3: e.tensor_tensor(
            out=scr8[:, 0:8], in0=m0_s[:, tc3 * 8:(tc3 + 1) * 8], in1=iotaE[:], op=ALU.mult))
        P.op("dv", lambda e, tc3=tc3: e.reduce_sum(
            out=rt_cols[:, tc3 * 8 + 0:tc3 * 8 + 1], in_=scr8[:, 0:8], axis=AX.X))
        P.op("dv", lambda e, tc3=tc3: e.tensor_tensor(
            out=scr8[:, 0:8], in0=m1_s[:, tc3 * 8:(tc3 + 1) * 8], in1=iotaE[:], op=ALU.mult))
        P.op("dv", lambda e, tc3=tc3: e.reduce_sum(
            out=rt_cols[:, tc3 * 8 + 1:tc3 * 8 + 2], in_=scr8[:, 0:8], axis=AX.X))
        tx = P.op("ac", lambda e: e.activation(out=colscr[:, 3:4], in_=colscr[:, 2:3],
                                               func=AF.Exp), deps=[td01])
        P.op("dv", lambda e: e.tensor_scalar_add(out=colscr[:, 3:4], in0=colscr[:, 3:4],
                                                 scalar1=1.0), deps=[tx])
        P.op("dv", lambda e, tc3=tc3: e.reciprocal(
            out=rt_cols[:, tc3 * 8 + 2:tc3 * 8 + 3], in_=colscr[:, 3:4]))
        P.op("dv", lambda e, tc3=tc3: e.tensor_scalar(
            out=rt_cols[:, tc3 * 8 + 3:tc3 * 8 + 4],
            in0=rt_cols[:, tc3 * 8 + 2:tc3 * 8 + 3], scalar1=-1.0, scalar2=1.0,
            op0=ALU.mult, op1=ALU.add))
        for (kk, msrc, aps) in ((0, m0_s, a_ps[0]), (1, m1_s, a_ps[1])):
            tmm = P.op("pe", lambda e, tc3=tc3, msrc=msrc: e.matmul(
                b_ps[:], ut_b[:], msrc[:, tc3 * 8:(tc3 + 1) * 8],
                start=True, stop=True, skip_group_check=True),
                deps=[tm0, tm1] + cnt_done[-1:], tok=True)
            if tc3 == 0:
                P.op("dv", lambda e, kk=kk: e.memset(arun[:, kk * 8:(kk + 1) * 8], 0.0))
            tp = P.op("dv", lambda e, kk=kk: e.tensor_tensor(
                out=scr8[:, 0:8], in0=b_ps[:], in1=arun[:, kk * 8:(kk + 1) * 8],
                op=ALU.add), deps=[tmm])
            P.op("dv", lambda e, tc3=tc3, kk=kk, msrc=msrc: e.tensor_tensor(
                out=scr8[:, 8:16], in0=scr8[:, 0:8], in1=msrc[:, tc3 * 8:(tc3 + 1) * 8],
                op=ALU.mult))
            P.op("dv", lambda e, tc3=tc3, kk=kk: e.reduce_sum(
                out=rt_cols[:, tc3 * 8 + 4 + kk:tc3 * 8 + 5 + kk], in_=scr8[:, 8:16],
                axis=AX.X))
            tam = P.op("pe", lambda e, tc3=tc3, msrc=msrc, aps=aps: e.matmul(
                aps[:], all1_b[:], msrc[:, tc3 * 8:(tc3 + 1) * 8],
                start=(tc3 == 0), stop=(tc3 == 15), skip_group_check=True),
                deps=[tp, tae_prev[kk]], tok=True)
            tae = P.op("dv", lambda e, kk=kk, aps=aps: e.tensor_copy(
                out=arun[:, kk * 8:(kk + 1) * 8], in_=aps[:]), deps=[tam])
            tae_prev[kk] = tae
            cnt_done.append(tae)
    tcnt = P.op("dv", lambda e: e.tensor_copy(out=cntrow[0:1, 0:8], in_=a_ps[0][0:1, :]),
                deps=cnt_done)
    tcnt = P.op("dv", lambda e: e.tensor_copy(out=cntrow[0:1, 8:16], in_=a_ps[1][0:1, :]),
                deps=[tcnt])
    tcnt_st = P.dma_on("sy", lambda e: e.dma_start(out=cnt_loc, in_=cntrow[0:1, :]),
                       "cntst", deps=[tcnt])
    t_ag_cnt = P.cc(lambda e: e.collective_compute(
        "AllGather", ALU.bypass, replica_groups=RG,
        ins=[cnt_loc.opt()], outs=[cnt_all_d.opt()]),
        deps=[tcnt_st])
    tsg2 = P.op("dv", lambda e: e.tensor_tensor_scan(
        out=segrow[0:1, :], data0=cntrow[0:1, :], data1=cntrow[0:1, :],
        initial=0.0, op0=ALU.add, op1=ALU.bypass), deps=[tcnt])
    tsg2 = P.op("dv", lambda e: e.tensor_tensor(
        out=segrow[0:1, :], in0=segrow[0:1, :], in1=cntrow[0:1, :], op=ALU.subtract))
    tbm2 = P.op("pe", lambda e: e.matmul(sg_ps[:], ones_f[:], segrow[0:1, :],
                                         start=True, stop=True, skip_group_check=True),
                deps=[tsg2], tok=True)
    tsgB = P.op("dv", lambda e: e.tensor_copy(out=segB[:], in_=sg_ps[:]), deps=[tbm2])

    bc_reg = {}
    ls_toks = []
    for tc3 in range(16):
        for kk in range(2):
            msrc = m0_s if kk == 0 else m1_s
            P.op("dv", lambda e, tc3=tc3, kk=kk, msrc=msrc: e.tensor_tensor(
                out=scr8[:, 0:8], in0=msrc[:, tc3 * 8:(tc3 + 1) * 8],
                in1=segB[:, kk * 8:(kk + 1) * 8], op=ALU.mult), deps=[tsgB])
            P.op("dv", lambda e: e.reduce_sum(out=colscr[:, 4:5], in_=scr8[:, 0:8],
                                              axis=AX.X))
            P.op("dv", lambda e, tc3=tc3, kk=kk: e.tensor_tensor(
                out=colscr[:, 5:6], in0=colscr[:, 4:5],
                in1=rt_cols[:, tc3 * 8 + 4 + kk:tc3 * 8 + 5 + kk], op=ALU.add))
            ci = tc3 * 2 + kk
            t4 = P.op("dv", lambda e, ci=ci: e.tensor_copy(out=ticol[:, ci:ci + 1],
                                                           in_=colscr[:, 5:6]))

            def do(e, tc3=tc3, ci=ci):
                if "bc" not in bc_reg:
                    bc_reg["bc"] = e.to_reg(CAP)
                return e.indirect_dma_start(
                    out=lsort_loc, out_offset=bass.IndirectOffsetOnAxis(ticol[:, ci:ci + 1], 0),
                    in_=gtok_f[:, tc3:tc3 + 1], in_offset=None,
                    bounds_check=bc_reg["bc"], oob_is_err=False)
            ls_toks.append(P.dma("gp", do, deps=[t4]))

    ph7.close()
    ph6s.close()
    ph4a.close()
    phA.close()
    barA = P.barrier()
    t_ag_ls = P.cc(lambda e: e.collective_compute(
        "AllGather", ALU.bypass, replica_groups=RG,
        ins=[lsort_loc.rearrange("(a b) o -> a (o b)", b=128).opt()],
        outs=[lsort_all.rearrange("(a b) o -> a (o b)", b=128).opt()]),
        deps=ls_toks)

    P.base_deps = barA
    phB = contextlib.ExitStack()
    sbB_ = lambda n, s, dt=F32: phB.enter_context(nc.sbuf_tensor(n, s, dt))
    cTB = sbB_("cTB", [16, 8])
    c2B = sbB_("c2B", [8, 16])
    s16B = sbB_("s16B", [16, 16])
    exrB = sbB_("exrB", [16, 2])
    exsB = sbB_("exsB", [8, 2 * 16])
    iodB = sbB_("iodB", [8, 1])
    io128 = sbB_("io128", [128, 1])
    b1c = sbB_("b1c", [128, 32])
    b2Bf = sbB_("b2Bf", [128, D], BF16)
    ones_bB = sbB_("ones_bB", [1, 128], BF16)
    hstg = [sbB_(f"hstg{p}", [128, 256], BF16) for p in range(2)]
    ystg = sbB_("ystg", [128, 512], BF16)
    w1_s = sbB_("w1_s", [128, 32 * D], BF16)
    w2_s = sbB_("w2_s", [128, 32 * D], BF16)
    dT = sbB_("dT", [128, 8 * CAP], BF16)
    phB2 = contextlib.ExitStack()
    sb2 = lambda n, s, dt=F32: phB2.enter_context(nc.sbuf_tensor(n, s, dt))
    globB = sb2("globB", [16, 8])
    lbase = sb2("lbase", [8, 16])
    srcb = sb2("srcb", [8, 2])
    sbrow = sb2("sbrow", [1, 18])
    srcrow = sb2("srcrow", [1, 16])
    sbB16 = sb2("sbB16", [128, 16])
    diffB = sb2("diffB", [128, 16])
    totB = sb2("totB", [128, 1])
    scrB = sb2("scrB", [128, 2 * 16])
    colsB = sb2("colsB", [128, 8])
    ticB = sb2("ticB", [128, 32], I32)
    invc = sb2("invc", [128, 32])
    dispf32 = sb2("dispf32", [128, 128])
    dispstg = [sb2(f"dispstg{p}", [128, D], BF16) for p in range(2)]

    phBp = contextlib.ExitStack()
    tB_ps = phBp.enter_context(nc.psum_tensor("tB_ps", [16, 8], F32))
    t1B_ps = phBp.enter_context(nc.psum_tensor("t1B_ps", [128, 16], F32))
    rB_ps = phBp.enter_context(nc.psum_tensor("rB_ps", [1, 8], F32))
    bbB_ps = phBp.enter_context(nc.psum_tensor("bbB_ps", [128, 512], F32))

    tloadB = [P.dma("sy", lambda e, a=a, b_=b_: e.dma_start(out=a, in_=b_))
              for a, b_ in [(s16B[:], s16), (exrB[:, 0:1], exr0),
                            (exrB[:, 1:2], exr1), (exsB[:, 0:16], exs0),
                            (exsB[:, 16:32], exs1), (iodB[:], iotad),
                            (io128[:], iota128), (ones_bB[:], onesbf)]]
    tb1c = P.dma("sy", lambda e: e.dma_start(out=b1c[:], in_=b1m))
    tb2l = P.dma("gp", lambda e: e.dma_start(out=b2Bf[0:1, :], in_=b2r))
    tloadB += [tb1c, tb2l]
    tc2l = P.dma("sy", lambda e: e.dma_start(out=c2B[:], in_=cnt_all_d),
                 deps=[t_ag_cnt])
    ttr = P.op("pe", lambda e: e.transpose(tB_ps[:], c2B[0:8, 0:16], id_f[0:8, 0:8]),
               deps=[tc2l], tok=True)
    tctl = P.op("dv", lambda e: e.tensor_copy(out=cTB[:], in_=tB_ps[:]), deps=[ttr])
    tloadB += [tc2l, tctl]
    w1tok = [P.dma("gp", lambda e, hc=hc: e.dma_start(
        out=w1_s[:, hc * D:(hc + 1) * D], in_=w1t[hc * 128:(hc + 1) * 128, :]))
        for hc in range(32)]
    w2tok = [P.dma("gp", lambda e, hc=hc: e.dma_start(
        out=w2_s[:, hc * D:(hc + 1) * D], in_=w2m[hc * 128:(hc + 1) * 128, :]))
        for hc in range(32)]

    P.op("dv", lambda e: e.tensor_tensor_scan(
        out=globB[:], data0=cTB[:], data1=cTB[:], initial=0.0, op0=ALU.add,
        op1=ALU.bypass), deps=tloadB)
    P.op("dv", lambda e: e.tensor_tensor(out=globB[:], in0=globB[:], in1=cTB[:],
                                         op=ALU.subtract))
    t = P.op("dv", lambda e: e.tensor_tensor(out=scrB[0:16, 0:1], in0=globB[:, 7:8],
                                             in1=cTB[:, 7:8], op=ALU.add))
    tm = P.op("pe", lambda e: e.matmul(tB_ps[:, 0:1], s16B[:], scrB[0:16, 0:1],
                                       start=True, stop=True), deps=[t, tctl], tok=True)
    tg = P.op("dv", lambda e: e.tensor_scalar(
        out=globB[:], in0=globB[:], scalar1=tB_ps[:, 0:1], scalar2=None, op0=ALU.add),
        deps=[tm])
    tm = P.op("pe", lambda e: e.matmul(tB_ps[0:1, :], exrB[:, 0:1], globB[:],
                                       start=True, stop=True), deps=[tg], tok=True)
    t = P.op("dv", lambda e: e.tensor_copy(out=sbrow[0:1, 0:8], in_=tB_ps[0:1, :]),
             deps=[tm])
    tm = P.op("pe", lambda e: e.matmul(tB_ps[0:1, :], exrB[:, 1:2], globB[:],
                                       start=True, stop=True), deps=[t], tok=True)
    t = P.op("dv", lambda e: e.tensor_copy(out=sbrow[0:1, 8:16], in_=tB_ps[0:1, :]),
             deps=[tm])
    t2 = P.op("dv", lambda e: e.tensor_tensor(out=scrB[0:16, 1:2], in0=globB[:, 7:8],
                                              in1=cTB[:, 7:8], op=ALU.add))
    tm = P.op("pe", lambda e: e.matmul(tB_ps[0:1, 0:1], exrB[:, 1:2], scrB[0:16, 1:2],
                                       start=True, stop=True), deps=[t2], tok=True)
    ttot = P.op("dv", lambda e: e.tensor_copy(out=sbrow[0:1, 16:17], in_=tB_ps[0:1, 0:1]),
                deps=[tm])
    P.op("dv", lambda e: e.tensor_tensor_scan(
        out=lbase[:], data0=c2B[:], data1=c2B[:], initial=0.0, op0=ALU.add,
        op1=ALU.bypass))
    P.op("dv", lambda e: e.tensor_tensor(out=lbase[:], in0=lbase[:], in1=c2B[:],
                                         op=ALU.subtract))
    tsb = None
    for kk in range(2):
        P.op("dv", lambda e, kk=kk: e.tensor_tensor(
            out=scrB[0:8, 0:16], in0=lbase[:], in1=exsB[:, kk * 16:(kk + 1) * 16],
            op=ALU.mult))
        P.op("dv", lambda e, kk=kk: e.reduce_sum(out=srcb[:, kk:kk + 1],
                                                 in_=scrB[0:8, 0:16], axis=AX.X))
        tsb = P.op("dv", lambda e, kk=kk: e.tensor_tensor(
            out=srcb[:, kk:kk + 1], in0=srcb[:, kk:kk + 1], in1=iodB[:], op=ALU.add))
    prev = None
    for kk in range(2):
        tt = P.op("pe", lambda e, kk=kk: e.transpose(
            rB_ps[0:1, :], srcb[:, kk:kk + 1], id_f[0:8, 0:8]),
            deps=[tsb, prev], tok=True)
        prev = P.op("dv", lambda e, kk=kk: e.tensor_copy(
            out=srcrow[0:1, kk * 8:(kk + 1) * 8], in_=rB_ps[0:1, :]), deps=[tt])
    tm = P.op("pe", lambda e: e.matmul(t1B_ps[:], ones_f[:], sbrow[0:1, 0:16],
                                       start=True, stop=True), deps=[ttot, prev], tok=True)
    tSB = P.op("dv", lambda e: e.tensor_copy(out=sbB16[:], in_=t1B_ps[:]), deps=[tm])
    tm = P.op("pe", lambda e: e.matmul(t1B_ps[:], ones_f[:], srcrow[0:1, :],
                                       start=True, stop=True), deps=[tSB], tok=True)
    P.op("dv", lambda e: e.tensor_copy(out=diffB[:], in_=t1B_ps[:]), deps=[tm])
    tDF = P.op("dv", lambda e: e.tensor_tensor(out=diffB[:], in0=diffB[:], in1=sbB16[:],
                                               op=ALU.subtract))
    tm = P.op("pe", lambda e: e.matmul(t1B_ps[:, 0:1], ones_f[:], sbrow[0:1, 16:17],
                                       start=True, stop=True), deps=[tDF], tok=True)
    tTB = P.op("dv", lambda e: e.tensor_copy(out=totB[:], in_=t1B_ps[:, 0:1]), deps=[tm])
    b2b_toks = []
    tprev2 = tb1c
    for half in range(2):
        tm = P.op("pe", lambda e, half=half: e.matmul(
            bbB_ps[:], ones_bB[:], b2Bf[0:1, half * 512:(half + 1) * 512],
            start=True, stop=True, skip_group_check=True),
            deps=[tprev2, tb2l], tok=True)
        tprev2 = P.op("dv", lambda e, half=half: e.tensor_copy(
            out=b2Bf[:, half * 512:(half + 1) * 512], in_=bbB_ps[:]), deps=[tm])
        b2b_toks.append(tprev2)
    tinv0 = P.op("gp", lambda e: e.memset(invc[:], 0.0))
    phBp.close()
    phB4 = contextlib.ExitStack()
    h_ps = [phB4.enter_context(nc.psum_tensor(f"h_ps{p}", [128, 256], F32))
            for p in range(2)]
    y_ps = [phB4.enter_context(nc.psum_tensor(f"y_ps{j}", [128, 1024], F32))
            for j in range(2)]
    tpT_ps = phB4.enter_context(nc.psum_tensor("tpT_ps2", [128, 256], F32))

    bcB = {}
    relD = {}
    dTev = {}
    tpT_rel = {}
    dTev_prev = {}

    def emit_b2_tile(t_):
        P.op("dv", lambda e, t_=t_: e.tensor_scalar(
            out=colsB[:, 0:1], in0=io128[:], scalar1=float(t_ * 128), scalar2=None,
            op0=ALU.add), deps=[tSB, tDF, tTB])
        P.op("dv", lambda e: e.tensor_scalar(
            out=scrB[:, 0:16], in0=sbB16[:], scalar1=colsB[:, 0:1], scalar2=None,
            op0=ALU.is_le))
        P.op("dv", lambda e: e.tensor_tensor(
            out=scrB[:, 16:31], in0=scrB[:, 0:15], in1=scrB[:, 1:16], op=ALU.subtract))
        P.op("dv", lambda e: e.tensor_copy(out=scrB[:, 31:32], in_=scrB[:, 15:16]))
        P.op("dv", lambda e: e.tensor_tensor(
            out=scrB[:, 16:32], in0=scrB[:, 16:32], in1=diffB[:], op=ALU.mult))
        P.op("dv", lambda e: e.reduce_sum(out=colsB[:, 1:2], in_=scrB[:, 16:32], axis=AX.X))
        P.op("dv", lambda e: e.tensor_tensor(
            out=colsB[:, 1:2], in0=colsB[:, 1:2], in1=colsB[:, 0:1], op=ALU.add))
        P.op("dv", lambda e: e.tensor_scalar(
            out=colsB[:, 2:3], in0=colsB[:, 0:1], scalar1=totB[:, 0:1], scalar2=None,
            op0=ALU.is_lt))
        P.op("dv", lambda e: e.tensor_scalar(
            out=colsB[:, 3:4], in0=colsB[:, 2:3], scalar1=-1e9, scalar2=1e9,
            op0=ALU.mult, op1=ALU.add))
        P.op("dv", lambda e: e.tensor_tensor(
            out=colsB[:, 1:2], in0=colsB[:, 1:2], in1=colsB[:, 3:4], op=ALU.add))
        tic = P.op("dv", lambda e, t_=t_: e.tensor_copy(out=ticB[:, t_:t_ + 1],
                                                        in_=colsB[:, 1:2]))

        def g1(e, t_=t_):
            if "r1" not in bcB:
                bcB["r1"] = e.to_reg(8 * CAP)
            return e.indirect_dma_start(
                out=invc[:, t_:t_ + 1], out_offset=None, in_=lsort_all,
                in_offset=bass.IndirectOffsetOnAxis(ticB[:, t_:t_ + 1], 0),
                bounds_check=bcB["r1"], oob_is_err=False)
        tg1 = P.dma("gp", g1, deps=[tic, tinv0, t_ag_ls])
        P.op("dv", lambda e, t_=t_: e.tensor_scalar(
            out=colsB[:, 4:5], in0=invc[:, t_:t_ + 1], scalar1=-1.0, scalar2=None,
            op0=ALU.add), deps=[tg1])
        P.op("dv", lambda e, t_=t_: e.tensor_scalar(
            out=colsB[:, 5:6], in0=invc[:, t_:t_ + 1], scalar1=0.5, scalar2=None,
            op0=ALU.is_ge))
        P.op("dv", lambda e: e.tensor_scalar(
            out=colsB[:, 6:7], in0=colsB[:, 5:6], scalar1=-1e9, scalar2=1e9,
            op0=ALU.mult, op1=ALU.add))
        P.op("dv", lambda e: e.tensor_tensor(
            out=colsB[:, 4:5], in0=colsB[:, 4:5], in1=colsB[:, 6:7], op=ALU.add))
        tic2 = P.op("dv", lambda e, t_=t_: e.tensor_copy(out=ticB[:, t_:t_ + 1],
                                                         in_=colsB[:, 4:5]))
        dpar = t_ % 2
        dstB = dispstg[dpar][:, 0:D]
        tms = P.op("gp", lambda e, dstB=dstB: e.memset(dstB, 0.0), deps=[relD.get(dpar)])

        def g2(e, t_=t_, dstB=dstB):
            if "r2" not in bcB:
                bcB["r2"] = e.to_reg(8 * TQ)
            return e.indirect_dma_start(
                out=dstB, out_offset=None, in_=h2_all,
                in_offset=bass.IndirectOffsetOnAxis(ticB[:, t_:t_ + 1], 0),
                bounds_check=bcB["r2"], oob_is_err=False)
        tg2 = P.dma("gp", g2, deps=[tic2, tms] + t_ag_h2_l)
        tcp = None
        for dc in range(8):
            hp2 = dc % 2
            tcp = P.op("dv", lambda e, dc=dc, dstB=dstB: e.tensor_copy(
                out=dispf32[:],
                in_=dstB[:, dc * 128:(dc + 1) * 128]),
                deps=[tg2, tpT_rel.get("f")])
            ttp = P.op("pe", lambda e, hp2=hp2: e.transpose(
                tpT_ps[:, hp2 * 128:(hp2 + 1) * 128],
                dispf32[:], id_f[:]),
                deps=[tcp, dTev_prev.get(hp2)], tok=True)
            tev = P.op("dv", lambda e, t_=t_, dc=dc, hp2=hp2: e.tensor_copy(
                out=dT[:, dc * CAP + t_ * 128:dc * CAP + (t_ + 1) * 128],
                in_=tpT_ps[:, hp2 * 128:(hp2 + 1) * 128]), deps=[ttp])
            tpT_rel["f"] = ttp
            dTev_prev[hp2] = tev
            dTev[(t_, dc)] = tev
        relD[dpar] = tcp

    SGS = [(i * 256, 256) for i in range(CAP // 256)]
    yst_prev = {}
    yB_toks = []
    y_chunk_deps = {c: [] for c in range(NCHUNK)}
    gel_prev = {}
    w2_last = {}

    def emit_b4_group(sgi):
        s0, n = SGS[sgi]
        nj = (n + 127) // 128
        tmm2 = None
        for hc in range(32):
            hp = hc % 2
            tm1 = None
            tmax = (s0 + n - 1) // 128
            for dc in range(8):
                tm1 = P.op("pe", lambda e, hc=hc, dc=dc, s0=s0, n=n, hp=hp: e.matmul(
                    h_ps[hp][:, 0:n], w1_s[:, hc * D + dc * 128:hc * D + (dc + 1) * 128],
                    dT[:, dc * CAP + s0:dc * CAP + s0 + n],
                    start=(dc == 0), stop=(dc == 7), skip_group_check=True),
                    deps=[dTev[(tmax, dc)], w1tok[hc], gel_prev.get(hp)],
                    tok=(dc == 7))
            tgel = P.op("ac", lambda e, hc=hc, n=n, hp=hp: e.activation(
                out=hstg[hp][:, 0:n], in_=h_ps[hp][:, 0:n], func=AF.Gelu,
                bias=b1c[:, hc:hc + 1]), deps=[tm1, tb1c, w2_last.get(hp)])
            gel_prev[hp] = tgel
            for j in range(nj):
                w = min(128, n - j * 128)
                for half in range(2):
                    tmm2 = P.op("pe", lambda e, hc=hc, j=j, w=w, half=half, hp=hp: e.matmul(
                        y_ps[j][:, half * 512:(half + 1) * 512],
                        hstg[hp][:, j * 128:j * 128 + w],
                        w2_s[:, hc * D + half * 512:hc * D + (half + 1) * 512],
                        start=(hc == 0), stop=(hc == 31), skip_group_check=True),
                        deps=[tgel, w2tok[hc]] + ([yst_prev.get(j)] if hc == 0 else []),
                        tok=(j == nj - 1 and half == 1))
            w2_last[hp] = tmm2
        tsl = None
        for j in range(nj):
            w = min(128, n - j * 128)
            r0, r1 = s0 + j * 128, s0 + j * 128 + w
            for half in range(2):
                tev = P.op("dv", lambda e, j=j, w=w, half=half: e.tensor_tensor(
                    out=ystg[0:w, :], in0=y_ps[j][0:w, half * 512:(half + 1) * 512],
                    in1=b2Bf[0:w, half * 512:(half + 1) * 512], op=ALU.add),
                    deps=[tmm2, tsl] + b2b_toks)
                ts = P.dma_on("sy", lambda e, r0=r0, w=w, half=half: e.dma_start(
                    out=ylc[r0 // YCH][r0 % YCH:r0 % YCH + w,
                                       half * 512:(half + 1) * 512],
                    in_=ystg[0:w, :]), f"ystc{r0 // YCH}", deps=[tev])
                tsl = ts
                yB_toks.append(ts)
                for cch in range(r0 // YCH, (r1 - 1) // YCH + 1):
                    y_chunk_deps[cch].append(ts)
            yst_prev[j] = tsl

    for t_ in range(32):
        emit_b2_tile(t_)
    for sgi in range(len(SGS)):
        emit_b4_group(sgi)
    phB2.close()

    t_ag_y_l = [P.cc(lambda e, c=c: e.collective_compute(
        "AllGather", ALU.bypass, replica_groups=RG,
        ins=[ylc[c].bitcast(F32).opt()],
        outs=[y_all[c * 8 * YCH:(c + 1) * 8 * YCH, :].bitcast(F32).opt()]),
        deps=y_chunk_deps[c]) for c in range(NCHUNK)]

    P.base_deps = P.barrier()
    phB4.close()
    phB.close()
    phC = contextlib.ExitStack()
    sbC = lambda n, s, dt=F32: phC.enter_context(nc.sbuf_tensor(n, s, dt))
    ctC = sbC("ctC", [16, 8])
    c2C = sbC("c2C", [8, 16])
    rkoC = sbC("rkoC", [16, 8])
    globC = sbC("globC", [16, 8])
    offcC = sbC("offcC", [16, 1])
    offrowC = sbC("offrowC", [1, 16])
    offBC = sbC("offBC", [128, 16])
    m0C = sbC("m0C", [128, 8])
    colsC = sbC("colsC", [128, 8])
    ticC = sbC("ticC", [128, 32], I32)
    wghC = sbC("wghC", [128, 32])
    y0sl = [sbC(f"y0s{p}", [128, D], BF16) for p in range(4)]
    y1sl = [sbC(f"y1s{p}", [128, D], BF16) for p in range(4)]
    accC = sbC("accC", [128, D])
    x2stg = sbC("x2stg", [128, 4 * D])
    ostg = sbC("ostg", [128, 4 * D])
    s16C = sbC("s16C", [16, 16])

    phCp = contextlib.ExitStack()
    tC_ps = phCp.enter_context(nc.psum_tensor("tC_ps", [16, 8], F32))
    rC_ps = phCp.enter_context(nc.psum_tensor("rC_ps", [1, 16], F32))
    bBC_ps = phCp.enter_context(nc.psum_tensor("bBC_ps", [128, 512], F32))

    tloadC = [P.dma("sy", lambda e, a=a, b_=b_: e.dma_start(out=a, in_=b_))
              for a, b_ in [(s16C[:], s16), (rkoC[:], rkoh)]]
    tc2lC = P.dma("sy", lambda e: e.dma_start(out=c2C[:], in_=cnt_all_d),
                  deps=[t_ag_cnt])
    ttrC = P.op("pe", lambda e: e.transpose(tC_ps[:], c2C[0:8, 0:16], id_f[0:8, 0:8]),
                deps=[tc2lC], tok=True)
    tctC = P.op("dv", lambda e: e.tensor_copy(out=ctC[:], in_=tC_ps[:]), deps=[ttrC])
    tloadC += [tc2lC, tctC]
    P.op("dv", lambda e: e.tensor_tensor_scan(
        out=globC[:], data0=ctC[:], data1=ctC[:], initial=0.0, op0=ALU.add,
        op1=ALU.bypass), deps=tloadC)
    P.op("dv", lambda e: e.tensor_tensor(out=globC[:], in0=globC[:], in1=ctC[:],
                                         op=ALU.subtract))
    t = P.op("dv", lambda e: e.tensor_tensor(out=offcC[:], in0=globC[:, 7:8],
                                             in1=ctC[:, 7:8], op=ALU.add))
    tm = P.op("pe", lambda e: e.matmul(tC_ps[:, 0:1], s16C[:], offcC[:],
                                       start=True, stop=True), deps=[t, tctC], tok=True)
    tg = P.op("dv", lambda e: e.tensor_scalar(
        out=globC[:], in0=globC[:], scalar1=tC_ps[:, 0:1], scalar2=None, op0=ALU.add),
        deps=[tm])
    P.op("dv", lambda e: e.tensor_tensor(out=ctC[:], in0=globC[:], in1=rkoC[:],
                                         op=ALU.mult), deps=[tg])
    toff = P.op("dv", lambda e: e.reduce_sum(out=offcC[:], in_=ctC[:], axis=AX.X))
    tt = P.op("pe", lambda e: e.transpose(rC_ps[0:1, :], offcC[:], id_f[0:16, 0:16]),
              deps=[toff], tok=True)
    t = P.op("dv", lambda e: e.tensor_copy(out=offrowC[0:1, :], in_=rC_ps[0:1, :]),
             deps=[tt])
    tm = P.op("pe", lambda e: e.matmul(bBC_ps[:, 0:16], ones_f[:], offrowC[0:1, :],
                                       start=True, stop=True, skip_group_check=True),
              deps=[t], tok=True)
    tOF = P.op("dv", lambda e: e.tensor_copy(out=offBC[:], in_=bBC_ps[:, 0:16]), deps=[tm])
    tms_y = [P.op("gp", lambda e, t=t: e.memset(t[:], 0.0))
             for t in y0sl + y1sl]

    tic_l = {}
    for tc in range(16):
        rr = rt_cols[:, tc * 8:(tc + 1) * 8]
        for kk in range(2):
            ecol = rr[:, kk:kk + 1]
            vcol = rr[:, 2 + kk:3 + kk]
            pcol = rr[:, 4 + kk:5 + kk]
            P.op("dv", lambda e, ecol=ecol: e.tensor_scalar(
                out=m0C[:], in0=iotaE[:], scalar1=ecol, scalar2=None, op0=ALU.is_equal),
                deps=[tOF])
            P.op("dv", lambda e, kk=kk: e.tensor_tensor(
                out=m0C[:], in0=m0C[:], in1=offBC[:, kk * 8:(kk + 1) * 8], op=ALU.mult))
            P.op("dv", lambda e: e.reduce_sum(out=colsC[:, 0:1], in_=m0C[:], axis=AX.X))
            P.op("dv", lambda e, pcol=pcol: e.tensor_tensor(
                out=colsC[:, 0:1], in0=colsC[:, 0:1], in1=pcol, op=ALU.add))
            P.op("dv", lambda e: e.tensor_scalar(
                out=colsC[:, 1:2], in0=colsC[:, 0:1], scalar1=float(CAP), scalar2=None,
                op0=ALU.is_lt))
            P.op("dv", lambda e, vcol=vcol, tc=tc, kk=kk: e.tensor_tensor(
                out=wghC[:, tc * 2 + kk:tc * 2 + kk + 1], in0=vcol, in1=colsC[:, 1:2],
                op=ALU.mult))
            for ich in range(1, NCHUNK):
                P.op("dv", lambda e, ich=ich: e.tensor_scalar(
                    out=colsC[:, 7:8], in0=colsC[:, 0:1], scalar1=float(ich * YCH),
                    scalar2=None, op0=ALU.is_ge))
                if ich == 1:
                    P.op("dv", lambda e: e.tensor_copy(out=colsC[:, 6:7],
                                                       in_=colsC[:, 7:8]))
                else:
                    P.op("dv", lambda e: e.tensor_tensor(
                        out=colsC[:, 6:7], in0=colsC[:, 6:7], in1=colsC[:, 7:8],
                        op=ALU.add))
            P.op("dv", lambda e, ecol=ecol: e.tensor_scalar(
                out=colsC[:, 2:3], in0=ecol, scalar1=float(YCH), scalar2=None,
                op0=ALU.mult))
            P.op("dv", lambda e: e.tensor_tensor(
                out=colsC[:, 2:3], in0=colsC[:, 2:3], in1=colsC[:, 0:1], op=ALU.add))
            if NCHUNK > 1:
                P.op("dv", lambda e: e.scalar_tensor_tensor(
                    out=colsC[:, 2:3], in0=colsC[:, 6:7], scalar=float(7 * YCH),
                    in1=colsC[:, 2:3], op0=ALU.mult, op1=ALU.add))
            P.op("dv", lambda e: e.tensor_scalar(
                out=colsC[:, 3:4], in0=colsC[:, 1:2], scalar1=-1e9, scalar2=1e9,
                op0=ALU.mult, op1=ALU.add))
            P.op("dv", lambda e: e.tensor_tensor(
                out=colsC[:, 2:3], in0=colsC[:, 2:3], in1=colsC[:, 3:4], op=ALU.add))
            tic_l[(tc, kk)] = P.op("dv", lambda e, tc=tc, kk=kk: e.tensor_copy(
                out=ticC[:, tc * 2 + kk:tc * 2 + kk + 1], in_=colsC[:, 2:3]))

    bcC = {}
    relX, relO, relG = {}, {}, {}
    for tc in range(16):
        par = tc % 4
        tx = P.dma("sy", lambda e, tc=tc, par=par: e.dma_start(
            out=x2stg[:, par * D:(par + 1) * D],
            in_=x2_dram[tc * 128:(tc + 1) * 128, :]), deps=[relX.get(par)])
        tgys = []
        for kk in range(2):
            def gy(e, tc=tc, kk=kk, par=par):
                if "r" not in bcC:
                    bcC["r"] = e.to_reg(8 * CAP)
                return e.indirect_dma_start(
                    out=(y0sl if kk == 0 else y1sl)[par][:], out_offset=None,
                    in_=y_all,
                    in_offset=bass.IndirectOffsetOnAxis(
                        ticC[:, tc * 2 + kk:tc * 2 + kk + 1], 0),
                    bounds_check=bcC["r"], oob_is_err=False)
            tgys.append(P.dma("gp", gy,
                              deps=[tic_l[(tc, kk)], relG.get(par)] + tms_y + t_ag_y_l))
        ta = P.op("dv", lambda e, tc=tc, par=par: e.tensor_scalar(
            out=accC[:], in0=y0sl[par][:], scalar1=wghC[:, tc * 2:tc * 2 + 1],
            scalar2=None, op0=ALU.mult), deps=tgys + [tic_l[(tc, 1)]])
        ta = P.op("dv", lambda e, tc=tc, par=par: e.scalar_tensor_tensor(
            out=accC[:], in0=y1sl[par][:], scalar=wghC[:, tc * 2 + 1:tc * 2 + 2],
            in1=accC[:], op0=ALU.mult, op1=ALU.add))
        relG[par] = ta
        P.op("dv", lambda e: e.tensor_tensor(
            out=accC[:], in0=accC[:], in1=gbB[:], op=ALU.mult))
        to2 = P.op("dv", lambda e, par=par: e.tensor_tensor(
            out=ostg[:, par * D:(par + 1) * D], in0=accC[:],
            in1=x2stg[:, par * D:(par + 1) * D], op=ALU.add),
            deps=[tx, relO.get(par)])
        relX[par] = to2
        ts = P.dma("sy", lambda e, tc=tc, par=par: e.dma_start(
            out=outo[tc * 128:(tc + 1) * 128, :], in_=ostg[:, par * D:(par + 1) * D]),
            deps=[to2])
        relO[par] = ts
        ST.append(ts)

    blk = es.enter_context(nc.Block())
    P.emit(blk)
    P.tail_wait(blk, ST)
    return nc, es


HOSTC = {}


def host_consts():
    if HOSTC:
        return HOSTC
    import ml_dtypes
    bf = ml_dtypes.bfloat16
    HOSTC["idf32"] = np.eye(128, dtype=np.float32)
    HOSTC["idbf"] = np.eye(128, dtype=np.float32).astype(bf)
    HOSTC["onesf32"] = np.ones((1, 128), np.float32)
    HOSTC["onesbf"] = np.ones((1, 128), bf)
    HOSTC["ones4k_bf"] = np.ones((1, TK), bf)
    HOSTC["ut_bf"] = np.triu(np.ones((128, 128), np.float32), 1).astype(bf)
    HOSTC["all1_bf"] = np.ones((128, 128), bf)
    eqk = np.zeros((128, 8 * 16), np.float32)
    for oc in range(8):
        for p in range(128):
            eqk[p, oc * 16 + 2 * oc + p // 64] = 1.0
    HOSTC["eq_k"] = eqk.astype(bf)
    eqr = np.zeros((128, 4 * 16), np.float32)
    for oc in range(4):
        for p in range(128):
            eqr[p, oc * 16 + 4 * oc + p // 32] = 1.0
    HOSTC["eq_r"] = eqr.astype(bf)
    HOSTC["iotaEi"] = np.tile(np.arange(8).astype(np.float32), (128, 1))
    return HOSTC


def host_inputs_a(inputs, core):
    hc = host_consts()
    b, qh = core // 2, core % 2
    x = np.asarray(inputs["L0_attn_x"][b])
    x_kv = np.concatenate([x[qh * TQ:(qh + 1) * TQ], x[(1 - qh) * TQ:(2 - qh) * TQ]], 0)
    c = np.asarray(inputs["L0_c"][b])
    t = np.arange(TQ)
    gtok = ((CH2 * core + t + 7 * CH2 * (t // CH2) + 1).astype(np.float32)
            .reshape(16, 128).T)
    d = {
        "x_kv": np.ascontiguousarray(x_kv),
        "c_colT": np.ascontiguousarray(c.reshape(8, 128).T),
        "wmod": np.asarray(inputs["L0_wmod"]),
        "wq": np.asarray(inputs["L0_attn_Wq"]), "wk": np.asarray(inputs["L0_attn_Wk"]),
        "wv": np.asarray(inputs["L0_attn_Wv"]), "wr": np.asarray(inputs["L0_attn_Wr"]),
        "wo": np.asarray(inputs["L0_attn_Wo"]),
        "bq": np.asarray(inputs["L0_attn_bq"]).reshape(1, -1),
        "bk": np.asarray(inputs["L0_attn_bk"]).reshape(1, -1),
        "bv": np.asarray(inputs["L0_attn_bv"]).reshape(1, -1),
        "br": np.asarray(inputs["L0_attn_br"]).reshape(1, -1),
        "bo": np.asarray(inputs["L0_attn_bo"]).reshape(1, -1),
        "wg_t": np.ascontiguousarray(
            np.asarray(inputs["L0_ffn_wg"]).reshape(8, 128, 8).transpose(1, 0, 2).reshape(128, 64)),
        "gtokp1": np.ascontiguousarray(gtok),
    }
    for k in ("idf32", "idbf", "onesf32", "onesbf", "ones4k_bf", "ut_bf", "all1_bf",
              "eq_k", "eq_r", "iotaEi"):
        d[k] = hc[k]
    return d


def host_inputs(inputs, core):
    d = host_inputs_a(inputs, core)
    w1 = np.asarray(inputs["L0_ffn_w1"][core])
    d["w1t"] = np.ascontiguousarray(
        w1.reshape(8, 128, 32, 128).transpose(2, 1, 0, 3).reshape(32 * 128, 1024))
    d["w2m"] = np.asarray(inputs["L0_ffn_w2"][core])
    d["b1m"] = np.ascontiguousarray(
        np.asarray(inputs["L0_ffn_b1"][core]).reshape(32, 128).T)
    d["b2r"] = np.asarray(inputs["L0_ffn_b2"][core]).reshape(1, -1)
    s16 = np.zeros((16, 16), np.float32)
    s16[np.arange(8), 8 + np.arange(8)] = 1.0
    d["s16"] = s16
    exr0 = np.zeros((16, 1), np.float32); exr0[core, 0] = 1.0
    exr1 = np.zeros((16, 1), np.float32); exr1[8 + core, 0] = 1.0
    exs0 = np.zeros((8, 16), np.float32); exs0[:, core] = 1.0
    exs1 = np.zeros((8, 16), np.float32); exs1[:, 8 + core] = 1.0
    d["exr0"], d["exr1"], d["exs0"], d["exs1"] = exr0, exr1, exs0, exs1
    d["iotad"] = (np.arange(8, dtype=np.float32) * CAP).reshape(8, 1)
    d["iota128"] = np.arange(128, dtype=np.float32).reshape(128, 1)
    rkoh = np.zeros((16, 8), np.float32); rkoh[:, core] = 1.0
    d["rkoh"] = rkoh
    return d


_CACHE = {}


def kernel(**inputs):
    if "F" not in _CACHE:
        _CACHE["F"] = build_fused()
    ncF, _ = _CACHE["F"]
    cores = list(range(8))
    res = run_bass_kernel_spmd(ncF, [host_inputs(inputs, c) for c in cores],
                               core_ids=cores)
    out = np.zeros((4, 4096, 1024), np.float32)
    for c in cores:
        b, qh = c // 2, c % 2
        out[b, qh * 2048:(qh + 1) * 2048] = np.asarray(res.results[c]["outo"])
    return out


# version 14
# speedup vs baseline: 1.0049x; 1.0049x over previous
import contextlib
import math
import numpy as np
import concourse.bass as bass
import concourse.mybir as mybir
from concourse.bass_utils import run_bass_kernel_spmd

class _Op:
    __slots__ = ("fn", "deps", "inc", "inck", "tokv", "sem", "tag")

    def __init__(self, fn, deps, inc, inck, tokv, sem, tag):
        self.fn = fn
        self.deps = deps
        self.inc = inc
        self.inck = inck
        self.tokv = tokv
        self.sem = sem
        self.tag = tag


class Prog:
    ENG = ("pe", "dv", "ac", "gp", "sy")
    CHAIN = {"dv", "ac", "gp"}
    DMASEM = {"sy": "sy_dma", "gp": "gp_dma"}

    def __init__(self, nc, es):
        self.nc = nc
        self.es = es
        self.ops = {e: [] for e in self.ENG}
        self.last_compute = {e: None for e in self.ENG}
        self.sems = {}
        self.base_deps = []
        self._ccn = 0
        self.last_cc = None
        for name in ("pe_s", "dv_s", "ac_s", "gp_s", "sy_dma", "gp_dma"):
            self.sems[name] = [es.enter_context(nc.semaphore(name)), 0]

    def barrier(self):
        return [(n, s[1]) for n, s in self.sems.items()
                if s[1] > 0 and not n.startswith("cc")]

    def cc(self, fn, deps=(), tag=""):
        name = f"cc{self._ccn}"
        self._ccn += 1
        self.sems[name] = [self.es.enter_context(self.nc.semaphore(name)), 1]
        mydeps = [d for d in deps if d is not None] + list(self.base_deps)
        o = _Op(fn, mydeps, True, None, 1, name, tag)
        self.ops["gp"].append(o)
        self.last_cc = (name, 1)
        return (name, 1)

    def op(self, eng, fn, deps=(), tok=False, tag=""):
        sem = eng + "_s"
        inc = bool(tok) or (eng in self.CHAIN)
        mydeps = [d for d in deps if d is not None] + list(self.base_deps)
        if eng in self.CHAIN and self.last_compute[eng] is not None:
            mydeps.append(self.last_compute[eng])
        tokv = None
        if inc:
            self.sems[sem][1] += 1
            tokv = self.sems[sem][1]
        o = _Op(fn, mydeps, inc, 1, tokv, sem, tag)
        self.ops[eng].append(o)
        token = (sem, tokv) if inc else None
        if eng in self.CHAIN:
            self.last_compute[eng] = token
        return token

    def dma_on(self, eng, fn, sem_name, deps=(), tag=""):
        if sem_name not in self.sems:
            self.sems[sem_name] = [self.es.enter_context(self.nc.semaphore(sem_name)), 0]
        self.sems[sem_name][1] += 16
        tokv = self.sems[sem_name][1]
        mydeps = [d for d in deps if d is not None] + list(self.base_deps)
        o = _Op(fn, mydeps, True, 16, tokv, sem_name, tag)
        self.ops[eng].append(o)
        return (sem_name, tokv)

    def dma(self, eng, fn, deps=(), tag=""):
        sem = self.DMASEM[eng]
        self.sems[sem][1] += 16
        tokv = self.sems[sem][1]
        mydeps = [d for d in deps if d is not None] + list(self.base_deps)
        o = _Op(fn, mydeps, True, 16, tokv, sem, tag)
        self.ops[eng].append(o)
        return (sem, tokv)

    def emit(self, block):
        builders = {"pe": block.tensor, "dv": block.vector, "ac": block.scalar,
                    "gp": block.gpsimd, "sy": block.sync}
        semh = {n: s[0] for n, s in self.sems.items()}
        for eng in self.ENG:
            ops = self.ops[eng]
            if not ops:
                continue

            def body(e, ops=ops):
                hw = {}
                for o in ops:
                    need = {}
                    for (sn, v) in o.deps:
                        if v and v > need.get(sn, 0):
                            need[sn] = v
                    for sn, v in sorted(need.items()):
                        if v > hw.get(sn, 0):
                            e.wait_ge(semh[sn], v)
                            hw[sn] = v
                    ins = o.fn(e)
                    if o.inc:
                        if o.inck is None:
                            ins.then_inc(semh[o.sem])
                        else:
                            ins.then_inc(semh[o.sem], o.inck)
            builders[eng](body)

    def tail_wait(self, block, tokens):
        semh = {n: s[0] for n, s in self.sems.items()}

        def body(e):
            need = {}
            for (sn, v) in tokens:
                if v and v > need.get(sn, 0):
                    need[sn] = v
            for sn, v in sorted(need.items()):
                e.wait_ge(semh[sn], v)
        block.sync(body)


F32 = mybir.dt.float32
BF16 = mybir.dt.bfloat16
I32 = mybir.dt.int32
AF = mybir.ActivationFunctionType
ALU = mybir.AluOpType
AX = mybir.AxisListType

D = 1024
TQ = 2048
TK = 4096
NH = 16
CAP = 4096
SCALE = 1.0 / math.sqrt(96.0)
NCHUNK = 4
CH2 = TQ // NCHUNK
YCH = CAP // NCHUNK


def build_fused(dbg=(), upto=99):
    nc = bass.Bass("TRN2", target_bir_lowering=False, debug=False)
    I = lambda n, s, dt=F32: nc.dram_tensor(n, s, dt, kind="ExternalInput").ap()
    O = lambda n, s, dt=F32: nc.dram_tensor(n, s, dt, kind="ExternalOutput").ap()
    N = lambda n, s, dt=BF16: nc.dram_tensor(n, s, dt).ap()

    x_kv = I("x_kv", [TK, D])
    c_colT = I("c_colT", [128, 8])
    wmod = I("wmod", [D, 6 * D])
    WQ, WK, WV, WO = (I(n, [D, D]) for n in ("wq", "wk", "wv", "wo"))
    WR = I("wr", [D, 512])
    BQ, BK, BV, BO = (I(n, [1, D]) for n in ("bq", "bk", "bv", "bo"))
    BR = I("br", [1, 512])
    wg_t = I("wg_t", [128, 64])
    idf32 = I("idf32", [128, 128])
    idbf = I("idbf", [128, 128], BF16)
    onesf32 = I("onesf32", [1, 128])
    onesbf = I("onesbf", [1, 128], BF16)
    ones4k_bf = I("ones4k_bf", [1, TK], BF16)
    ut_bf = I("ut_bf", [128, 128], BF16)
    all1_bf = I("all1_bf", [128, 128], BF16)
    eq_k = I("eq_k", [128, 8 * 16], BF16)
    eq_r = I("eq_r", [128, 4 * 16], BF16)
    gtokp1 = I("gtokp1", [128, 16])
    iotaEi = I("iotaEi", [128, 8])
    w1t = I("w1t", [32 * 128, D])
    w2m = I("w2m", [4096, D])
    b1m = I("b1m", [128, 32])
    b2r = I("b2r", [1, D])
    s16 = I("s16", [16, 16])
    exr0 = I("exr0", [16, 1])
    exr1 = I("exr1", [16, 1])
    exs0 = I("exs0", [8, 16])
    exs1 = I("exs1", [8, 16])
    iotad = I("iotad", [8, 1])
    iota128 = I("iota128", [128, 1])
    rkoh = I("rkoh", [16, 8])

    outo = O("outo", [TQ, D])
    x2_dram = N("x2_dram", [TQ, D], F32)
    lsort_loc = N("lsort_loc", [CAP, 1], F32)
    cnt_loc = N("cnt_loc", [1, 16], F32)
    h2ch = [N(f"h2ch{c}", [CH2, D]) for c in range(NCHUNK)]
    h2_all = N("h2_all", [8 * TQ, D])
    lsort_all = N("lsort_all", [8 * CAP, 1], F32)
    cnt_all_d = N("cnt_all", [8, 16], F32)
    ylc = [N(f"ylc{c}", [YCH, D]) for c in range(NCHUNK)]
    y_all = N("y_all", [8 * CAP, D])
    dbg_out = {}
    for name, shp, dt in [
        ("d_mod", [1, 6 * D], F32), ("d_xn", [TK, D], BF16),
        ("d_kt", [D, TK], BF16), ("d_qt", [D, TQ], BF16),
        ("d_rt", [512, TK], BF16), ("d_vp", [TK, 1040], BF16),
        ("d_normk", [16, TK], BF16), ("d_negm", [16, TQ], BF16),
        ("d_attn", [128, 8 * TQ], BF16), ("d_logits", [TQ, 8], F32),
        ("d_gab", [128, D], F32), ("d_modraw", [1, 6 * D], F32),
        ("d_wops", [TQ, D], F32), ("d_wos", [128, 8 * D], BF16),
    ]:
        if name in dbg:
            dbg_out[name] = nc.dram_tensor(name, shp, dt, kind="ExternalOutput").ap()

    xn_dram = N("xn_dram", [TK, D])
    kt_dram = N("kt_dram", [D, TK])
    rt_dram = N("rt_dram", [512, TK])
    qt_dram = N("qt_dram", [D, TQ])
    vp_dram = N("vp_dram", [TK, 1040])
    negm_dram = N("negm_dram", [16, TQ])

    es = contextlib.ExitStack()
    sb = lambda n, s, dt=F32: es.enter_context(nc.sbuf_tensor(n, s, dt))
    P = Prog(nc, es)
    ST = []

    id_f = sb("id_f", [128, 128])
    ones_f = sb("ones_f", [1, 128])
    iotaE = sb("iotaE", [128, 8])
    rt_cols = sb("rt_cols", [128, 16 * 8])
    gbB = sb("gbB", [128, D])
    phA = contextlib.ExitStack()
    sa = lambda n, s, dt=F32: phA.enter_context(nc.sbuf_tensor(n, s, dt))
    id_b = sa("id_b", [128, 128], BF16)
    ones_b = sa("ones_b", [1, 128], BF16)
    ut_b = sa("ut_b", [128, 128], BF16)
    all1_b = sa("all1_b", [128, 128], BF16)
    eqk_b = sa("eqk_b", [128, 8 * 16], BF16)
    eqr_b = sa("eqr_b", [128, 4 * 16], BF16)
    gtok_f = sa("gtok_f", [128, 16])
    wg_b = sa("wg_b", [128, 64], BF16)
    caCol1 = sa("caCol1", [128, 8])
    saColb = sa("saColb", [128, 8], BF16)
    cb1B = sa("cb1B", [128, D])
    sbB = sa("sbB", [128, D])
    bvB = sa("bvB", [128, D])
    gabo_b = sa("gabo_b", [1, D], BF16)
    bqCol = sa("bqCol", [128, 8])
    bkCol = sa("bkCol", [128, 8])
    brCol = sa("brCol", [128, 4])
    normk = sa("normk", [16, TK], BF16)
    normq = sa("normq", [16, TQ], BF16)
    negm_s = sa("negm_s", [16, TQ], BF16)
    wo_s = sa("wo_s", [128, 8 * D], BF16)
    m0_s = sa("m0_s", [128, 16 * 8], BF16)
    m1_s = sa("m1_s", [128, 16 * 8], BF16)
    xbuf = sa("xbuf", [128, 2 * D])
    sqscr = sa("sqscr", [128, D], BF16)
    epscol = sa("epscol", [128, 1])
    c_s = sa("c_s", [128, 8])
    wgt_f = sa("wgt_f", [128, 64])
    xcbuf = sa("xcbuf", [128, 2 * D])
    xnstg = sa("xnstg", [128, 2 * D], BF16)
    stat = sa("stat", [128, 8])
    kmc = sa("kmc", [16, 2])
    phW = contextlib.ExitStack()
    wq_s = phW.enter_context(nc.sbuf_tensor("wq_s", [128, 8 * D], BF16))
    wk_s = phW.enter_context(nc.sbuf_tensor("wk_s", [128, 8 * D], BF16))
    wv_s = phW.enter_context(nc.sbuf_tensor("wv_s", [128, 8 * D], BF16))
    wr_s = phW.enter_context(nc.sbuf_tensor("wr_s", [128, 8 * 512], BF16))
    ph01 = contextlib.ExitStack()
    mod_row = ph01.enter_context(nc.sbuf_tensor("mod_row", [1, 6 * D], F32))
    brow = ph01.enter_context(nc.sbuf_tensor("brow", [1, 5 * D], F32))
    beff = ph01.enter_context(nc.sbuf_tensor("beff", [1, 2 * D], F32))
    gaB = ph01.enter_context(nc.sbuf_tensor("gaB", [128, D], F32))

    for a, b_ in [(id_f[:], idf32), (id_b[:], idbf), (ones_f[:], onesf32),
                  (ones_b[:], onesbf), (ut_b[:], ut_bf), (all1_b[:], all1_bf),
                  (eqk_b[:], eq_k), (eqr_b[:], eq_r), (gtok_f[:], gtokp1),
                  (iotaE[:], iotaEi),
                  (brow[0:1, 0:D], BQ), (brow[0:1, D:2 * D], BK),
                  (brow[0:1, 2 * D:3 * D], BV), (brow[0:1, 3 * D:4 * D], BO),
                  (brow[0:1, 4 * D:4 * D + 512], BR)]:
        P.dma("sy", lambda e, a=a, b_=b_: e.dma_start(out=a, in_=b_))
    tc1 = P.dma("sy", lambda e: e.dma_start(out=c_s[:], in_=c_colT))
    tc2 = P.dma("sy", lambda e: e.dma_start(out=wgt_f[:], in_=wg_t))
    P.op("dv", lambda e: e.tensor_copy(out=wg_b[:], in_=wgt_f[:]), deps=[tc2])

    ph0 = contextlib.ExitStack()
    wm_s = ph0.enter_context(nc.sbuf_tensor("wm_s", [128, 4096], BF16))
    mod_ps = ph0.enter_context(nc.psum_tensor("mod_ps", [1, 512], F32))
    tp_ps = ph0.enter_context(nc.psum_tensor("tp_ps", [128, 8], F32))
    bc_ps = ph0.enter_context(nc.psum_tensor("bc_ps", [128, 512], F32))
    sig = ph0.enter_context(nc.sbuf_tensor("sig", [128, 8], F32))
    silu_b = ph0.enter_context(nc.sbuf_tensor("silu_b", [128, 8], BF16))

    teps = P.op("dv", lambda e: e.memset(epscol[:], 1e-5))
    tsg = P.op("ac", lambda e: e.activation(out=sig[:], in_=c_s[:], func=AF.Exp,
                                            scale=-1.0), deps=[tc1])
    P.op("dv", lambda e: e.tensor_scalar_add(out=sig[:], in0=sig[:], scalar1=1.0),
         deps=[tsg])
    P.op("dv", lambda e: e.reciprocal(out=sig[:], in_=sig[:]))
    P.op("dv", lambda e: e.tensor_tensor(out=sig[:], in0=sig[:], in1=c_s[:], op=ALU.mult))
    tsilu = P.op("dv", lambda e: e.tensor_copy(out=silu_b[:], in_=sig[:]))

    mod_evac = []
    tm_prev = None
    for cb in range(12):
        tw = P.dma("gp", lambda e, cb=cb: e.dma_start(
            out=wm_s[:].rearrange("p (dc f) -> p dc f", dc=8),
            in_=wmod[:, cb * 512:(cb + 1) * 512].rearrange("(dc p) f -> p dc f", p=128)),
            deps=([tm_prev] if tm_prev else []))
        tm = None
        for dc in range(8):
            tm = P.op("pe", lambda e, dc=dc: e.matmul(
                mod_ps[:], silu_b[:, dc:dc + 1],
                wm_s[:, dc * 512:(dc + 1) * 512],
                start=(dc == 0), stop=(dc == 7)),
                deps=[tw, tsilu] + (mod_evac[-1:] if mod_evac else []), tok=(dc == 7))
        tm_prev = tm
        mod_evac.append(P.op("dv", lambda e, cb=cb: e.tensor_copy(
            out=mod_row[0:1, cb * 512:(cb + 1) * 512], in_=mod_ps[0:1, :]), deps=[tm]))

    tclamp = mod_evac[-1]
    for base in (2 * D, 5 * D):
        r = mod_row[0:1, base:base + D]
        P.op("dv", lambda e, r=r: e.tensor_scalar(
            out=beff[0:1, 0:D], in0=r, scalar1=0.0, scalar2=None, op0=ALU.is_ge))
        P.op("dv", lambda e: e.tensor_scalar(
            out=beff[0:1, 0:D], in0=beff[0:1, 0:D], scalar1=2.0, scalar2=-1.0,
            op0=ALU.mult, op1=ALU.add))
        P.op("dv", lambda e, r=r: e.tensor_tensor(
            out=beff[0:1, D:2 * D], in0=r, in1=beff[0:1, 0:D], op=ALU.mult))
        P.op("dv", lambda e: e.tensor_scalar_max(
            out=beff[0:1, D:2 * D], in0=beff[0:1, D:2 * D], scalar1=0.25))
        tclamp = P.op("dv", lambda e, r=r: e.tensor_tensor(
            out=r, in0=beff[0:1, 0:D], in1=beff[0:1, D:2 * D], op=ALU.mult))
    garow = mod_row[0:1, 2 * D:3 * D]
    gbrow = mod_row[0:1, 5 * D:6 * D]
    if "d_mod" in dbg_out:
        ST.append(P.dma("sy", lambda e: e.dma_start(out=dbg_out["d_mod"],
                                                    in_=mod_row[0:1, :]), deps=[tclamp]))

    tt = None
    for j in range(8):
        tt = P.op("pe", lambda e, j=j: e.transpose(
            tp_ps[:, j:j + 1], mod_row[0:1, D + j * 128:D + (j + 1) * 128], id_f[0:1, 0:1]),
            deps=[mod_evac[-1]], tok=(j == 7))
    tca = P.op("dv", lambda e: e.tensor_scalar_add(out=caCol1[:], in0=tp_ps[:], scalar1=1.0),
               deps=[tt])
    tt = None
    for j in range(8):
        tt = P.op("pe", lambda e, j=j: e.transpose(
            tp_ps[:, j:j + 1], mod_row[0:1, j * 128:(j + 1) * 128], id_f[0:1, 0:1]),
            deps=[tca], tok=(j == 7))
    tsa = P.op("dv", lambda e: e.tensor_copy(out=saColb[:], in_=tp_ps[:]), deps=[tt])

    tprev = tclamp
    for (dst, base, add1) in ((cb1B, 4 * D, True), (sbB, 3 * D, False),
                              (gaB, 2 * D, False), (gbB, 5 * D, False)):
        for half in range(2):
            tm = P.op("pe", lambda e, base=base, half=half: e.matmul(
                bc_ps[:], ones_f[:], mod_row[0:1, base + half * 512:base + (half + 1) * 512],
                start=True, stop=True), deps=[tprev, tclamp], tok=True)
            tprev = P.op("dv", lambda e, dst=dst, half=half, add1=add1:
                         (e.tensor_scalar_add(out=dst[:, half * 512:(half + 1) * 512],
                                              in0=bc_ps[:], scalar1=1.0) if add1 else
                          e.tensor_copy(out=dst[:, half * 512:(half + 1) * 512],
                                        in_=bc_ps[:])), deps=[tm])
    tgabo = P.op("dv", lambda e: e.tensor_tensor(
        out=gabo_b[0:1, :], in0=garow, in1=brow[0:1, 3 * D:4 * D], op=ALU.mult))
    if "d_gab" in dbg_out:
        ST.append(P.dma("sy", lambda e: e.dma_start(out=dbg_out["d_gab"], in_=gaB[:]),
                        deps=[tprev]))
    tiles_done = tprev
    ph0.close()
    if upto < 1:
        blk = es.enter_context(nc.Block())
        P.emit(blk)
        P.tail_wait(blk, ST)
        return nc, es

    ph1 = contextlib.ExitStack()
    bias_ps = ph1.enter_context(nc.psum_tensor("bias_ps", [1, 512], F32))
    tpb_ps = ph1.enter_context(nc.psum_tensor("tpb_ps", [128, 8], F32))
    bcb_ps = ph1.enter_context(nc.psum_tensor("bcb_ps", [128, 512], F32))

    wload = {}
    for (name, wsb, src) in (("q", wq_s, WQ), ("k", wk_s, WK), ("v", wv_s, WV),
                             ("r", wr_s, WR), ("o", wo_s, WO)):
        cw = wsb.shape[1] // 8
        wload[name] = [P.dma("gp", lambda e, wsb=wsb, src=src, dc=dc, cw=cw: e.dma_start(
            out=wsb[:, dc * cw:(dc + 1) * cw], in_=src[dc * 128:(dc + 1) * 128, :]))
            for dc in range(8)]

    bias_done = {}
    fold_done = []
    prev_bias_ps = None
    prev_tp = tsa
    prev_bc = tiles_done
    for (name, wsb, no, boff, col_dst) in (
            ("q", wq_s, 8, 0, bqCol), ("k", wk_s, 8, D, bkCol),
            ("r", wr_s, 4, 4 * D, brCol), ("v", wv_s, 8, 2 * D, None)):
        cw = no * 128
        ev = []
        nhalf = (no * 128) // 512
        for half in range(nhalf):
            tm = None
            for dc in range(8):
                tm = P.op("pe", lambda e, wsb=wsb, dc=dc, half=half, cw=cw: e.matmul(
                    bias_ps[:], saColb[:, dc:dc + 1],
                    wsb[:, dc * cw + half * 512:dc * cw + half * 512 + 512],
                    start=(dc == 0), stop=(dc == 7)),
                    deps=[wload[name][dc], tsa, prev_bias_ps], tok=(dc == 7))
            tev = P.op("dv", lambda e, boff=boff, half=half: e.tensor_tensor(
                out=beff[0:1, half * 512:(half + 1) * 512], in0=bias_ps[0:1, :],
                in1=brow[0:1, boff + half * 512:boff + (half + 1) * 512], op=ALU.add),
                deps=[tm])
            prev_bias_ps = tev
            ev.append(tev)
        if col_dst is not None:
            tt = None
            for j in range(no):
                tt = P.op("pe", lambda e, j=j: e.transpose(
                    tpb_ps[:, j:j + 1], beff[0:1, j * 128:(j + 1) * 128], id_f[0:1, 0:1]),
                    deps=ev + [prev_tp], tok=(j == no - 1))
            tcd = P.op("dv", lambda e, col_dst=col_dst, no=no: e.tensor_copy(
                out=col_dst[:, 0:no], in_=tpb_ps[:, 0:no]), deps=[tt])
            prev_tp = tcd
            bias_done[name] = tcd
        else:
            tb = None
            for half in range(2):
                tm = P.op("pe", lambda e, half=half: e.matmul(
                    bcb_ps[:], ones_f[:], beff[0:1, half * 512:(half + 1) * 512],
                    start=True, stop=True), deps=ev + [prev_bc], tok=True)
                tb = P.op("dv", lambda e, half=half: e.tensor_copy(
                    out=bvB[:, half * 512:(half + 1) * 512], in_=bcb_ps[:]), deps=[tm])
                prev_bc = tb
            bias_done[name] = tb
        for dc in range(8):
            fold_done.append(P.op("dv", lambda e, wsb=wsb, dc=dc, cw=cw: e.tensor_scalar(
                out=wsb[:, dc * cw:(dc + 1) * cw], in0=wsb[:, dc * cw:(dc + 1) * cw],
                scalar1=caCol1[:, dc:dc + 1], scalar2=None, op0=ALU.mult),
                deps=[bias_done[name], tca]))
    for dc in range(8):
        fold_done.append(P.op("dv", lambda e, dc=dc: e.tensor_tensor(
            out=wo_s[:, dc * D:(dc + 1) * D], in0=wo_s[:, dc * D:(dc + 1) * D],
            in1=gaB[:], op=ALU.mult), deps=[wload["o"][dc], tiles_done]))
    if "d_wos" in dbg_out:
        ST.append(P.dma("sy", lambda e: e.dma_start(out=dbg_out["d_wos"], in_=wo_s[:]),
                        deps=fold_done[-8:]))
    ph1.close()
    ph01.close()
    if upto < 2:
        blk = es.enter_context(nc.Block())
        P.emit(blk)
        P.tail_wait(blk, ST)
        return nc, es

    xn_done = []
    relL, relC, relN = {}, {}, {}
    for i in range(32):
        par = i % 2
        xb = xbuf[:, par * D:(par + 1) * D]
        xc = xcbuf[:, par * D:(par + 1) * D]
        xn = xnstg[:, par * D:(par + 1) * D]
        st = stat[:, par * 4:par * 4 + 4]
        tl = P.dma("sy", lambda e, i=i, xb=xb: e.dma_start(
            out=xb, in_=x_kv[i * 128:(i + 1) * 128, :]), deps=[relL.get(par)])
        P.op("dv", lambda e, xb=xb, st=st: e.reduce_sum(out=st[:, 0:1], in_=xb, axis=AX.X),
             deps=[tl])
        P.op("dv", lambda e, st=st: e.tensor_scalar(
            out=st[:, 1:2], in0=st[:, 0:1], scalar1=1.0 / D, scalar2=None, op0=ALU.mult))
        txc = P.op("dv", lambda e, xb=xb, xc=xc, st=st: e.tensor_scalar(
            out=xc, in0=xb, scalar1=st[:, 1:2], scalar2=None, op0=ALU.subtract),
            deps=[relC.get(par)])
        relL[par] = txc
        P.op("ac", lambda e, xc=xc, st=st: e.activation(
            out=sqscr[:], in_=xc, func=AF.Square, accum_out=st[:, 2:3]), deps=[txc])
        P.op("ac", lambda e, st=st: e.activation(
            out=st[:, 3:4], in_=st[:, 2:3], func=AF.Ln, scale=1.0 / D,
            bias=epscol[:, 0:1]), deps=[teps])
        trho = P.op("ac", lambda e, st=st: e.activation(
            out=st[:, 3:4], in_=st[:, 3:4], func=AF.Exp, scale=-0.5))
        txn = P.op("dv", lambda e, xc=xc, xn=xn, st=st: e.tensor_scalar(
            out=xn, in0=xc, scalar1=st[:, 3:4], scalar2=None, op0=ALU.mult),
            deps=[trho, relN.get(par)])
        relC[par] = txn
        ts = P.dma_on("sy", lambda e, i=i, xn=xn: e.dma_start(
            out=xn_dram[i * 128:(i + 1) * 128, :], in_=xn),
            f"xnst{i // 16}", deps=[txn])
        relN[par] = ts
        xn_done.append(ts)
    if "d_xn" in dbg_out:
        ST.append(P.dma("sy", lambda e: e.dma_start(out=dbg_out["d_xn"], in_=xn_dram),
                        deps=xn_done))
    if upto < 3:
        blk = es.enter_context(nc.Block())
        P.emit(blk)
        P.tail_wait(blk, ST)
        return nc, es

    ph3 = contextlib.ExitStack()
    xnT = ph3.enter_context(nc.sbuf_tensor("xnT", [128, 8 * TK], BF16))
    xnT_tok = [[P.dma("sy", lambda e, dcx=dcx, hf=hf: e.dma_start(
        out=xnT[:, dcx * TK + hf * 2048:dcx * TK + (hf + 1) * 2048],
        in_=xn_dram[hf * 2048:(hf + 1) * 2048, dcx * 128:(dcx + 1) * 128],
        transpose=True), deps=xn_done[hf * 16:hf * 16 + 16])
        for hf in range(2)] for dcx in range(8)]

    pj_ps = [ph3.enter_context(nc.psum_tensor(f"pj_ps{j}", [128, 512], F32)) for j in range(2)]
    nk_ps = ph3.enter_context(nc.psum_tensor("nk_ps", [16, 512], F32))
    nq_ps = ph3.enter_context(nc.psum_tensor("nq_ps", [16, 512], F32))
    vps = [[ph3.enter_context(nc.psum_tensor(f"v_ps{p}_{j}", [128, 512], F32))
            for j in range(2)] for p in range(2)]
    kstgs = ph3.enter_context(nc.sbuf_tensor("kstg", [128, 2 * 512], BF16))
    sqstg = ph3.enter_context(nc.sbuf_tensor("sqstg", [128, 2 * 512], BF16))
    vstg = ph3.enter_context(nc.sbuf_tensor("vstg", [128, 2 * 1040], BF16))

    relPP, relKS, relSQ = {}, {}, {}
    state = {"cnt": 0}
    kt_toks, qt_toks, v_toks = [], [], []
    nk_evac, nq_evac = [], []

    def one_proj(blkidx, wsb, cw, oc, dst, bcol, bcol_tok, eqt, nps, nst, nstop, nprev=None,
                 nps2=None, nst2=False, nstop2=False, nprev2=None):
        par = state["cnt"] % 2
        state["cnt"] += 1
        kst = kstgs[:, par * 512:(par + 1) * 512]
        sst = sqstg[:, par * 512:(par + 1) * 512]
        pp = pj_ps[par]
        tm = None
        for dc in range(8):
            tm = P.op("pe", lambda e, wsb=wsb, cw=cw, oc=oc, dc=dc, blkidx=blkidx, pp=pp: e.matmul(
                pp[:], wsb[:, dc * cw + oc * 128:dc * cw + (oc + 1) * 128],
                xnT[:, dc * TK + blkidx * 512:dc * TK + (blkidx + 1) * 512],
                start=(dc == 0), stop=(dc == 7)),
                deps=[xnT_tok[dc][blkidx // 4], relPP.get(par)] + fold_done,
                tok=(dc == 7))
        tev = P.op("dv", lambda e, pp=pp, kst=kst, bcol=bcol, oc=oc: e.tensor_scalar(
            out=kst, in0=pp[:], scalar1=bcol[:, oc:oc + 1], scalar2=None, op0=ALU.add),
            deps=[tm, bcol_tok, relKS.get(par)])
        relPP[par] = tev
        tsq = P.op("ac", lambda e, kst=kst, sst=sst: e.activation(
            out=sst, in_=kst, func=AF.Square), deps=[tev, relSQ.get(par)])
        tnm = P.op("pe", lambda e, eqt=eqt, oc=oc, sst=sst, nps=nps, nst=nst, nstop=nstop: e.matmul(
            nps[:], eqt[:, oc * 16:(oc + 1) * 16], sst, start=nst, stop=nstop,
            skip_group_check=True), deps=[tsq, nprev], tok=True)
        if nps2 is not None:
            tnm = P.op("pe", lambda e, eqt=eqt, oc=oc, sst=sst, nps2=nps2, nst2=nst2, nstop2=nstop2: e.matmul(
                nps2[:], eqt[:, oc * 16:(oc + 1) * 16], sst, start=nst2, stop=nstop2,
                skip_group_check=True), deps=[tsq, nprev2], tok=True)
        relSQ[par] = tnm
        tst = P.dma("sy", lambda e, dst=dst, oc=oc, blkidx=blkidx, kst=kst: e.dma_start(
            out=dst[oc * 128:(oc + 1) * 128, blkidx * 512:(blkidx + 1) * 512], in_=kst),
            deps=[tev])
        relKS[par] = tst
        return tst, tnm

    last_nq_r = [None]
    for blkidx in range(8):
        qside = blkidx < 4
        last_nk = None
        for oc in range(8):
            tst, last_nk = one_proj(blkidx, wk_s, D, oc, kt_dram, bkCol,
                                    bias_done["k"], eqk_b, nk_ps, oc == 0, False,
                                    nprev=(nk_evac[-1] if (oc == 0 and nk_evac) else None))
            kt_toks.append(tst)
        if qside:
            for oc in range(8):
                tst, _ = one_proj(blkidx, wq_s, D, oc, qt_dram, bqCol,
                                  bias_done["q"], eqk_b, nq_ps, oc == 0, False,
                                  nprev=(nq_evac[-1] if (oc == 0 and nq_evac) else None))
                qt_toks.append(tst)
        for oc in range(4):
            tst, last_nk = one_proj(blkidx, wr_s, 512, oc, rt_dram, brCol,
                                    bias_done["r"], eqr_b, nk_ps, False, oc == 3,
                                    nps2=(nq_ps if qside else None), nst2=False,
                                    nstop2=(oc == 3))
            kt_toks.append(tst)
            if qside:
                last_nq_r[0] = relSQ[(state["cnt"] - 1) % 2]
        nk_evac.append(P.op("dv", lambda e, blkidx=blkidx: e.tensor_copy(
            out=normk[:, blkidx * 512:(blkidx + 1) * 512], in_=nk_ps[:]), deps=[last_nk]))
        if qside:
            nq_evac.append(P.op("dv", lambda e, blkidx=blkidx: e.tensor_copy(
                out=normq[:, blkidx * 512:(blkidx + 1) * 512], in_=nq_ps[:]),
                deps=[last_nq_r[0]]))


    tmset = []
    relVV, relVS = {}, {}
    for tc in range(32):
        par = tc % 2
        vst = vstg[:, par * 1040:(par + 1) * 1040]
        if tc < 2:
            tmset.append(P.op("gp", lambda e, vst=vst: e.memset(vst, 1.0)))
        tms = []
        for half in range(2):
            tm = None
            for dc in range(8):
                tm = P.op("pe", lambda e, tc=tc, dc=dc, par=par, half=half: e.matmul(
                    vps[par][half][:], xnT[:, dc * TK + tc * 128:dc * TK + (tc + 1) * 128],
                    wv_s[:, dc * D + half * 512:dc * D + (half + 1) * 512],
                    start=(dc == 0), stop=(dc == 7)),
                    deps=[xnT_tok[dc][tc // 16], relVV.get((par, half))] + fold_done,
                    tok=(dc == 7))
            tms.append(tm)
        tev = None
        for half in range(2):
            tev = P.op("dv", lambda e, vst=vst, par=par, half=half: e.tensor_tensor(
                out=vst.rearrange("p (h j) -> p h j", j=65)[:, half * 8:(half + 1) * 8, 0:64],
                in0=vps[par][half][:].rearrange("p (h j) -> p h j", j=64),
                in1=bvB[:, half * 512:(half + 1) * 512].rearrange("p (h j) -> p h j", j=64),
                op=ALU.add),
                deps=[tms[half], bias_done["v"], relVS.get(par)] + (tmset if tc < 2 else []))
            relVV[(par, half)] = tev
        ts = P.dma("sy", lambda e, tc=tc, vst=vst: e.dma_start(
            out=vp_dram[tc * 128:(tc + 1) * 128, :], in_=vst), deps=[tev])
        relVS[par] = ts
        v_toks.append(ts)

    t = P.op("dv", lambda e: e.reduce_max(out=kmc[:, 0:1], in_=normk[:], axis=AX.X),
             deps=nk_evac)
    t = P.op("ac", lambda e: e.activation(out=kmc[:, 1:2], in_=kmc[:, 0:1], func=AF.Ln),
             deps=[t])
    t = P.op("ac", lambda e: e.activation(out=kmc[:, 0:1], in_=kmc[:, 1:2], func=AF.Exp,
                                          scale=0.5), deps=[t])
    t2 = P.op("ac", lambda e: e.activation(out=normq[:], in_=normq[:], func=AF.Ln),
              deps=nq_evac)
    t2 = P.op("ac", lambda e: e.activation(out=normq[:], in_=normq[:], func=AF.Exp,
                                           scale=0.5), deps=[t2])
    tneg = P.op("dv", lambda e: e.tensor_scalar(
        out=negm_s[:], in0=normq[:], scalar1=kmc[:, 0:1], scalar2=-1.0,
        op0=ALU.mult, op1=ALU.mult), deps=[t, t2])
    tnegst = P.dma("sy", lambda e: e.dma_start(out=negm_dram, in_=negm_s[:]), deps=[tneg])
    for nm, dr, tks in (("d_kt", kt_dram, kt_toks), ("d_rt", rt_dram, kt_toks),
                        ("d_qt", qt_dram, qt_toks), ("d_vp", vp_dram, v_toks),
                        ("d_normk", normk[:], nk_evac), ("d_negm", negm_s[:], [tneg])):
        if nm in dbg_out:
            ST.append(P.dma("sy", lambda e, nm=nm, dr=dr: e.dma_start(
                out=dbg_out[nm], in_=dr), deps=tks))
    if upto < 4:
        blk = es.enter_context(nc.Block())
        P.emit(blk)
        P.tail_wait(blk, ST)
        return nc, es

    ph3.close()
    phW.close()
    ph4a = contextlib.ExitStack()
    attnT = ph4a.enter_context(nc.sbuf_tensor("attnT", [128, 8 * TQ], BF16))
    ph4 = contextlib.ExitStack()
    kbuf = [ph4.enter_context(nc.sbuf_tensor(f"kbuf{j}", [128, TK], BF16)) for j in range(2)]
    qbuf = [ph4.enter_context(nc.sbuf_tensor(f"qbuf{j}", [128, TQ], BF16)) for j in range(2)]
    vbuf = [ph4.enter_context(nc.sbuf_tensor(f"vbuf{j}", [128, 32 * 260], BF16)) for j in range(2)]
    pstg = [ph4.enter_context(nc.sbuf_tensor(f"pstg{j}", [128, 1536], BF16)) for j in range(2)]
    rlb_s = ph4.enter_context(nc.sbuf_tensor("rlb_s", [64, 512], F32))
    rl_s = ph4.enter_context(nc.sbuf_tensor("rl_s", [1, 512], F32))
    s_ps = [ph4.enter_context(nc.psum_tensor(f"s_ps{j}", [128, 1536], F32)) for j in range(2)]
    av_ps = ph4.enter_context(nc.psum_tensor("av_ps", [65, 512], F32))
    rlb_ps = ph4.enter_context(nc.psum_tensor("rlb_ps", [64, 512], F32))

    GRPS = []
    k0 = 0
    while k0 < 32:
        GRPS.append((k0, min(k0 + 3, 32)))
        k0 += 3

    kq_load, v_load, head_rel, vg_rel = {}, {}, {}, {}
    ones_init = [P.dma("sy", lambda e, j=j: e.dma_start(
        out=kbuf[j][96:97, :], in_=ones4k_bf)) for j in range(2)]

    def load_head(h):
        par = h % 2
        d0 = [kt_toks[-1], qt_toks[-1], tnegst, head_rel.get(h - 2), ones_init[par]]
        toks = [
            P.dma("sy", lambda e, h=h, par=par: e.dma_start(
                out=kbuf[par][0:64, :], in_=kt_dram[h * 64:(h + 1) * 64, :]), deps=d0),
            P.dma("sy", lambda e, h=h, par=par: e.dma_start(
                out=kbuf[par][64:96, :], in_=rt_dram[h * 32:(h + 1) * 32, :]), deps=d0),
            P.dma("sy", lambda e, h=h, par=par: e.dma_start(
                out=qbuf[par][0:64, :], in_=qt_dram[h * 64:(h + 1) * 64, :]), deps=d0),
            P.dma("sy", lambda e, h=h, par=par: e.dma_start(
                out=qbuf[par][64:96, :], in_=rt_dram[h * 32:(h + 1) * 32, 0:TQ]), deps=d0),
            P.dma("sy", lambda e, h=h, par=par: e.dma_start(
                out=qbuf[par][96:97, :], in_=negm_dram[h:h + 1, :]), deps=d0),
        ]
        kq_load[h] = toks

    def load_vg(g):
        par = g % 2
        deps = [vg_rel.get(g - 2), v_toks[-1]]
        v_load[g] = [P.dma("sy", lambda e, g=g, par=par, piece=piece: e.dma_start(
            out=vbuf[par][:, piece * 8 * 260:(piece + 1) * 8 * 260].rearrange(
                "p (kc j) -> p kc j", j=260),
            in_=vp_dram.rearrange("(kc p) (h j) -> p kc h j", p=128, j=65)[
                :, piece * 8:(piece + 1) * 8, 4 * g:4 * g + 4, :].rearrange(
                "p kc h j -> p kc (h j)")), deps=deps) for piece in range(4)]

    load_head(0)
    load_head(1)
    load_vg(0)
    load_vg(1)
    sg_rel, pst_rel = {}, {}
    attn_evac = []
    for h in range(NH):
        par = h % 2
        g = h // 4
        gpar = g % 2
        for qb in range(4):
            av_last = None
            tS_l = {}

            def emit_qk(gi, h=h, qb=qb, par=par):
                g0, g1 = GRPS[gi]
                spar = gi % 2
                tS = None
                for kc in range(g0, g1):
                    tS = P.op("pe", lambda e, h=h, qb=qb, kc=kc, g0=g0, spar=spar, par=par: e.matmul(
                        s_ps[spar][:, (kc - g0) * 512:(kc - g0 + 1) * 512],
                        kbuf[par][0:97, kc * 128:(kc + 1) * 128],
                        qbuf[par][0:97, qb * 512:(qb + 1) * 512],
                        start=True, stop=True, skip_group_check=True),
                        deps=kq_load[h] + [sg_rel.get(spar)], tok=(kc == g1 - 1))
                tS_l[gi] = tS
            emit_qk(0)
            for gi, (g0, g1) in enumerate(GRPS):
                spar = gi % 2
                if gi + 1 < len(GRPS):
                    emit_qk(gi + 1)
                tE = P.op("ac", lambda e, spar=spar, n=(g1 - g0): e.activation(
                    out=pstg[spar][:, 0:n * 512], in_=s_ps[spar][:, 0:n * 512],
                    func=AF.Exp, scale=SCALE), deps=[tS_l[gi], pst_rel.get(spar)])
                sg_rel[spar] = tE
                for kc in range(g0, g1):
                    av_last = P.op("pe", lambda e, h=h, kc=kc, g0=g0, gpar=gpar, spar=spar: e.matmul(
                        av_ps[:],
                        vbuf[gpar][:, kc * 260 + (h % 4) * 65:kc * 260 + (h % 4) * 65 + 65],
                        pstg[spar][:, (kc - g0) * 512:(kc - g0 + 1) * 512],
                        start=(kc == 0), stop=(kc == 31), skip_group_check=True),
                        deps=[tE] + v_load[g] + (attn_evac[-1:] if attn_evac else []),
                        tok=(kc == 31))
                pst_rel[spar] = av_last
            trl = P.op("dv", lambda e: e.reciprocal(out=rl_s[0:1, :], in_=av_ps[64:65, :]),
                       deps=[av_last])
            tbm = P.op("pe", lambda e: e.matmul(rlb_ps[:], ones_f[0:1, 0:64], rl_s[0:1, :],
                                                start=True, stop=True, skip_group_check=True),
                       deps=[trl], tok=True)
            tbe = P.op("dv", lambda e: e.tensor_copy(out=rlb_s[:], in_=rlb_ps[:]), deps=[tbm])
            tev = P.op("dv", lambda e, h=h, qb=qb: e.tensor_tensor(
                out=attnT[(h % 2) * 64:(h % 2) * 64 + 64,
                          (h // 2) * TQ + qb * 512:(h // 2) * TQ + (qb + 1) * 512],
                in0=av_ps[0:64, :], in1=rlb_s[:], op=ALU.mult), deps=[tbe])
            attn_evac.append(tev)
        head_rel[h] = attn_evac[-1]
        if h % 4 == 3:
            vg_rel[g] = attn_evac[-1]
        if h + 2 < NH:
            load_head(h + 2)
        if h % 4 == 3 and g + 2 < 4:
            load_vg(g + 2)
    if "d_attn" in dbg_out:
        ST.append(P.dma("sy", lambda e: e.dma_start(out=dbg_out["d_attn"], in_=attnT[:]),
                        deps=attn_evac))
    if upto < 5:
        blk = es.enter_context(nc.Block())
        P.emit(blk)
        P.tail_wait(blk, ST)
        return nc, es

    ph4.close()
    ph6s = contextlib.ExitStack()
    x2_s = ph6s.enter_context(nc.sbuf_tensor("x2_s", [128, 2 * D], F32))
    h2_stg = ph6s.enter_context(nc.sbuf_tensor("h2_stg", [128, 2 * D], BF16))
    xn2buf = ph6s.enter_context(nc.sbuf_tensor("xn2buf", [128, 2 * D], F32))
    stat2 = ph6s.enter_context(nc.sbuf_tensor("stat2", [128, 8], F32))
    ph6 = contextlib.ExitStack()
    wo_ps = [[ph6.enter_context(nc.psum_tensor(f"wo_ps{p}_{j}", [128, 512], F32))
              for j in range(2)] for p in range(2)]
    x_rel = dict(relL)
    relWO = {}
    relH2 = {}
    relX2S = {}
    h2_toks = []
    x2_toks = []
    for tc2 in range(16):
        par = tc2 % 2
        xb = xbuf[:, par * D:(par + 1) * D]
        tl = P.dma("sy", lambda e, tc2=tc2, xb=xb: e.dma_start(
            out=xb, in_=x_kv[tc2 * 128:(tc2 + 1) * 128, :]), deps=[x_rel.get(par)])
        tdone = []
        for half in range(2):
            for ac_ in range(8):
                P.op("pe", lambda e, tc2=tc2, ac_=ac_, par=par, half=half: e.matmul(
                    wo_ps[par][half][:],
                    attnT[:, ac_ * TQ + tc2 * 128:ac_ * TQ + (tc2 + 1) * 128],
                    wo_s[:, ac_ * D + half * 512:ac_ * D + (half + 1) * 512],
                    start=(ac_ == 0), stop=False, skip_group_check=True),
                    deps=[attn_evac[-1], relWO.get((par, half))] + fold_done[-8:])
            tm = P.op("pe", lambda e, par=par, half=half: e.matmul(
                wo_ps[par][half][:], ones_b[0:1, 0:128],
                gabo_b[0:1, half * 512:(half + 1) * 512], start=False, stop=True,
                skip_group_check=True), deps=[tgabo], tok=True)
            tdone.append(tm)
        if "d_wops" in dbg_out:
            for half in range(2):
                tdbg = P.op("dv", lambda e, par=par, half=half: e.tensor_copy(
                    out=xn2buf[:, half * 512:(half + 1) * 512], in_=wo_ps[par][half][:]),
                    deps=[tdone[half]])
            ST.append(P.dma("sy", lambda e, tc2=tc2: e.dma_start(
                out=dbg_out["d_wops"][tc2 * 128:(tc2 + 1) * 128, :],
                in_=xn2buf[:, 0:D]), deps=[tdbg]))
        tx2 = None
        for half in range(2):
            tx2 = P.op("dv", lambda e, tc2=tc2, par=par, half=half, xb=xb: e.tensor_tensor(
                out=x2_s[:, par * D + half * 512:par * D + (half + 1) * 512],
                in0=wo_ps[par][half][:], in1=xb[:, half * 512:(half + 1) * 512], op=ALU.add),
                deps=[tdone[half], tl] + (relX2S.get(par) or []))
            relWO[(par, half)] = tx2
        x_rel[par] = tx2
        tso = P.dma("sy", lambda e, tc2=tc2, par=par: e.dma_start(
            out=x2_dram[tc2 * 128:(tc2 + 1) * 128, :], in_=x2_s[:, par * D:(par + 1) * D]),
            deps=[tx2])
        x2_toks.append(tso)
        x2t = x2_s[:, par * D:(par + 1) * D]
        xc2 = xn2buf[:, par * D:(par + 1) * D]
        st = stat2[:, par * 4:par * 4 + 4]
        P.op("dv", lambda e, x2t=x2t, st=st: e.reduce_sum(out=st[:, 0:1], in_=x2t, axis=AX.X),
             deps=[tx2])
        P.op("dv", lambda e, st=st: e.tensor_scalar(
            out=st[:, 1:2], in0=st[:, 0:1], scalar1=1.0 / D, scalar2=None, op0=ALU.mult))
        txc = P.op("dv", lambda e, x2t=x2t, xc2=xc2, st=st: e.tensor_scalar(
            out=xc2, in0=x2t, scalar1=st[:, 1:2], scalar2=None, op0=ALU.subtract))
        relX2S[par] = [txc, tso]
        P.op("ac", lambda e, xc2=xc2, st=st: e.activation(
            out=sqscr[:], in_=xc2, func=AF.Square, accum_out=st[:, 2:3]), deps=[txc])
        P.op("ac", lambda e, st=st: e.activation(
            out=st[:, 3:4], in_=st[:, 2:3], func=AF.Ln, scale=1.0 / D,
            bias=epscol[:, 0:1]), deps=[teps])
        trho = P.op("ac", lambda e, st=st: e.activation(
            out=st[:, 3:4], in_=st[:, 3:4], func=AF.Exp, scale=-0.5))
        P.op("dv", lambda e, xc2=xc2, st=st: e.tensor_scalar(
            out=xc2, in0=xc2, scalar1=st[:, 3:4], scalar2=None, op0=ALU.mult), deps=[trho])
        P.op("dv", lambda e, xc2=xc2: e.tensor_tensor(
            out=xc2, in0=xc2, in1=cb1B[:], op=ALU.mult), deps=[tiles_done])
        h2t = h2_stg[:, par * D:(par + 1) * D]
        th2 = P.op("dv", lambda e, xc2=xc2, h2t=h2t: e.tensor_tensor(
            out=h2t, in0=xc2, in1=sbB[:], op=ALU.add), deps=[relH2.get(par)])
        tpc = 16 // NCHUNK
        cchunk = tc2 // tpc
        ts1 = P.dma_on("sy", lambda e, tc2=tc2, h2t=h2t, cchunk=cchunk, tpc=tpc:
                       e.dma_start(
                           out=h2ch[cchunk][(tc2 % tpc) * 128:(tc2 % tpc + 1) * 128, :],
                           in_=h2t),
                       f"h2st{cchunk}", deps=[th2])
        relH2[par] = ts1
        h2_toks.append(ts1)
        if tc2 % tpc == tpc - 1:
            if cchunk == 0:
                RG = [list(range(8))]
                t_ag_h2_l = []
            t_ag_h2_l.append(P.cc(
                lambda e, c=cchunk: e.collective_compute(
                    "AllGather", ALU.bypass, replica_groups=RG,
                    ins=[h2ch[c].bitcast(F32).opt()],
                    outs=[h2_all[c * 8 * CH2:(c + 1) * 8 * CH2, :].bitcast(F32).opt()]),
                deps=h2_toks[-tpc:]))
    if upto < 6:
        blk = es.enter_context(nc.Block())
        P.emit(blk)
        P.tail_wait(blk, ST)
        return nc, es

    ph6.close()
    ph7 = contextlib.ExitStack()
    h2T = ph7.enter_context(nc.sbuf_tensor("h2T", [128, 8 * TQ], BF16))
    lgT_ps = ph7.enter_context(nc.psum_tensor("lgT_ps", [8, 512], F32))
    lg_ps = ph7.enter_context(nc.psum_tensor("lg_ps", [128, 8], BF16))
    b_ps = ph7.enter_context(nc.psum_tensor("b_ps", [128, 8], F32))
    a_ps = [ph7.enter_context(nc.psum_tensor(f"a_ps{j}", [128, 8], F32)) for j in range(2)]
    sg_ps = ph7.enter_context(nc.psum_tensor("sg_ps", [128, 16], F32))
    lgT_s = ph7.enter_context(nc.sbuf_tensor("lgT_s", [8, TQ], BF16))
    lgt = ph7.enter_context(nc.sbuf_tensor("lgt", [128, 16 * 8], F32))
    scr8 = ph7.enter_context(nc.sbuf_tensor("scr8", [128, 4 * 8], F32))
    colscr = ph7.enter_context(nc.sbuf_tensor("colscr", [128, 8], F32))
    arun = ph7.enter_context(nc.sbuf_tensor("arun", [128, 2 * 8], F32))
    cntrow = ph7.enter_context(nc.sbuf_tensor("cntrow", [1, 16], F32))
    segrow = ph7.enter_context(nc.sbuf_tensor("segrow", [1, 16], F32))
    segB = ph7.enter_context(nc.sbuf_tensor("segB", [128, 16], F32))
    ticol = ph7.enter_context(nc.sbuf_tensor("ticol", [128, 32], I32))

    h2T_tok = [[P.dma("sy", lambda e, dcx=dcx, c=c: e.dma_start(
        out=h2T[:, dcx * TQ + c * CH2:dcx * TQ + (c + 1) * CH2],
        in_=h2ch[c][:, dcx * 128:(dcx + 1) * 128], transpose=True),
        deps=h2_toks[c * (16 // NCHUNK):(c + 1) * (16 // NCHUNK)])
        for c in range(NCHUNK)] for dcx in range(8)]
    lgT_evac = []
    for qb in range(4):
        tm = None
        for dc in range(8):
            tm = P.op("pe", lambda e, qb=qb, dc=dc: e.matmul(
                lgT_ps[:], wg_b[:, dc * 8:(dc + 1) * 8],
                h2T[:, dc * TQ + qb * 512:dc * TQ + (qb + 1) * 512],
                start=(dc == 0), stop=(dc == 7)),
                deps=[h2T_tok[dc][qb * 512 // CH2]] + (lgT_evac[-1:] if lgT_evac else []),
                tok=(dc == 7))
        lgT_evac.append(P.op("dv", lambda e, qb=qb: e.tensor_copy(
            out=lgT_s[:, qb * 512:(qb + 1) * 512], in_=lgT_ps[:]), deps=[tm]))
    lg_toks = []
    prev_lg = None
    for tc3 in range(16):
        tt = P.op("pe", lambda e, tc3=tc3: e.transpose(
            lg_ps[:, 0:8], lgT_s[0:8, tc3 * 128:(tc3 + 1) * 128], id_b[0:8, 0:8]),
            deps=[lgT_evac[-1], prev_lg], tok=True)
        prev_lg = P.op("dv", lambda e, tc3=tc3: e.tensor_copy(
            out=lgt[:, tc3 * 8:(tc3 + 1) * 8], in_=lg_ps[:, 0:8]), deps=[tt])
        lg_toks.append(prev_lg)
    if "d_logits" in dbg_out:
        ST.append(P.dma("sy", lambda e: e.dma_start(
            out=dbg_out["d_logits"].rearrange("(t p) e -> p t e", p=128),
            in_=lgt[:].rearrange("p (t e) -> p t e", e=8)), deps=lg_toks))

    cnt_done = []
    tae_prev = {0: None, 1: None}
    for tc3 in range(16):
        lg = lgt[:, tc3 * 8:(tc3 + 1) * 8]
        P.op("dv", lambda e, lg=lg: e.reduce_max(out=colscr[:, 0:1], in_=lg, axis=AX.X),
             deps=[lg_toks[tc3]])
        P.op("dv", lambda e, lg=lg: e.tensor_scalar(
            out=scr8[:, 0:8], in0=lg, scalar1=colscr[:, 0:1], scalar2=None,
            op0=ALU.is_equal))
        P.op("dv", lambda e: e.tensor_tensor_scan(
            out=scr8[:, 8:16], data0=scr8[:, 0:8], data1=scr8[:, 0:8],
            initial=0.0, op0=ALU.add, op1=ALU.bypass))
        P.op("dv", lambda e: e.tensor_scalar(
            out=scr8[:, 8:16], in0=scr8[:, 8:16], scalar1=1.0, scalar2=None,
            op0=ALU.is_equal))
        tm0 = P.op("dv", lambda e, tc3=tc3: e.tensor_tensor(
            out=m0_s[:, tc3 * 8:(tc3 + 1) * 8], in0=scr8[:, 0:8], in1=scr8[:, 8:16],
            op=ALU.mult))
        P.op("dv", lambda e, tc3=tc3: e.tensor_scalar(
            out=scr8[:, 16:24], in0=m0_s[:, tc3 * 8:(tc3 + 1) * 8], scalar1=-1e30,
            scalar2=None, op0=ALU.mult))
        P.op("dv", lambda e, lg=lg: e.tensor_tensor(
            out=scr8[:, 16:24], in0=lg, in1=scr8[:, 16:24], op=ALU.add))
        P.op("dv", lambda e: e.reduce_max(out=colscr[:, 1:2], in_=scr8[:, 16:24], axis=AX.X))
        P.op("dv", lambda e: e.tensor_scalar(
            out=scr8[:, 24:32], in0=scr8[:, 16:24], scalar1=colscr[:, 1:2], scalar2=None,
            op0=ALU.is_equal))
        P.op("dv", lambda e: e.tensor_tensor_scan(
            out=scr8[:, 16:24], data0=scr8[:, 24:32], data1=scr8[:, 24:32],
            initial=0.0, op0=ALU.add, op1=ALU.bypass))
        P.op("dv", lambda e: e.tensor_scalar(
            out=scr8[:, 16:24], in0=scr8[:, 16:24], scalar1=1.0, scalar2=None,
            op0=ALU.is_equal))
        tm1 = P.op("dv", lambda e, tc3=tc3: e.tensor_tensor(
            out=m1_s[:, tc3 * 8:(tc3 + 1) * 8], in0=scr8[:, 24:32], in1=scr8[:, 16:24],
            op=ALU.mult))
        td01 = P.op("dv", lambda e: e.tensor_tensor(
            out=colscr[:, 2:3], in0=colscr[:, 1:2], in1=colscr[:, 0:1], op=ALU.subtract))
        P.op("dv", lambda e, tc3=tc3: e.tensor_tensor(
            out=scr8[:, 0:8], in0=m0_s[:, tc3 * 8:(tc3 + 1) * 8], in1=iotaE[:], op=ALU.mult))
        P.op("dv", lambda e, tc3=tc3: e.reduce_sum(
            out=rt_cols[:, tc3 * 8 + 0:tc3 * 8 + 1], in_=scr8[:, 0:8], axis=AX.X))
        P.op("dv", lambda e, tc3=tc3: e.tensor_tensor(
            out=scr8[:, 0:8], in0=m1_s[:, tc3 * 8:(tc3 + 1) * 8], in1=iotaE[:], op=ALU.mult))
        P.op("dv", lambda e, tc3=tc3: e.reduce_sum(
            out=rt_cols[:, tc3 * 8 + 1:tc3 * 8 + 2], in_=scr8[:, 0:8], axis=AX.X))
        tx = P.op("ac", lambda e: e.activation(out=colscr[:, 3:4], in_=colscr[:, 2:3],
                                               func=AF.Exp), deps=[td01])
        P.op("dv", lambda e: e.tensor_scalar_add(out=colscr[:, 3:4], in0=colscr[:, 3:4],
                                                 scalar1=1.0), deps=[tx])
        P.op("dv", lambda e, tc3=tc3: e.reciprocal(
            out=rt_cols[:, tc3 * 8 + 2:tc3 * 8 + 3], in_=colscr[:, 3:4]))
        P.op("dv", lambda e, tc3=tc3: e.tensor_scalar(
            out=rt_cols[:, tc3 * 8 + 3:tc3 * 8 + 4],
            in0=rt_cols[:, tc3 * 8 + 2:tc3 * 8 + 3], scalar1=-1.0, scalar2=1.0,
            op0=ALU.mult, op1=ALU.add))
        for (kk, msrc, aps) in ((0, m0_s, a_ps[0]), (1, m1_s, a_ps[1])):
            tmm = P.op("pe", lambda e, tc3=tc3, msrc=msrc: e.matmul(
                b_ps[:], ut_b[:], msrc[:, tc3 * 8:(tc3 + 1) * 8],
                start=True, stop=True, skip_group_check=True),
                deps=[tm0, tm1] + cnt_done[-1:], tok=True)
            if tc3 == 0:
                P.op("dv", lambda e, kk=kk: e.memset(arun[:, kk * 8:(kk + 1) * 8], 0.0))
            tp = P.op("dv", lambda e, kk=kk: e.tensor_tensor(
                out=scr8[:, 0:8], in0=b_ps[:], in1=arun[:, kk * 8:(kk + 1) * 8],
                op=ALU.add), deps=[tmm])
            P.op("dv", lambda e, tc3=tc3, kk=kk, msrc=msrc: e.tensor_tensor(
                out=scr8[:, 8:16], in0=scr8[:, 0:8], in1=msrc[:, tc3 * 8:(tc3 + 1) * 8],
                op=ALU.mult))
            P.op("dv", lambda e, tc3=tc3, kk=kk: e.reduce_sum(
                out=rt_cols[:, tc3 * 8 + 4 + kk:tc3 * 8 + 5 + kk], in_=scr8[:, 8:16],
                axis=AX.X))
            tam = P.op("pe", lambda e, tc3=tc3, msrc=msrc, aps=aps: e.matmul(
                aps[:], all1_b[:], msrc[:, tc3 * 8:(tc3 + 1) * 8],
                start=(tc3 == 0), stop=(tc3 == 15), skip_group_check=True),
                deps=[tp, tae_prev[kk]], tok=True)
            tae = P.op("dv", lambda e, kk=kk, aps=aps: e.tensor_copy(
                out=arun[:, kk * 8:(kk + 1) * 8], in_=aps[:]), deps=[tam])
            tae_prev[kk] = tae
            cnt_done.append(tae)
    tcnt = P.op("dv", lambda e: e.tensor_copy(out=cntrow[0:1, 0:8], in_=a_ps[0][0:1, :]),
                deps=cnt_done)
    tcnt = P.op("dv", lambda e: e.tensor_copy(out=cntrow[0:1, 8:16], in_=a_ps[1][0:1, :]),
                deps=[tcnt])
    tcnt_st = P.dma_on("sy", lambda e: e.dma_start(out=cnt_loc, in_=cntrow[0:1, :]),
                       "cntst", deps=[tcnt])
    t_ag_cnt = P.cc(lambda e: e.collective_compute(
        "AllGather", ALU.bypass, replica_groups=RG,
        ins=[cnt_loc.opt()], outs=[cnt_all_d.opt()]),
        deps=[tcnt_st])
    tsg2 = P.op("dv", lambda e: e.tensor_tensor_scan(
        out=segrow[0:1, :], data0=cntrow[0:1, :], data1=cntrow[0:1, :],
        initial=0.0, op0=ALU.add, op1=ALU.bypass), deps=[tcnt])
    tsg2 = P.op("dv", lambda e: e.tensor_tensor(
        out=segrow[0:1, :], in0=segrow[0:1, :], in1=cntrow[0:1, :], op=ALU.subtract))
    tbm2 = P.op("pe", lambda e: e.matmul(sg_ps[:], ones_f[:], segrow[0:1, :],
                                         start=True, stop=True, skip_group_check=True),
                deps=[tsg2], tok=True)
    tsgB = P.op("dv", lambda e: e.tensor_copy(out=segB[:], in_=sg_ps[:]), deps=[tbm2])

    bc_reg = {}
    ls_toks = []
    for tc3 in range(16):
        for kk in range(2):
            msrc = m0_s if kk == 0 else m1_s
            P.op("dv", lambda e, tc3=tc3, kk=kk, msrc=msrc: e.tensor_tensor(
                out=scr8[:, 0:8], in0=msrc[:, tc3 * 8:(tc3 + 1) * 8],
                in1=segB[:, kk * 8:(kk + 1) * 8], op=ALU.mult), deps=[tsgB])
            P.op("dv", lambda e: e.reduce_sum(out=colscr[:, 4:5], in_=scr8[:, 0:8],
                                              axis=AX.X))
            P.op("dv", lambda e, tc3=tc3, kk=kk: e.tensor_tensor(
                out=colscr[:, 5:6], in0=colscr[:, 4:5],
                in1=rt_cols[:, tc3 * 8 + 4 + kk:tc3 * 8 + 5 + kk], op=ALU.add))
            ci = tc3 * 2 + kk
            t4 = P.op("dv", lambda e, ci=ci: e.tensor_copy(out=ticol[:, ci:ci + 1],
                                                           in_=colscr[:, 5:6]))

            def do(e, tc3=tc3, ci=ci):
                if "bc" not in bc_reg:
                    bc_reg["bc"] = e.to_reg(CAP)
                return e.indirect_dma_start(
                    out=lsort_loc, out_offset=bass.IndirectOffsetOnAxis(ticol[:, ci:ci + 1], 0),
                    in_=gtok_f[:, tc3:tc3 + 1], in_offset=None,
                    bounds_check=bc_reg["bc"], oob_is_err=False)
            ls_toks.append(P.dma("gp", do, deps=[t4]))

    ph7.close()
    ph6s.close()
    ph4a.close()
    phA.close()
    barA = P.barrier()
    t_ag_ls = P.cc(lambda e: e.collective_compute(
        "AllGather", ALU.bypass, replica_groups=RG,
        ins=[lsort_loc.rearrange("(a b) o -> a (o b)", b=128).opt()],
        outs=[lsort_all.rearrange("(a b) o -> a (o b)", b=128).opt()]),
        deps=ls_toks)

    P.base_deps = barA
    phB = contextlib.ExitStack()
    sbB_ = lambda n, s, dt=F32: phB.enter_context(nc.sbuf_tensor(n, s, dt))
    cTB = sbB_("cTB", [16, 8])
    c2B = sbB_("c2B", [8, 16])
    s16B = sbB_("s16B", [16, 16])
    exrB = sbB_("exrB", [16, 2])
    exsB = sbB_("exsB", [8, 2 * 16])
    iodB = sbB_("iodB", [8, 1])
    io128 = sbB_("io128", [128, 1])
    b1c = sbB_("b1c", [128, 32])
    b2Bf = sbB_("b2Bf", [128, D], BF16)
    ones_bB = sbB_("ones_bB", [1, 128], BF16)
    hstg = [sbB_(f"hstg{p}", [128, 256], BF16) for p in range(2)]
    ystg = sbB_("ystg", [128, 512], BF16)
    w1_s = sbB_("w1_s", [128, 32 * D], BF16)
    w2_s = sbB_("w2_s", [128, 32 * D], BF16)
    dT = sbB_("dT", [128, 8 * CAP], BF16)
    phB2 = contextlib.ExitStack()
    sb2 = lambda n, s, dt=F32: phB2.enter_context(nc.sbuf_tensor(n, s, dt))
    globB = sb2("globB", [16, 8])
    lbase = sb2("lbase", [8, 16])
    srcb = sb2("srcb", [8, 2])
    sbrow = sb2("sbrow", [1, 18])
    srcrow = sb2("srcrow", [1, 16])
    sbB16 = sb2("sbB16", [128, 16])
    diffB = sb2("diffB", [128, 16])
    totB = sb2("totB", [128, 1])
    scrB = sb2("scrB", [128, 2 * 16])
    colsB = sb2("colsB", [128, 8])
    ticB = sb2("ticB", [128, 32], I32)
    invc = sb2("invc", [128, 32])
    dispf32 = sb2("dispf32", [128, 128])
    dispstg = [sb2(f"dispstg{p}", [128, D], BF16) for p in range(2)]

    phBp = contextlib.ExitStack()
    tB_ps = phBp.enter_context(nc.psum_tensor("tB_ps", [16, 8], F32))
    t1B_ps = phBp.enter_context(nc.psum_tensor("t1B_ps", [128, 16], F32))
    rB_ps = phBp.enter_context(nc.psum_tensor("rB_ps", [1, 8], F32))
    bbB_ps = phBp.enter_context(nc.psum_tensor("bbB_ps", [128, 512], F32))

    tloadB = [P.dma("sy", lambda e, a=a, b_=b_: e.dma_start(out=a, in_=b_))
              for a, b_ in [(s16B[:], s16), (exrB[:, 0:1], exr0),
                            (exrB[:, 1:2], exr1), (exsB[:, 0:16], exs0),
                            (exsB[:, 16:32], exs1), (iodB[:], iotad),
                            (io128[:], iota128), (ones_bB[:], onesbf)]]
    tb1c = P.dma("sy", lambda e: e.dma_start(out=b1c[:], in_=b1m))
    tb2l = P.dma("gp", lambda e: e.dma_start(out=b2Bf[0:1, :], in_=b2r))
    tloadB += [tb1c, tb2l]
    tc2l = P.dma("sy", lambda e: e.dma_start(out=c2B[:], in_=cnt_all_d),
                 deps=[t_ag_cnt])
    ttr = P.op("pe", lambda e: e.transpose(tB_ps[:], c2B[0:8, 0:16], id_f[0:8, 0:8]),
               deps=[tc2l], tok=True)
    tctl = P.op("dv", lambda e: e.tensor_copy(out=cTB[:], in_=tB_ps[:]), deps=[ttr])
    tloadB += [tc2l, tctl]
    w1tok = [P.dma("gp", lambda e, hc=hc: e.dma_start(
        out=w1_s[:, hc * D:(hc + 1) * D], in_=w1t[hc * 128:(hc + 1) * 128, :]))
        for hc in range(32)]
    w2tok = [P.dma("gp", lambda e, hc=hc: e.dma_start(
        out=w2_s[:, hc * D:(hc + 1) * D], in_=w2m[hc * 128:(hc + 1) * 128, :]))
        for hc in range(32)]

    P.op("dv", lambda e: e.tensor_tensor_scan(
        out=globB[:], data0=cTB[:], data1=cTB[:], initial=0.0, op0=ALU.add,
        op1=ALU.bypass), deps=tloadB)
    P.op("dv", lambda e: e.tensor_tensor(out=globB[:], in0=globB[:], in1=cTB[:],
                                         op=ALU.subtract))
    t = P.op("dv", lambda e: e.tensor_tensor(out=scrB[0:16, 0:1], in0=globB[:, 7:8],
                                             in1=cTB[:, 7:8], op=ALU.add))
    tm = P.op("pe", lambda e: e.matmul(tB_ps[:, 0:1], s16B[:], scrB[0:16, 0:1],
                                       start=True, stop=True), deps=[t, tctl], tok=True)
    tg = P.op("dv", lambda e: e.tensor_scalar(
        out=globB[:], in0=globB[:], scalar1=tB_ps[:, 0:1], scalar2=None, op0=ALU.add),
        deps=[tm])
    tm = P.op("pe", lambda e: e.matmul(tB_ps[0:1, :], exrB[:, 0:1], globB[:],
                                       start=True, stop=True), deps=[tg], tok=True)
    t = P.op("dv", lambda e: e.tensor_copy(out=sbrow[0:1, 0:8], in_=tB_ps[0:1, :]),
             deps=[tm])
    tm = P.op("pe", lambda e: e.matmul(tB_ps[0:1, :], exrB[:, 1:2], globB[:],
                                       start=True, stop=True), deps=[t], tok=True)
    t = P.op("dv", lambda e: e.tensor_copy(out=sbrow[0:1, 8:16], in_=tB_ps[0:1, :]),
             deps=[tm])
    t2 = P.op("dv", lambda e: e.tensor_tensor(out=scrB[0:16, 1:2], in0=globB[:, 7:8],
                                              in1=cTB[:, 7:8], op=ALU.add))
    tm = P.op("pe", lambda e: e.matmul(tB_ps[0:1, 0:1], exrB[:, 1:2], scrB[0:16, 1:2],
                                       start=True, stop=True), deps=[t2], tok=True)
    ttot = P.op("dv", lambda e: e.tensor_copy(out=sbrow[0:1, 16:17], in_=tB_ps[0:1, 0:1]),
                deps=[tm])
    P.op("dv", lambda e: e.tensor_tensor_scan(
        out=lbase[:], data0=c2B[:], data1=c2B[:], initial=0.0, op0=ALU.add,
        op1=ALU.bypass))
    P.op("dv", lambda e: e.tensor_tensor(out=lbase[:], in0=lbase[:], in1=c2B[:],
                                         op=ALU.subtract))
    tsb = None
    for kk in range(2):
        P.op("dv", lambda e, kk=kk: e.tensor_tensor(
            out=scrB[0:8, 0:16], in0=lbase[:], in1=exsB[:, kk * 16:(kk + 1) * 16],
            op=ALU.mult))
        P.op("dv", lambda e, kk=kk: e.reduce_sum(out=srcb[:, kk:kk + 1],
                                                 in_=scrB[0:8, 0:16], axis=AX.X))
        tsb = P.op("dv", lambda e, kk=kk: e.tensor_tensor(
            out=srcb[:, kk:kk + 1], in0=srcb[:, kk:kk + 1], in1=iodB[:], op=ALU.add))
    prev = None
    for kk in range(2):
        tt = P.op("pe", lambda e, kk=kk: e.transpose(
            rB_ps[0:1, :], srcb[:, kk:kk + 1], id_f[0:8, 0:8]),
            deps=[tsb, prev], tok=True)
        prev = P.op("dv", lambda e, kk=kk: e.tensor_copy(
            out=srcrow[0:1, kk * 8:(kk + 1) * 8], in_=rB_ps[0:1, :]), deps=[tt])
    tm = P.op("pe", lambda e: e.matmul(t1B_ps[:], ones_f[:], sbrow[0:1, 0:16],
                                       start=True, stop=True), deps=[ttot, prev], tok=True)
    tSB = P.op("dv", lambda e: e.tensor_copy(out=sbB16[:], in_=t1B_ps[:]), deps=[tm])
    tm = P.op("pe", lambda e: e.matmul(t1B_ps[:], ones_f[:], srcrow[0:1, :],
                                       start=True, stop=True), deps=[tSB], tok=True)
    P.op("dv", lambda e: e.tensor_copy(out=diffB[:], in_=t1B_ps[:]), deps=[tm])
    tDF = P.op("dv", lambda e: e.tensor_tensor(out=diffB[:], in0=diffB[:], in1=sbB16[:],
                                               op=ALU.subtract))
    tm = P.op("pe", lambda e: e.matmul(t1B_ps[:, 0:1], ones_f[:], sbrow[0:1, 16:17],
                                       start=True, stop=True), deps=[tDF], tok=True)
    tTB = P.op("dv", lambda e: e.tensor_copy(out=totB[:], in_=t1B_ps[:, 0:1]), deps=[tm])
    b2b_toks = []
    tprev2 = tb1c
    for half in range(2):
        tm = P.op("pe", lambda e, half=half: e.matmul(
            bbB_ps[:], ones_bB[:], b2Bf[0:1, half * 512:(half + 1) * 512],
            start=True, stop=True, skip_group_check=True),
            deps=[tprev2, tb2l], tok=True)
        tprev2 = P.op("dv", lambda e, half=half: e.tensor_copy(
            out=b2Bf[:, half * 512:(half + 1) * 512], in_=bbB_ps[:]), deps=[tm])
        b2b_toks.append(tprev2)
    tinv0 = P.op("gp", lambda e: e.memset(invc[:], 0.0))
    phBp.close()
    phB4 = contextlib.ExitStack()
    h_ps = [phB4.enter_context(nc.psum_tensor(f"h_ps{p}", [128, 256], F32))
            for p in range(2)]
    y_ps = [phB4.enter_context(nc.psum_tensor(f"y_ps{j}", [128, 1024], F32))
            for j in range(2)]
    tpT_ps = phB4.enter_context(nc.psum_tensor("tpT_ps2", [128, 256], F32))

    bcB = {}
    relD = {}
    dTev = {}
    tpT_rel = {}
    dTev_prev = {}

    def emit_b2_tile(t_):
        P.op("dv", lambda e, t_=t_: e.tensor_scalar(
            out=colsB[:, 0:1], in0=io128[:], scalar1=float(t_ * 128), scalar2=None,
            op0=ALU.add), deps=[tSB, tDF, tTB])
        P.op("dv", lambda e: e.tensor_scalar(
            out=scrB[:, 0:16], in0=sbB16[:], scalar1=colsB[:, 0:1], scalar2=None,
            op0=ALU.is_le))
        P.op("dv", lambda e: e.tensor_tensor(
            out=scrB[:, 16:31], in0=scrB[:, 0:15], in1=scrB[:, 1:16], op=ALU.subtract))
        P.op("dv", lambda e: e.tensor_copy(out=scrB[:, 31:32], in_=scrB[:, 15:16]))
        P.op("dv", lambda e: e.tensor_tensor(
            out=scrB[:, 16:32], in0=scrB[:, 16:32], in1=diffB[:], op=ALU.mult))
        P.op("dv", lambda e: e.reduce_sum(out=colsB[:, 1:2], in_=scrB[:, 16:32], axis=AX.X))
        P.op("dv", lambda e: e.tensor_tensor(
            out=colsB[:, 1:2], in0=colsB[:, 1:2], in1=colsB[:, 0:1], op=ALU.add))
        P.op("dv", lambda e: e.tensor_scalar(
            out=colsB[:, 2:3], in0=colsB[:, 0:1], scalar1=totB[:, 0:1], scalar2=None,
            op0=ALU.is_lt))
        P.op("dv", lambda e: e.tensor_scalar(
            out=colsB[:, 3:4], in0=colsB[:, 2:3], scalar1=-1e9, scalar2=1e9,
            op0=ALU.mult, op1=ALU.add))
        P.op("dv", lambda e: e.tensor_tensor(
            out=colsB[:, 1:2], in0=colsB[:, 1:2], in1=colsB[:, 3:4], op=ALU.add))
        tic = P.op("dv", lambda e, t_=t_: e.tensor_copy(out=ticB[:, t_:t_ + 1],
                                                        in_=colsB[:, 1:2]))

        def g1(e, t_=t_):
            if "r1" not in bcB:
                bcB["r1"] = e.to_reg(8 * CAP)
            return e.indirect_dma_start(
                out=invc[:, t_:t_ + 1], out_offset=None, in_=lsort_all,
                in_offset=bass.IndirectOffsetOnAxis(ticB[:, t_:t_ + 1], 0),
                bounds_check=bcB["r1"], oob_is_err=False)
        tg1 = P.dma("gp", g1, deps=[tic, tinv0, t_ag_ls])
        P.op("dv", lambda e, t_=t_: e.tensor_scalar(
            out=colsB[:, 4:5], in0=invc[:, t_:t_ + 1], scalar1=-1.0, scalar2=None,
            op0=ALU.add), deps=[tg1])
        P.op("dv", lambda e, t_=t_: e.tensor_scalar(
            out=colsB[:, 5:6], in0=invc[:, t_:t_ + 1], scalar1=0.5, scalar2=None,
            op0=ALU.is_ge))
        P.op("dv", lambda e: e.tensor_scalar(
            out=colsB[:, 6:7], in0=colsB[:, 5:6], scalar1=-1e9, scalar2=1e9,
            op0=ALU.mult, op1=ALU.add))
        P.op("dv", lambda e: e.tensor_tensor(
            out=colsB[:, 4:5], in0=colsB[:, 4:5], in1=colsB[:, 6:7], op=ALU.add))
        tic2 = P.op("dv", lambda e, t_=t_: e.tensor_copy(out=ticB[:, t_:t_ + 1],
                                                         in_=colsB[:, 4:5]))
        dpar = t_ % 2
        dstB = dispstg[dpar][:, 0:D]
        tms = P.op("gp", lambda e, dstB=dstB: e.memset(dstB, 0.0), deps=[relD.get(dpar)])

        def g2(e, t_=t_, dstB=dstB):
            if "r2" not in bcB:
                bcB["r2"] = e.to_reg(8 * TQ)
            return e.indirect_dma_start(
                out=dstB, out_offset=None, in_=h2_all,
                in_offset=bass.IndirectOffsetOnAxis(ticB[:, t_:t_ + 1], 0),
                bounds_check=bcB["r2"], oob_is_err=False)
        tg2 = P.dma("gp", g2, deps=[tic2, tms] + t_ag_h2_l)
        tcp = None
        for dc in range(8):
            hp2 = dc % 2
            tcp = P.op("dv", lambda e, dc=dc, dstB=dstB: e.tensor_copy(
                out=dispf32[:],
                in_=dstB[:, dc * 128:(dc + 1) * 128]),
                deps=[tg2, tpT_rel.get("f")])
            ttp = P.op("pe", lambda e, hp2=hp2: e.transpose(
                tpT_ps[:, hp2 * 128:(hp2 + 1) * 128],
                dispf32[:], id_f[:]),
                deps=[tcp, dTev_prev.get(hp2)], tok=True)
            tev = P.op("dv", lambda e, t_=t_, dc=dc, hp2=hp2: e.tensor_copy(
                out=dT[:, dc * CAP + t_ * 128:dc * CAP + (t_ + 1) * 128],
                in_=tpT_ps[:, hp2 * 128:(hp2 + 1) * 128]), deps=[ttp])
            tpT_rel["f"] = ttp
            dTev_prev[hp2] = tev
            dTev[(t_, dc)] = tev
        relD[dpar] = tcp

    SGS = [(i * 256, 256) for i in range(CAP // 256)]
    yst_prev = {}
    yB_toks = []
    y_chunk_deps = {c: [] for c in range(NCHUNK)}
    gel_prev = {}
    w2_last = {}

    def emit_b4_group(sgi):
        s0, n = SGS[sgi]
        nj = (n + 127) // 128
        tmm2 = None
        for hc in range(32):
            hp = hc % 2
            tm1 = None
            tmax = (s0 + n - 1) // 128
            for dc in range(8):
                tm1 = P.op("pe", lambda e, hc=hc, dc=dc, s0=s0, n=n, hp=hp: e.matmul(
                    h_ps[hp][:, 0:n], w1_s[:, hc * D + dc * 128:hc * D + (dc + 1) * 128],
                    dT[:, dc * CAP + s0:dc * CAP + s0 + n],
                    start=(dc == 0), stop=(dc == 7), skip_group_check=True),
                    deps=[dTev[(tmax, dc)], w1tok[hc], gel_prev.get(hp)],
                    tok=(dc == 7))
            tgel = P.op("ac", lambda e, hc=hc, n=n, hp=hp: e.activation(
                out=hstg[hp][:, 0:n], in_=h_ps[hp][:, 0:n], func=AF.Gelu,
                bias=b1c[:, hc:hc + 1]), deps=[tm1, tb1c, w2_last.get(hp)])
            gel_prev[hp] = tgel
            for j in range(nj):
                w = min(128, n - j * 128)
                for half in range(2):
                    tmm2 = P.op("pe", lambda e, hc=hc, j=j, w=w, half=half, hp=hp: e.matmul(
                        y_ps[j][:, half * 512:(half + 1) * 512],
                        hstg[hp][:, j * 128:j * 128 + w],
                        w2_s[:, hc * D + half * 512:hc * D + (half + 1) * 512],
                        start=(hc == 0), stop=(hc == 31), skip_group_check=True),
                        deps=[tgel, w2tok[hc]] + ([yst_prev.get(j)] if hc == 0 else []),
                        tok=(j == nj - 1 and half == 1))
            w2_last[hp] = tmm2
        tsl = None
        for j in range(nj):
            w = min(128, n - j * 128)
            r0, r1 = s0 + j * 128, s0 + j * 128 + w
            for half in range(2):
                tev = P.op("dv", lambda e, j=j, w=w, half=half: e.tensor_tensor(
                    out=ystg[0:w, :], in0=y_ps[j][0:w, half * 512:(half + 1) * 512],
                    in1=b2Bf[0:w, half * 512:(half + 1) * 512], op=ALU.add),
                    deps=[tmm2, tsl] + b2b_toks)
                ts = P.dma_on("sy", lambda e, r0=r0, w=w, half=half: e.dma_start(
                    out=ylc[r0 // YCH][r0 % YCH:r0 % YCH + w,
                                       half * 512:(half + 1) * 512],
                    in_=ystg[0:w, :]), f"ystc{r0 // YCH}", deps=[tev])
                tsl = ts
                yB_toks.append(ts)
                for cch in range(r0 // YCH, (r1 - 1) // YCH + 1):
                    y_chunk_deps[cch].append(ts)
            yst_prev[j] = tsl

    for t_ in range(32):
        emit_b2_tile(t_)
    for sgi in range(len(SGS)):
        emit_b4_group(sgi)
    phB2.close()

    t_ag_y_l = [P.cc(lambda e, c=c: e.collective_compute(
        "AllGather", ALU.bypass, replica_groups=RG,
        ins=[ylc[c].bitcast(F32).opt()],
        outs=[y_all[c * 8 * YCH:(c + 1) * 8 * YCH, :].bitcast(F32).opt()]),
        deps=y_chunk_deps[c]) for c in range(NCHUNK)]

    P.base_deps = P.barrier()
    phB4.close()
    phB.close()
    phC = contextlib.ExitStack()
    sbC = lambda n, s, dt=F32: phC.enter_context(nc.sbuf_tensor(n, s, dt))
    ctC = sbC("ctC", [16, 8])
    c2C = sbC("c2C", [8, 16])
    rkoC = sbC("rkoC", [16, 8])
    globC = sbC("globC", [16, 8])
    offcC = sbC("offcC", [16, 1])
    offrowC = sbC("offrowC", [1, 16])
    offBC = sbC("offBC", [128, 16])
    m0C = sbC("m0C", [128, 8])
    colsC = sbC("colsC", [128, 8])
    ticC = sbC("ticC", [128, 32], I32)
    wghC = sbC("wghC", [128, 32])
    y0sl = [sbC(f"y0s{p}", [128, D], BF16) for p in range(4)]
    y1sl = [sbC(f"y1s{p}", [128, D], BF16) for p in range(4)]
    accC = sbC("accC", [128, D])
    x2stg = sbC("x2stg", [128, 4 * D])
    ostg = sbC("ostg", [128, 4 * D])
    s16C = sbC("s16C", [16, 16])

    phCp = contextlib.ExitStack()
    tC_ps = phCp.enter_context(nc.psum_tensor("tC_ps", [16, 8], F32))
    rC_ps = phCp.enter_context(nc.psum_tensor("rC_ps", [1, 16], F32))
    bBC_ps = phCp.enter_context(nc.psum_tensor("bBC_ps", [128, 512], F32))

    tloadC = [P.dma("sy", lambda e, a=a, b_=b_: e.dma_start(out=a, in_=b_))
              for a, b_ in [(s16C[:], s16), (rkoC[:], rkoh)]]
    tc2lC = P.dma("sy", lambda e: e.dma_start(out=c2C[:], in_=cnt_all_d),
                  deps=[t_ag_cnt])
    ttrC = P.op("pe", lambda e: e.transpose(tC_ps[:], c2C[0:8, 0:16], id_f[0:8, 0:8]),
                deps=[tc2lC], tok=True)
    tctC = P.op("dv", lambda e: e.tensor_copy(out=ctC[:], in_=tC_ps[:]), deps=[ttrC])
    tloadC += [tc2lC, tctC]
    P.op("dv", lambda e: e.tensor_tensor_scan(
        out=globC[:], data0=ctC[:], data1=ctC[:], initial=0.0, op0=ALU.add,
        op1=ALU.bypass), deps=tloadC)
    P.op("dv", lambda e: e.tensor_tensor(out=globC[:], in0=globC[:], in1=ctC[:],
                                         op=ALU.subtract))
    t = P.op("dv", lambda e: e.tensor_tensor(out=offcC[:], in0=globC[:, 7:8],
                                             in1=ctC[:, 7:8], op=ALU.add))
    tm = P.op("pe", lambda e: e.matmul(tC_ps[:, 0:1], s16C[:], offcC[:],
                                       start=True, stop=True), deps=[t, tctC], tok=True)
    tg = P.op("dv", lambda e: e.tensor_scalar(
        out=globC[:], in0=globC[:], scalar1=tC_ps[:, 0:1], scalar2=None, op0=ALU.add),
        deps=[tm])
    P.op("dv", lambda e: e.tensor_tensor(out=ctC[:], in0=globC[:], in1=rkoC[:],
                                         op=ALU.mult), deps=[tg])
    toff = P.op("dv", lambda e: e.reduce_sum(out=offcC[:], in_=ctC[:], axis=AX.X))
    tt = P.op("pe", lambda e: e.transpose(rC_ps[0:1, :], offcC[:], id_f[0:16, 0:16]),
              deps=[toff], tok=True)
    t = P.op("dv", lambda e: e.tensor_copy(out=offrowC[0:1, :], in_=rC_ps[0:1, :]),
             deps=[tt])
    tm = P.op("pe", lambda e: e.matmul(bBC_ps[:, 0:16], ones_f[:], offrowC[0:1, :],
                                       start=True, stop=True, skip_group_check=True),
              deps=[t], tok=True)
    tOF = P.op("dv", lambda e: e.tensor_copy(out=offBC[:], in_=bBC_ps[:, 0:16]), deps=[tm])
    tms_y = [P.op("gp", lambda e, t=t: e.memset(t[:], 0.0))
             for t in y0sl + y1sl]

    tic_l = {}
    for tc in range(16):
        rr = rt_cols[:, tc * 8:(tc + 1) * 8]
        for kk in range(2):
            ecol = rr[:, kk:kk + 1]
            vcol = rr[:, 2 + kk:3 + kk]
            pcol = rr[:, 4 + kk:5 + kk]
            P.op("dv", lambda e, ecol=ecol: e.tensor_scalar(
                out=m0C[:], in0=iotaE[:], scalar1=ecol, scalar2=None, op0=ALU.is_equal),
                deps=[tOF])
            P.op("dv", lambda e, kk=kk: e.tensor_tensor(
                out=m0C[:], in0=m0C[:], in1=offBC[:, kk * 8:(kk + 1) * 8], op=ALU.mult))
            P.op("dv", lambda e: e.reduce_sum(out=colsC[:, 0:1], in_=m0C[:], axis=AX.X))
            P.op("dv", lambda e, pcol=pcol: e.tensor_tensor(
                out=colsC[:, 0:1], in0=colsC[:, 0:1], in1=pcol, op=ALU.add))
            P.op("dv", lambda e: e.tensor_scalar(
                out=colsC[:, 1:2], in0=colsC[:, 0:1], scalar1=float(CAP), scalar2=None,
                op0=ALU.is_lt))
            P.op("dv", lambda e, vcol=vcol, tc=tc, kk=kk: e.tensor_tensor(
                out=wghC[:, tc * 2 + kk:tc * 2 + kk + 1], in0=vcol, in1=colsC[:, 1:2],
                op=ALU.mult))
            for ich in range(1, NCHUNK):
                P.op("dv", lambda e, ich=ich: e.tensor_scalar(
                    out=colsC[:, 7:8], in0=colsC[:, 0:1], scalar1=float(ich * YCH),
                    scalar2=None, op0=ALU.is_ge))
                if ich == 1:
                    P.op("dv", lambda e: e.tensor_copy(out=colsC[:, 6:7],
                                                       in_=colsC[:, 7:8]))
                else:
                    P.op("dv", lambda e: e.tensor_tensor(
                        out=colsC[:, 6:7], in0=colsC[:, 6:7], in1=colsC[:, 7:8],
                        op=ALU.add))
            P.op("dv", lambda e, ecol=ecol: e.tensor_scalar(
                out=colsC[:, 2:3], in0=ecol, scalar1=float(YCH), scalar2=None,
                op0=ALU.mult))
            P.op("dv", lambda e: e.tensor_tensor(
                out=colsC[:, 2:3], in0=colsC[:, 2:3], in1=colsC[:, 0:1], op=ALU.add))
            if NCHUNK > 1:
                P.op("dv", lambda e: e.scalar_tensor_tensor(
                    out=colsC[:, 2:3], in0=colsC[:, 6:7], scalar=float(7 * YCH),
                    in1=colsC[:, 2:3], op0=ALU.mult, op1=ALU.add))
            P.op("dv", lambda e: e.tensor_scalar(
                out=colsC[:, 3:4], in0=colsC[:, 1:2], scalar1=-1e9, scalar2=1e9,
                op0=ALU.mult, op1=ALU.add))
            P.op("dv", lambda e: e.tensor_tensor(
                out=colsC[:, 2:3], in0=colsC[:, 2:3], in1=colsC[:, 3:4], op=ALU.add))
            tic_l[(tc, kk)] = P.op("dv", lambda e, tc=tc, kk=kk: e.tensor_copy(
                out=ticC[:, tc * 2 + kk:tc * 2 + kk + 1], in_=colsC[:, 2:3]))

    bcC = {}
    relX, relO, relG = {}, {}, {}
    for tc in range(16):
        par = tc % 4
        tx = P.dma("sy", lambda e, tc=tc, par=par: e.dma_start(
            out=x2stg[:, par * D:(par + 1) * D],
            in_=x2_dram[tc * 128:(tc + 1) * 128, :]), deps=[relX.get(par)])
        tgys = []
        for kk in range(2):
            def gy(e, tc=tc, kk=kk, par=par):
                if "r" not in bcC:
                    bcC["r"] = e.to_reg(8 * CAP)
                return e.indirect_dma_start(
                    out=(y0sl if kk == 0 else y1sl)[par][:], out_offset=None,
                    in_=y_all,
                    in_offset=bass.IndirectOffsetOnAxis(
                        ticC[:, tc * 2 + kk:tc * 2 + kk + 1], 0),
                    bounds_check=bcC["r"], oob_is_err=False)
            tgys.append(P.dma("gp", gy,
                              deps=[tic_l[(tc, kk)], relG.get(par)] + tms_y + t_ag_y_l))
        ta = P.op("dv", lambda e, tc=tc, par=par: e.tensor_scalar(
            out=accC[:], in0=y0sl[par][:], scalar1=wghC[:, tc * 2:tc * 2 + 1],
            scalar2=None, op0=ALU.mult), deps=tgys + [tic_l[(tc, 1)]])
        ta = P.op("dv", lambda e, tc=tc, par=par: e.scalar_tensor_tensor(
            out=accC[:], in0=y1sl[par][:], scalar=wghC[:, tc * 2 + 1:tc * 2 + 2],
            in1=accC[:], op0=ALU.mult, op1=ALU.add))
        relG[par] = ta
        P.op("dv", lambda e: e.tensor_tensor(
            out=accC[:], in0=accC[:], in1=gbB[:], op=ALU.mult))
        to2 = P.op("dv", lambda e, par=par: e.tensor_tensor(
            out=ostg[:, par * D:(par + 1) * D], in0=accC[:],
            in1=x2stg[:, par * D:(par + 1) * D], op=ALU.add),
            deps=[tx, relO.get(par)])
        relX[par] = to2
        ts = P.dma("sy", lambda e, tc=tc, par=par: e.dma_start(
            out=outo[tc * 128:(tc + 1) * 128, :], in_=ostg[:, par * D:(par + 1) * D]),
            deps=[to2])
        relO[par] = ts
        ST.append(ts)

    blk = es.enter_context(nc.Block())
    P.emit(blk)
    P.tail_wait(blk, ST)
    return nc, es


HOSTC = {}


def host_consts():
    if HOSTC:
        return HOSTC
    import ml_dtypes
    bf = ml_dtypes.bfloat16
    HOSTC["idf32"] = np.eye(128, dtype=np.float32)
    HOSTC["idbf"] = np.eye(128, dtype=np.float32).astype(bf)
    HOSTC["onesf32"] = np.ones((1, 128), np.float32)
    HOSTC["onesbf"] = np.ones((1, 128), bf)
    HOSTC["ones4k_bf"] = np.ones((1, TK), bf)
    HOSTC["ut_bf"] = np.triu(np.ones((128, 128), np.float32), 1).astype(bf)
    HOSTC["all1_bf"] = np.ones((128, 128), bf)
    eqk = np.zeros((128, 8 * 16), np.float32)
    for oc in range(8):
        for p in range(128):
            eqk[p, oc * 16 + 2 * oc + p // 64] = 1.0
    HOSTC["eq_k"] = eqk.astype(bf)
    eqr = np.zeros((128, 4 * 16), np.float32)
    for oc in range(4):
        for p in range(128):
            eqr[p, oc * 16 + 4 * oc + p // 32] = 1.0
    HOSTC["eq_r"] = eqr.astype(bf)
    HOSTC["iotaEi"] = np.tile(np.arange(8).astype(np.float32), (128, 1))
    return HOSTC


def host_inputs_a(inputs, core):
    hc = host_consts()
    b, qh = core // 2, core % 2
    x = np.asarray(inputs["L0_attn_x"][b])
    x_kv = np.concatenate([x[qh * TQ:(qh + 1) * TQ], x[(1 - qh) * TQ:(2 - qh) * TQ]], 0)
    c = np.asarray(inputs["L0_c"][b])
    t = np.arange(TQ)
    gtok = ((CH2 * core + t + 7 * CH2 * (t // CH2) + 1).astype(np.float32)
            .reshape(16, 128).T)
    d = {
        "x_kv": np.ascontiguousarray(x_kv),
        "c_colT": np.ascontiguousarray(c.reshape(8, 128).T),
        "wmod": np.asarray(inputs["L0_wmod"]),
        "wq": np.asarray(inputs["L0_attn_Wq"]), "wk": np.asarray(inputs["L0_attn_Wk"]),
        "wv": np.asarray(inputs["L0_attn_Wv"]), "wr": np.asarray(inputs["L0_attn_Wr"]),
        "wo": np.asarray(inputs["L0_attn_Wo"]),
        "bq": np.asarray(inputs["L0_attn_bq"]).reshape(1, -1),
        "bk": np.asarray(inputs["L0_attn_bk"]).reshape(1, -1),
        "bv": np.asarray(inputs["L0_attn_bv"]).reshape(1, -1),
        "br": np.asarray(inputs["L0_attn_br"]).reshape(1, -1),
        "bo": np.asarray(inputs["L0_attn_bo"]).reshape(1, -1),
        "wg_t": np.ascontiguousarray(
            np.asarray(inputs["L0_ffn_wg"]).reshape(8, 128, 8).transpose(1, 0, 2).reshape(128, 64)),
        "gtokp1": np.ascontiguousarray(gtok),
    }
    for k in ("idf32", "idbf", "onesf32", "onesbf", "ones4k_bf", "ut_bf", "all1_bf",
              "eq_k", "eq_r", "iotaEi"):
        d[k] = hc[k]
    return d


def host_inputs(inputs, core):
    d = host_inputs_a(inputs, core)
    w1 = np.asarray(inputs["L0_ffn_w1"][core])
    d["w1t"] = np.ascontiguousarray(
        w1.reshape(8, 128, 32, 128).transpose(2, 1, 0, 3).reshape(32 * 128, 1024))
    d["w2m"] = np.asarray(inputs["L0_ffn_w2"][core])
    d["b1m"] = np.ascontiguousarray(
        np.asarray(inputs["L0_ffn_b1"][core]).reshape(32, 128).T)
    d["b2r"] = np.asarray(inputs["L0_ffn_b2"][core]).reshape(1, -1)
    s16 = np.zeros((16, 16), np.float32)
    s16[np.arange(8), 8 + np.arange(8)] = 1.0
    d["s16"] = s16
    exr0 = np.zeros((16, 1), np.float32); exr0[core, 0] = 1.0
    exr1 = np.zeros((16, 1), np.float32); exr1[8 + core, 0] = 1.0
    exs0 = np.zeros((8, 16), np.float32); exs0[:, core] = 1.0
    exs1 = np.zeros((8, 16), np.float32); exs1[:, 8 + core] = 1.0
    d["exr0"], d["exr1"], d["exs0"], d["exs1"] = exr0, exr1, exs0, exs1
    d["iotad"] = (np.arange(8, dtype=np.float32) * CAP).reshape(8, 1)
    d["iota128"] = np.arange(128, dtype=np.float32).reshape(128, 1)
    rkoh = np.zeros((16, 8), np.float32); rkoh[:, core] = 1.0
    d["rkoh"] = rkoh
    return d


_CACHE = {}


def kernel(**inputs):
    if "F" not in _CACHE:
        _CACHE["F"] = build_fused()
    ncF, _ = _CACHE["F"]
    cores = list(range(8))
    res = run_bass_kernel_spmd(ncF, [host_inputs(inputs, c) for c in cores],
                               core_ids=cores)
    out = np.zeros((4, 4096, 1024), np.float32)
    for c in cores:
        b, qh = c // 2, c % 2
        out[b, qh * 2048:(qh + 1) * 2048] = np.asarray(res.results[c]["outo"])
    return out


# version 17 (final)
# speedup vs baseline: 1.0909x; 1.0856x over previous
import contextlib
import math
import numpy as np
import concourse.bass as bass
import concourse.mybir as mybir
from concourse.bass_utils import run_bass_kernel_spmd

class _Op:
    __slots__ = ("fn", "deps", "inc", "inck", "tokv", "sem", "tag")

    def __init__(self, fn, deps, inc, inck, tokv, sem, tag):
        self.fn = fn
        self.deps = deps
        self.inc = inc
        self.inck = inck
        self.tokv = tokv
        self.sem = sem
        self.tag = tag


class Prog:
    ENG = ("pe", "dv", "ac", "gp", "sy")
    CHAIN = {"dv", "ac", "gp"}
    DMASEM = {"sy": "sy_dma", "gp": "gp_dma"}

    def __init__(self, nc, es):
        self.nc = nc
        self.es = es
        self.ops = {e: [] for e in self.ENG}
        self.last_compute = {e: None for e in self.ENG}
        self.sems = {}
        self.base_deps = []
        self._ccn = 0
        for name in ("pe_s", "dv_s", "ac_s", "gp_s", "sy_dma", "gp_dma"):
            self.sems[name] = [es.enter_context(nc.semaphore(name)), 0]

    def barrier(self):
        return [(n, s[1]) for n, s in self.sems.items()
                if s[1] > 0 and not n.startswith("cc")]

    def cc(self, fn, deps=(), tag=""):
        name = f"cc{self._ccn}"
        self._ccn += 1
        self.sems[name] = [self.es.enter_context(self.nc.semaphore(name)), 1]
        mydeps = [d for d in deps if d is not None] + list(self.base_deps)
        o = _Op(fn, mydeps, True, None, 1, name, tag)
        self.ops["gp"].append(o)
        return (name, 1)

    def op(self, eng, fn, deps=(), tok=False, tag=""):
        sem = eng + "_s"
        inc = bool(tok) or (eng in self.CHAIN)
        mydeps = [d for d in deps if d is not None] + list(self.base_deps)
        if eng in self.CHAIN and self.last_compute[eng] is not None:
            mydeps.append(self.last_compute[eng])
        tokv = None
        if inc:
            self.sems[sem][1] += 1
            tokv = self.sems[sem][1]
        o = _Op(fn, mydeps, inc, 1, tokv, sem, tag)
        self.ops[eng].append(o)
        token = (sem, tokv) if inc else None
        if eng in self.CHAIN:
            self.last_compute[eng] = token
        return token

    def dma(self, eng, fn, deps=(), tag=""):
        sem = self.DMASEM[eng]
        self.sems[sem][1] += 16
        tokv = self.sems[sem][1]
        mydeps = [d for d in deps if d is not None] + list(self.base_deps)
        o = _Op(fn, mydeps, True, 16, tokv, sem, tag)
        self.ops[eng].append(o)
        return (sem, tokv)

    def emit(self, block):
        builders = {"pe": block.tensor, "dv": block.vector, "ac": block.scalar,
                    "gp": block.gpsimd, "sy": block.sync}
        semh = {n: s[0] for n, s in self.sems.items()}
        for eng in self.ENG:
            ops = self.ops[eng]
            if not ops:
                continue

            def body(e, ops=ops):
                hw = {}
                for o in ops:
                    need = {}
                    for (sn, v) in o.deps:
                        if v and v > need.get(sn, 0):
                            need[sn] = v
                    for sn, v in sorted(need.items()):
                        if v > hw.get(sn, 0):
                            e.wait_ge(semh[sn], v)
                            hw[sn] = v
                    ins = o.fn(e)
                    if o.inc:
                        if o.inck is None:
                            ins.then_inc(semh[o.sem])
                        else:
                            ins.then_inc(semh[o.sem], o.inck)
            builders[eng](body)

    def tail_wait(self, block, tokens):
        semh = {n: s[0] for n, s in self.sems.items()}

        def body(e):
            need = {}
            for (sn, v) in tokens:
                if v and v > need.get(sn, 0):
                    need[sn] = v
            for sn, v in sorted(need.items()):
                e.wait_ge(semh[sn], v)
        block.sync(body)


F32 = mybir.dt.float32
BF16 = mybir.dt.bfloat16
I32 = mybir.dt.int32
AF = mybir.ActivationFunctionType
ALU = mybir.AluOpType
AX = mybir.AxisListType

D = 1024
TQ = 2048
TK = 4096
NH = 16
CAP = 4096
SCALE = 1.0 / math.sqrt(96.0)


def build_fused(dbg=(), upto=99):
    nc = bass.Bass("TRN2", target_bir_lowering=False, debug=False)
    I = lambda n, s, dt=F32: nc.dram_tensor(n, s, dt, kind="ExternalInput").ap()
    O = lambda n, s, dt=F32: nc.dram_tensor(n, s, dt, kind="ExternalOutput").ap()
    N = lambda n, s, dt=BF16: nc.dram_tensor(n, s, dt).ap()

    x_kv = I("x_kv", [TK, D])
    c_colT = I("c_colT", [128, 8])
    wmod = I("wmod", [D, 6 * D])
    WQ, WK, WV, WO = (I(n, [D, D]) for n in ("wq", "wk", "wv", "wo"))
    WR = I("wr", [D, 512])
    BQ, BK, BV, BO = (I(n, [1, D]) for n in ("bq", "bk", "bv", "bo"))
    BR = I("br", [1, 512])
    wg_t = I("wg_t", [128, 64])
    idf32 = I("idf32", [128, 128])
    idbf = I("idbf", [128, 128], BF16)
    onesf32 = I("onesf32", [1, 128])
    onesbf = I("onesbf", [1, 128], BF16)
    ones4k_bf = I("ones4k_bf", [1, TK], BF16)
    ut_bf = I("ut_bf", [128, 128], BF16)
    all1_bf = I("all1_bf", [128, 128], BF16)
    eq_k = I("eq_k", [128, 8 * 16], BF16)
    eq_r = I("eq_r", [128, 4 * 16], BF16)
    gtokp1 = I("gtokp1", [128, 16])
    iotaEi = I("iotaEi", [128, 8])
    w1t = I("w1t", [32 * 128, D])
    w2m = I("w2m", [4096, D])
    b1m = I("b1m", [32, 128])
    b2r = I("b2r", [1, D])
    s16 = I("s16", [16, 16])
    exr0 = I("exr0", [16, 1])
    exr1 = I("exr1", [16, 1])
    exs0 = I("exs0", [8, 16])
    exs1 = I("exs1", [8, 16])
    iotad = I("iotad", [8, 1])
    iota128 = I("iota128", [128, 1])
    rkoh = I("rkoh", [16, 8])

    outo = O("outo", [TQ, D])
    x2_dram = N("x2_dram", [TQ, D], F32)
    lsort_loc = N("lsort_loc", [CAP, 1], F32)
    cnt_loc = N("cnt_loc", [1, 16], F32)
    h2_all = nc.dram_tensor("h2_all", [8 * TQ, D], BF16,
                            addr_space="Shared").ap()
    lsort_all = N("lsort_all", [8 * CAP, 1], F32)
    cnt_all_d = N("cnt_all", [8, 16], F32)
    y_loc = N("y_loc", [CAP, D])
    y_all = nc.dram_tensor("y_all", [8 * CAP, D], BF16,
                           addr_space="Shared").ap()
    disp_dram = N("disp_dram", [CAP, D])
    dbg_out = {}
    for name, shp, dt in [
        ("d_mod", [1, 6 * D], F32), ("d_xn", [TK, D], BF16),
        ("d_kt", [D, TK], BF16), ("d_qt", [D, TQ], BF16),
        ("d_rt", [512, TK], BF16), ("d_vp", [TK, 1040], BF16),
        ("d_normk", [16, TK], BF16), ("d_negm", [16, TQ], BF16),
        ("d_attn", [128, 8 * TQ], BF16), ("d_logits", [TQ, 8], F32),
        ("d_gab", [128, D], F32), ("d_modraw", [1, 6 * D], F32),
        ("d_wops", [TQ, D], F32), ("d_wos", [128, 8 * D], BF16),
    ]:
        if name in dbg:
            dbg_out[name] = nc.dram_tensor(name, shp, dt, kind="ExternalOutput").ap()

    xn_dram = N("xn_dram", [TK, D])
    kt_dram = N("kt_dram", [D, TK])
    rt_dram = N("rt_dram", [512, TK])
    qt_dram = N("qt_dram", [D, TQ])
    vp_dram = N("vp_dram", [TK, 1040])
    negm_dram = N("negm_dram", [16, TQ])
    h2_dram = N("h2_dram", [TQ, D])

    es = contextlib.ExitStack()
    sb = lambda n, s, dt=F32: es.enter_context(nc.sbuf_tensor(n, s, dt))
    P = Prog(nc, es)
    ST = []

    id_f = sb("id_f", [128, 128])
    ones_f = sb("ones_f", [1, 128])
    iotaE = sb("iotaE", [128, 8])
    rt_cols = sb("rt_cols", [128, 16 * 8])
    gbB = sb("gbB", [128, D])
    phA = contextlib.ExitStack()
    sa = lambda n, s, dt=F32: phA.enter_context(nc.sbuf_tensor(n, s, dt))
    id_b = sa("id_b", [128, 128], BF16)
    ones_b = sa("ones_b", [1, 128], BF16)
    ut_b = sa("ut_b", [128, 128], BF16)
    all1_b = sa("all1_b", [128, 128], BF16)
    eqk_b = sa("eqk_b", [128, 8 * 16], BF16)
    eqr_b = sa("eqr_b", [128, 4 * 16], BF16)
    gtok_f = sa("gtok_f", [128, 16])
    wg_b = sa("wg_b", [128, 64], BF16)
    caCol1 = sa("caCol1", [128, 8])
    saColb = sa("saColb", [128, 8], BF16)
    cb1B = sa("cb1B", [128, D])
    sbB = sa("sbB", [128, D])
    bvB = sa("bvB", [128, D])
    gabo_b = sa("gabo_b", [1, D], BF16)
    bqCol = sa("bqCol", [128, 8])
    bkCol = sa("bkCol", [128, 8])
    brCol = sa("brCol", [128, 4])
    normk = sa("normk", [16, TK], BF16)
    normq = sa("normq", [16, TQ], BF16)
    negm_s = sa("negm_s", [16, TQ], BF16)
    wo_s = sa("wo_s", [128, 8 * D], BF16)
    m0_s = sa("m0_s", [128, 16 * 8], BF16)
    m1_s = sa("m1_s", [128, 16 * 8], BF16)
    xbuf = sa("xbuf", [128, 2 * D])
    sqscr = sa("sqscr", [128, D], BF16)
    epscol = sa("epscol", [128, 1])
    c_s = sa("c_s", [128, 8])
    wgt_f = sa("wgt_f", [128, 64])
    xcbuf = sa("xcbuf", [128, 2 * D])
    xnstg = sa("xnstg", [128, 2 * D], BF16)
    stat = sa("stat", [128, 8])
    kmc = sa("kmc", [16, 2])
    phW = contextlib.ExitStack()
    wq_s = phW.enter_context(nc.sbuf_tensor("wq_s", [128, 8 * D], BF16))
    wk_s = phW.enter_context(nc.sbuf_tensor("wk_s", [128, 8 * D], BF16))
    wv_s = phW.enter_context(nc.sbuf_tensor("wv_s", [128, 8 * D], BF16))
    wr_s = phW.enter_context(nc.sbuf_tensor("wr_s", [128, 8 * 512], BF16))
    ph01 = contextlib.ExitStack()
    mod_row = ph01.enter_context(nc.sbuf_tensor("mod_row", [1, 6 * D], F32))
    brow = ph01.enter_context(nc.sbuf_tensor("brow", [1, 5 * D], F32))
    beff = ph01.enter_context(nc.sbuf_tensor("beff", [1, 2 * D], F32))
    gaB = ph01.enter_context(nc.sbuf_tensor("gaB", [128, D], F32))

    for a, b_ in [(id_f[:], idf32), (id_b[:], idbf), (ones_f[:], onesf32),
                  (ones_b[:], onesbf), (ut_b[:], ut_bf), (all1_b[:], all1_bf),
                  (eqk_b[:], eq_k), (eqr_b[:], eq_r), (gtok_f[:], gtokp1),
                  (iotaE[:], iotaEi),
                  (brow[0:1, 0:D], BQ), (brow[0:1, D:2 * D], BK),
                  (brow[0:1, 2 * D:3 * D], BV), (brow[0:1, 3 * D:4 * D], BO),
                  (brow[0:1, 4 * D:4 * D + 512], BR)]:
        P.dma("sy", lambda e, a=a, b_=b_: e.dma_start(out=a, in_=b_))
    tc1 = P.dma("sy", lambda e: e.dma_start(out=c_s[:], in_=c_colT))
    tc2 = P.dma("sy", lambda e: e.dma_start(out=wgt_f[:], in_=wg_t))
    P.op("dv", lambda e: e.tensor_copy(out=wg_b[:], in_=wgt_f[:]), deps=[tc2])

    ph0 = contextlib.ExitStack()
    wm_s = ph0.enter_context(nc.sbuf_tensor("wm_s", [128, 4096], BF16))
    mod_ps = ph0.enter_context(nc.psum_tensor("mod_ps", [1, 512], F32))
    tp_ps = ph0.enter_context(nc.psum_tensor("tp_ps", [128, 8], F32))
    bc_ps = ph0.enter_context(nc.psum_tensor("bc_ps", [128, 512], F32))
    sig = ph0.enter_context(nc.sbuf_tensor("sig", [128, 8], F32))
    silu_b = ph0.enter_context(nc.sbuf_tensor("silu_b", [128, 8], BF16))

    teps = P.op("dv", lambda e: e.memset(epscol[:], 1e-5))
    tsg = P.op("ac", lambda e: e.activation(out=sig[:], in_=c_s[:], func=AF.Exp,
                                            scale=-1.0), deps=[tc1])
    P.op("dv", lambda e: e.tensor_scalar_add(out=sig[:], in0=sig[:], scalar1=1.0),
         deps=[tsg])
    P.op("dv", lambda e: e.reciprocal(out=sig[:], in_=sig[:]))
    P.op("dv", lambda e: e.tensor_tensor(out=sig[:], in0=sig[:], in1=c_s[:], op=ALU.mult))
    tsilu = P.op("dv", lambda e: e.tensor_copy(out=silu_b[:], in_=sig[:]))

    mod_evac = []
    tm_prev = None
    for cb in range(12):
        tw = P.dma("gp", lambda e, cb=cb: e.dma_start(
            out=wm_s[:].rearrange("p (dc f) -> p dc f", dc=8),
            in_=wmod[:, cb * 512:(cb + 1) * 512].rearrange("(dc p) f -> p dc f", p=128)),
            deps=([tm_prev] if tm_prev else []))
        tm = None
        for dc in range(8):
            tm = P.op("pe", lambda e, dc=dc: e.matmul(
                mod_ps[:], silu_b[:, dc:dc + 1],
                wm_s[:, dc * 512:(dc + 1) * 512],
                start=(dc == 0), stop=(dc == 7)),
                deps=[tw, tsilu] + (mod_evac[-1:] if mod_evac else []), tok=(dc == 7))
        tm_prev = tm
        mod_evac.append(P.op("dv", lambda e, cb=cb: e.tensor_copy(
            out=mod_row[0:1, cb * 512:(cb + 1) * 512], in_=mod_ps[0:1, :]), deps=[tm]))

    tclamp = mod_evac[-1]
    for base in (2 * D, 5 * D):
        r = mod_row[0:1, base:base + D]
        P.op("dv", lambda e, r=r: e.tensor_scalar(
            out=beff[0:1, 0:D], in0=r, scalar1=0.0, scalar2=None, op0=ALU.is_ge))
        P.op("dv", lambda e: e.tensor_scalar(
            out=beff[0:1, 0:D], in0=beff[0:1, 0:D], scalar1=2.0, scalar2=-1.0,
            op0=ALU.mult, op1=ALU.add))
        P.op("dv", lambda e, r=r: e.tensor_tensor(
            out=beff[0:1, D:2 * D], in0=r, in1=beff[0:1, 0:D], op=ALU.mult))
        P.op("dv", lambda e: e.tensor_scalar_max(
            out=beff[0:1, D:2 * D], in0=beff[0:1, D:2 * D], scalar1=0.25))
        tclamp = P.op("dv", lambda e, r=r: e.tensor_tensor(
            out=r, in0=beff[0:1, 0:D], in1=beff[0:1, D:2 * D], op=ALU.mult))
    garow = mod_row[0:1, 2 * D:3 * D]
    gbrow = mod_row[0:1, 5 * D:6 * D]
    if "d_mod" in dbg_out:
        ST.append(P.dma("sy", lambda e: e.dma_start(out=dbg_out["d_mod"],
                                                    in_=mod_row[0:1, :]), deps=[tclamp]))

    tt = None
    for j in range(8):
        tt = P.op("pe", lambda e, j=j: e.transpose(
            tp_ps[:, j:j + 1], mod_row[0:1, D + j * 128:D + (j + 1) * 128], id_f[0:1, 0:1]),
            deps=[mod_evac[-1]], tok=(j == 7))
    tca = P.op("dv", lambda e: e.tensor_scalar_add(out=caCol1[:], in0=tp_ps[:], scalar1=1.0),
               deps=[tt])
    tt = None
    for j in range(8):
        tt = P.op("pe", lambda e, j=j: e.transpose(
            tp_ps[:, j:j + 1], mod_row[0:1, j * 128:(j + 1) * 128], id_f[0:1, 0:1]),
            deps=[tca], tok=(j == 7))
    tsa = P.op("dv", lambda e: e.tensor_copy(out=saColb[:], in_=tp_ps[:]), deps=[tt])

    tprev = tclamp
    for (dst, base, add1) in ((cb1B, 4 * D, True), (sbB, 3 * D, False),
                              (gaB, 2 * D, False), (gbB, 5 * D, False)):
        for half in range(2):
            tm = P.op("pe", lambda e, base=base, half=half: e.matmul(
                bc_ps[:], ones_f[:], mod_row[0:1, base + half * 512:base + (half + 1) * 512],
                start=True, stop=True), deps=[tprev, tclamp], tok=True)
            tprev = P.op("dv", lambda e, dst=dst, half=half, add1=add1:
                         (e.tensor_scalar_add(out=dst[:, half * 512:(half + 1) * 512],
                                              in0=bc_ps[:], scalar1=1.0) if add1 else
                          e.tensor_copy(out=dst[:, half * 512:(half + 1) * 512],
                                        in_=bc_ps[:])), deps=[tm])
    tgabo = P.op("dv", lambda e: e.tensor_tensor(
        out=gabo_b[0:1, :], in0=garow, in1=brow[0:1, 3 * D:4 * D], op=ALU.mult))
    if "d_gab" in dbg_out:
        ST.append(P.dma("sy", lambda e: e.dma_start(out=dbg_out["d_gab"], in_=gaB[:]),
                        deps=[tprev]))
    tiles_done = tprev
    ph0.close()
    if upto < 1:
        blk = es.enter_context(nc.Block())
        P.emit(blk)
        P.tail_wait(blk, ST)
        return nc, es

    ph1 = contextlib.ExitStack()
    bias_ps = ph1.enter_context(nc.psum_tensor("bias_ps", [1, 512], F32))
    tpb_ps = ph1.enter_context(nc.psum_tensor("tpb_ps", [128, 8], F32))
    bcb_ps = ph1.enter_context(nc.psum_tensor("bcb_ps", [128, 512], F32))

    wload = {}
    for (name, wsb, src) in (("q", wq_s, WQ), ("k", wk_s, WK), ("v", wv_s, WV),
                             ("r", wr_s, WR), ("o", wo_s, WO)):
        cw = wsb.shape[1] // 8
        wload[name] = [P.dma("gp", lambda e, wsb=wsb, src=src, dc=dc, cw=cw: e.dma_start(
            out=wsb[:, dc * cw:(dc + 1) * cw], in_=src[dc * 128:(dc + 1) * 128, :]))
            for dc in range(8)]

    bias_done = {}
    fold_done = []
    prev_bias_ps = None
    prev_tp = tsa
    prev_bc = tiles_done
    for (name, wsb, no, boff, col_dst) in (
            ("q", wq_s, 8, 0, bqCol), ("k", wk_s, 8, D, bkCol),
            ("r", wr_s, 4, 4 * D, brCol), ("v", wv_s, 8, 2 * D, None)):
        cw = no * 128
        ev = []
        nhalf = (no * 128) // 512
        for half in range(nhalf):
            tm = None
            for dc in range(8):
                tm = P.op("pe", lambda e, wsb=wsb, dc=dc, half=half, cw=cw: e.matmul(
                    bias_ps[:], saColb[:, dc:dc + 1],
                    wsb[:, dc * cw + half * 512:dc * cw + half * 512 + 512],
                    start=(dc == 0), stop=(dc == 7)),
                    deps=[wload[name][dc], tsa, prev_bias_ps], tok=(dc == 7))
            tev = P.op("dv", lambda e, boff=boff, half=half: e.tensor_tensor(
                out=beff[0:1, half * 512:(half + 1) * 512], in0=bias_ps[0:1, :],
                in1=brow[0:1, boff + half * 512:boff + (half + 1) * 512], op=ALU.add),
                deps=[tm])
            prev_bias_ps = tev
            ev.append(tev)
        if col_dst is not None:
            tt = None
            for j in range(no):
                tt = P.op("pe", lambda e, j=j: e.transpose(
                    tpb_ps[:, j:j + 1], beff[0:1, j * 128:(j + 1) * 128], id_f[0:1, 0:1]),
                    deps=ev + [prev_tp], tok=(j == no - 1))
            tcd = P.op("dv", lambda e, col_dst=col_dst, no=no: e.tensor_copy(
                out=col_dst[:, 0:no], in_=tpb_ps[:, 0:no]), deps=[tt])
            prev_tp = tcd
            bias_done[name] = tcd
        else:
            tb = None
            for half in range(2):
                tm = P.op("pe", lambda e, half=half: e.matmul(
                    bcb_ps[:], ones_f[:], beff[0:1, half * 512:(half + 1) * 512],
                    start=True, stop=True), deps=ev + [prev_bc], tok=True)
                tb = P.op("dv", lambda e, half=half: e.tensor_copy(
                    out=bvB[:, half * 512:(half + 1) * 512], in_=bcb_ps[:]), deps=[tm])
                prev_bc = tb
            bias_done[name] = tb
        for dc in range(8):
            fold_done.append(P.op("dv", lambda e, wsb=wsb, dc=dc, cw=cw: e.tensor_scalar(
                out=wsb[:, dc * cw:(dc + 1) * cw], in0=wsb[:, dc * cw:(dc + 1) * cw],
                scalar1=caCol1[:, dc:dc + 1], scalar2=None, op0=ALU.mult),
                deps=[bias_done[name], tca]))
    for dc in range(8):
        fold_done.append(P.op("dv", lambda e, dc=dc: e.tensor_tensor(
            out=wo_s[:, dc * D:(dc + 1) * D], in0=wo_s[:, dc * D:(dc + 1) * D],
            in1=gaB[:], op=ALU.mult), deps=[wload["o"][dc], tiles_done]))
    if "d_wos" in dbg_out:
        ST.append(P.dma("sy", lambda e: e.dma_start(out=dbg_out["d_wos"], in_=wo_s[:]),
                        deps=fold_done[-8:]))
    ph1.close()
    ph01.close()
    if upto < 2:
        blk = es.enter_context(nc.Block())
        P.emit(blk)
        P.tail_wait(blk, ST)
        return nc, es

    xn_done = []
    relL, relC, relN = {}, {}, {}
    for i in range(32):
        par = i % 2
        xb = xbuf[:, par * D:(par + 1) * D]
        xc = xcbuf[:, par * D:(par + 1) * D]
        xn = xnstg[:, par * D:(par + 1) * D]
        st = stat[:, par * 4:par * 4 + 4]
        tl = P.dma("sy", lambda e, i=i, xb=xb: e.dma_start(
            out=xb, in_=x_kv[i * 128:(i + 1) * 128, :]), deps=[relL.get(par)])
        P.op("dv", lambda e, xb=xb, st=st: e.reduce_sum(out=st[:, 0:1], in_=xb, axis=AX.X),
             deps=[tl])
        P.op("dv", lambda e, st=st: e.tensor_scalar(
            out=st[:, 1:2], in0=st[:, 0:1], scalar1=1.0 / D, scalar2=None, op0=ALU.mult))
        txc = P.op("dv", lambda e, xb=xb, xc=xc, st=st: e.tensor_scalar(
            out=xc, in0=xb, scalar1=st[:, 1:2], scalar2=None, op0=ALU.subtract),
            deps=[relC.get(par)])
        relL[par] = txc
        P.op("ac", lambda e, xc=xc, st=st: e.activation(
            out=sqscr[:], in_=xc, func=AF.Square, accum_out=st[:, 2:3]), deps=[txc])
        P.op("ac", lambda e, st=st: e.activation(
            out=st[:, 3:4], in_=st[:, 2:3], func=AF.Ln, scale=1.0 / D,
            bias=epscol[:, 0:1]), deps=[teps])
        trho = P.op("ac", lambda e, st=st: e.activation(
            out=st[:, 3:4], in_=st[:, 3:4], func=AF.Exp, scale=-0.5))
        txn = P.op("dv", lambda e, xc=xc, xn=xn, st=st: e.tensor_scalar(
            out=xn, in0=xc, scalar1=st[:, 3:4], scalar2=None, op0=ALU.mult),
            deps=[trho, relN.get(par)])
        relC[par] = txn
        ts = P.dma("sy", lambda e, i=i, xn=xn: e.dma_start(
            out=xn_dram[i * 128:(i + 1) * 128, :], in_=xn), deps=[txn])
        relN[par] = ts
        xn_done.append(ts)
    if "d_xn" in dbg_out:
        ST.append(P.dma("sy", lambda e: e.dma_start(out=dbg_out["d_xn"], in_=xn_dram),
                        deps=xn_done))
    if upto < 3:
        blk = es.enter_context(nc.Block())
        P.emit(blk)
        P.tail_wait(blk, ST)
        return nc, es

    ph3 = contextlib.ExitStack()
    xnT = ph3.enter_context(nc.sbuf_tensor("xnT", [128, 8 * TK], BF16))
    xnT_tok = [P.dma("sy", lambda e, dcx=dcx: e.dma_start(
        out=xnT[:, dcx * TK:(dcx + 1) * TK],
        in_=xn_dram[:, dcx * 128:(dcx + 1) * 128], transpose=True), deps=xn_done)
        for dcx in range(8)]

    pj_ps = [ph3.enter_context(nc.psum_tensor(f"pj_ps{j}", [128, 512], F32)) for j in range(2)]
    nk_ps = ph3.enter_context(nc.psum_tensor("nk_ps", [16, 512], F32))
    nq_ps = ph3.enter_context(nc.psum_tensor("nq_ps", [16, 512], F32))
    vps = [[ph3.enter_context(nc.psum_tensor(f"v_ps{p}_{j}", [128, 512], F32))
            for j in range(2)] for p in range(2)]
    kstgs = ph3.enter_context(nc.sbuf_tensor("kstg", [128, 2 * 512], BF16))
    sqstg = ph3.enter_context(nc.sbuf_tensor("sqstg", [128, 2 * 512], BF16))
    vstg = ph3.enter_context(nc.sbuf_tensor("vstg", [128, 2 * 1040], BF16))

    relPP, relKS, relSQ = {}, {}, {}
    state = {"cnt": 0}
    kt_toks, qt_toks, v_toks = [], [], []
    nk_evac, nq_evac = [], []

    def one_proj(blkidx, wsb, cw, oc, dst, bcol, bcol_tok, eqt, nps, nst, nstop, nprev=None,
                 nps2=None, nst2=False, nstop2=False, nprev2=None):
        par = state["cnt"] % 2
        state["cnt"] += 1
        kst = kstgs[:, par * 512:(par + 1) * 512]
        sst = sqstg[:, par * 512:(par + 1) * 512]
        pp = pj_ps[par]
        tm = None
        for dc in range(8):
            tm = P.op("pe", lambda e, wsb=wsb, cw=cw, oc=oc, dc=dc, blkidx=blkidx, pp=pp: e.matmul(
                pp[:], wsb[:, dc * cw + oc * 128:dc * cw + (oc + 1) * 128],
                xnT[:, dc * TK + blkidx * 512:dc * TK + (blkidx + 1) * 512],
                start=(dc == 0), stop=(dc == 7)),
                deps=[xnT_tok[dc], relPP.get(par)] + fold_done, tok=(dc == 7))
        tev = P.op("dv", lambda e, pp=pp, kst=kst, bcol=bcol, oc=oc: e.tensor_scalar(
            out=kst, in0=pp[:], scalar1=bcol[:, oc:oc + 1], scalar2=None, op0=ALU.add),
            deps=[tm, bcol_tok, relKS.get(par)])
        relPP[par] = tev
        tsq = P.op("ac", lambda e, kst=kst, sst=sst: e.activation(
            out=sst, in_=kst, func=AF.Square), deps=[tev, relSQ.get(par)])
        tnm = P.op("pe", lambda e, eqt=eqt, oc=oc, sst=sst, nps=nps, nst=nst, nstop=nstop: e.matmul(
            nps[:], eqt[:, oc * 16:(oc + 1) * 16], sst, start=nst, stop=nstop,
            skip_group_check=True), deps=[tsq, nprev], tok=True)
        if nps2 is not None:
            tnm = P.op("pe", lambda e, eqt=eqt, oc=oc, sst=sst, nps2=nps2, nst2=nst2, nstop2=nstop2: e.matmul(
                nps2[:], eqt[:, oc * 16:(oc + 1) * 16], sst, start=nst2, stop=nstop2,
                skip_group_check=True), deps=[tsq, nprev2], tok=True)
        relSQ[par] = tnm
        tst = P.dma("sy", lambda e, dst=dst, oc=oc, blkidx=blkidx, kst=kst: e.dma_start(
            out=dst[oc * 128:(oc + 1) * 128, blkidx * 512:(blkidx + 1) * 512], in_=kst),
            deps=[tev])
        relKS[par] = tst
        return tst, tnm

    last_nq_r = [None]
    for blkidx in range(8):
        qside = blkidx < 4
        last_nk = None
        for oc in range(8):
            tst, last_nk = one_proj(blkidx, wk_s, D, oc, kt_dram, bkCol,
                                    bias_done["k"], eqk_b, nk_ps, oc == 0, False,
                                    nprev=(nk_evac[-1] if (oc == 0 and nk_evac) else None))
            kt_toks.append(tst)
        if qside:
            for oc in range(8):
                tst, _ = one_proj(blkidx, wq_s, D, oc, qt_dram, bqCol,
                                  bias_done["q"], eqk_b, nq_ps, oc == 0, False,
                                  nprev=(nq_evac[-1] if (oc == 0 and nq_evac) else None))
                qt_toks.append(tst)
        for oc in range(4):
            tst, last_nk = one_proj(blkidx, wr_s, 512, oc, rt_dram, brCol,
                                    bias_done["r"], eqr_b, nk_ps, False, oc == 3,
                                    nps2=(nq_ps if qside else None), nst2=False,
                                    nstop2=(oc == 3))
            kt_toks.append(tst)
            if qside:
                last_nq_r[0] = relSQ[(state["cnt"] - 1) % 2]
        nk_evac.append(P.op("dv", lambda e, blkidx=blkidx: e.tensor_copy(
            out=normk[:, blkidx * 512:(blkidx + 1) * 512], in_=nk_ps[:]), deps=[last_nk]))
        if qside:
            nq_evac.append(P.op("dv", lambda e, blkidx=blkidx: e.tensor_copy(
                out=normq[:, blkidx * 512:(blkidx + 1) * 512], in_=nq_ps[:]),
                deps=[last_nq_r[0]]))


    tmset = []
    relVV, relVS = {}, {}
    for tc in range(32):
        par = tc % 2
        vst = vstg[:, par * 1040:(par + 1) * 1040]
        if tc < 2:
            tmset.append(P.op("gp", lambda e, vst=vst: e.memset(vst, 1.0)))
        tms = []
        for half in range(2):
            tm = None
            for dc in range(8):
                tm = P.op("pe", lambda e, tc=tc, dc=dc, par=par, half=half: e.matmul(
                    vps[par][half][:], xnT[:, dc * TK + tc * 128:dc * TK + (tc + 1) * 128],
                    wv_s[:, dc * D + half * 512:dc * D + (half + 1) * 512],
                    start=(dc == 0), stop=(dc == 7)),
                    deps=[xnT_tok[dc], relVV.get((par, half))] + fold_done,
                    tok=(dc == 7))
            tms.append(tm)
        tev = None
        for half in range(2):
            tev = P.op("dv", lambda e, vst=vst, par=par, half=half: e.tensor_tensor(
                out=vst.rearrange("p (h j) -> p h j", j=65)[:, half * 8:(half + 1) * 8, 0:64],
                in0=vps[par][half][:].rearrange("p (h j) -> p h j", j=64),
                in1=bvB[:, half * 512:(half + 1) * 512].rearrange("p (h j) -> p h j", j=64),
                op=ALU.add),
                deps=[tms[half], bias_done["v"], relVS.get(par)] + (tmset if tc < 2 else []))
            relVV[(par, half)] = tev
        ts = P.dma("sy", lambda e, tc=tc, vst=vst: e.dma_start(
            out=vp_dram[tc * 128:(tc + 1) * 128, :], in_=vst), deps=[tev])
        relVS[par] = ts
        v_toks.append(ts)

    t = P.op("dv", lambda e: e.reduce_max(out=kmc[:, 0:1], in_=normk[:], axis=AX.X),
             deps=nk_evac)
    t = P.op("ac", lambda e: e.activation(out=kmc[:, 1:2], in_=kmc[:, 0:1], func=AF.Ln),
             deps=[t])
    t = P.op("ac", lambda e: e.activation(out=kmc[:, 0:1], in_=kmc[:, 1:2], func=AF.Exp,
                                          scale=0.5), deps=[t])
    t2 = P.op("ac", lambda e: e.activation(out=normq[:], in_=normq[:], func=AF.Ln),
              deps=nq_evac)
    t2 = P.op("ac", lambda e: e.activation(out=normq[:], in_=normq[:], func=AF.Exp,
                                           scale=0.5), deps=[t2])
    tneg = P.op("dv", lambda e: e.tensor_scalar(
        out=negm_s[:], in0=normq[:], scalar1=kmc[:, 0:1], scalar2=-1.0,
        op0=ALU.mult, op1=ALU.mult), deps=[t, t2])
    tnegst = P.dma("sy", lambda e: e.dma_start(out=negm_dram, in_=negm_s[:]), deps=[tneg])
    for nm, dr, tks in (("d_kt", kt_dram, kt_toks), ("d_rt", rt_dram, kt_toks),
                        ("d_qt", qt_dram, qt_toks), ("d_vp", vp_dram, v_toks),
                        ("d_normk", normk[:], nk_evac), ("d_negm", negm_s[:], [tneg])):
        if nm in dbg_out:
            ST.append(P.dma("sy", lambda e, nm=nm, dr=dr: e.dma_start(
                out=dbg_out[nm], in_=dr), deps=tks))
    if upto < 4:
        blk = es.enter_context(nc.Block())
        P.emit(blk)
        P.tail_wait(blk, ST)
        return nc, es

    ph3.close()
    phW.close()
    ph4a = contextlib.ExitStack()
    attnT = ph4a.enter_context(nc.sbuf_tensor("attnT", [128, 8 * TQ], BF16))
    ph4 = contextlib.ExitStack()
    kbuf = [ph4.enter_context(nc.sbuf_tensor(f"kbuf{j}", [128, TK], BF16)) for j in range(2)]
    qbuf = [ph4.enter_context(nc.sbuf_tensor(f"qbuf{j}", [128, TQ], BF16)) for j in range(2)]
    vbuf = [ph4.enter_context(nc.sbuf_tensor(f"vbuf{j}", [128, 32 * 260], BF16)) for j in range(2)]
    pstg = [ph4.enter_context(nc.sbuf_tensor(f"pstg{j}", [128, 1536], BF16)) for j in range(2)]
    rlb_s = ph4.enter_context(nc.sbuf_tensor("rlb_s", [64, 512], F32))
    rl_s = ph4.enter_context(nc.sbuf_tensor("rl_s", [1, 512], F32))
    s_ps = [ph4.enter_context(nc.psum_tensor(f"s_ps{j}", [128, 1536], F32)) for j in range(2)]
    av_ps = ph4.enter_context(nc.psum_tensor("av_ps", [65, 512], F32))
    rlb_ps = ph4.enter_context(nc.psum_tensor("rlb_ps", [64, 512], F32))

    GRPS = []
    k0 = 0
    while k0 < 32:
        GRPS.append((k0, min(k0 + 3, 32)))
        k0 += 3

    kq_load, v_load, head_rel, vg_rel = {}, {}, {}, {}
    ones_init = [P.dma("sy", lambda e, j=j: e.dma_start(
        out=kbuf[j][96:97, :], in_=ones4k_bf)) for j in range(2)]

    def load_head(h):
        par = h % 2
        d0 = [kt_toks[-1], qt_toks[-1], tnegst, head_rel.get(h - 2), ones_init[par]]
        toks = [
            P.dma("sy", lambda e, h=h, par=par: e.dma_start(
                out=kbuf[par][0:64, :], in_=kt_dram[h * 64:(h + 1) * 64, :]), deps=d0),
            P.dma("sy", lambda e, h=h, par=par: e.dma_start(
                out=kbuf[par][64:96, :], in_=rt_dram[h * 32:(h + 1) * 32, :]), deps=d0),
            P.dma("sy", lambda e, h=h, par=par: e.dma_start(
                out=qbuf[par][0:64, :], in_=qt_dram[h * 64:(h + 1) * 64, :]), deps=d0),
            P.dma("sy", lambda e, h=h, par=par: e.dma_start(
                out=qbuf[par][64:96, :], in_=rt_dram[h * 32:(h + 1) * 32, 0:TQ]), deps=d0),
            P.dma("sy", lambda e, h=h, par=par: e.dma_start(
                out=qbuf[par][96:97, :], in_=negm_dram[h:h + 1, :]), deps=d0),
        ]
        kq_load[h] = toks

    def load_vg(g):
        par = g % 2
        deps = [vg_rel.get(g - 2), v_toks[-1]]
        v_load[g] = [P.dma("sy", lambda e, g=g, par=par, piece=piece: e.dma_start(
            out=vbuf[par][:, piece * 8 * 260:(piece + 1) * 8 * 260].rearrange(
                "p (kc j) -> p kc j", j=260),
            in_=vp_dram.rearrange("(kc p) (h j) -> p kc h j", p=128, j=65)[
                :, piece * 8:(piece + 1) * 8, 4 * g:4 * g + 4, :].rearrange(
                "p kc h j -> p kc (h j)")), deps=deps) for piece in range(4)]

    load_head(0)
    load_head(1)
    load_vg(0)
    load_vg(1)
    sg_rel, pst_rel = {}, {}
    attn_evac = []
    for h in range(NH):
        par = h % 2
        g = h // 4
        gpar = g % 2
        for qb in range(4):
            av_last = None
            tS_l = {}

            def emit_qk(gi, h=h, qb=qb, par=par):
                g0, g1 = GRPS[gi]
                spar = gi % 2
                tS = None
                for kc in range(g0, g1):
                    tS = P.op("pe", lambda e, h=h, qb=qb, kc=kc, g0=g0, spar=spar, par=par: e.matmul(
                        s_ps[spar][:, (kc - g0) * 512:(kc - g0 + 1) * 512],
                        kbuf[par][0:97, kc * 128:(kc + 1) * 128],
                        qbuf[par][0:97, qb * 512:(qb + 1) * 512],
                        start=True, stop=True, skip_group_check=True),
                        deps=kq_load[h] + [sg_rel.get(spar)], tok=(kc == g1 - 1))
                tS_l[gi] = tS
            emit_qk(0)
            for gi, (g0, g1) in enumerate(GRPS):
                spar = gi % 2
                if gi + 1 < len(GRPS):
                    emit_qk(gi + 1)
                tE = P.op("ac", lambda e, spar=spar, n=(g1 - g0): e.activation(
                    out=pstg[spar][:, 0:n * 512], in_=s_ps[spar][:, 0:n * 512],
                    func=AF.Exp, scale=SCALE), deps=[tS_l[gi], pst_rel.get(spar)])
                sg_rel[spar] = tE
                for kc in range(g0, g1):
                    av_last = P.op("pe", lambda e, h=h, kc=kc, g0=g0, gpar=gpar, spar=spar: e.matmul(
                        av_ps[:],
                        vbuf[gpar][:, kc * 260 + (h % 4) * 65:kc * 260 + (h % 4) * 65 + 65],
                        pstg[spar][:, (kc - g0) * 512:(kc - g0 + 1) * 512],
                        start=(kc == 0), stop=(kc == 31), skip_group_check=True),
                        deps=[tE] + v_load[g] + (attn_evac[-1:] if attn_evac else []),
                        tok=(kc == 31))
                pst_rel[spar] = av_last
            trl = P.op("dv", lambda e: e.reciprocal(out=rl_s[0:1, :], in_=av_ps[64:65, :]),
                       deps=[av_last])
            tbm = P.op("pe", lambda e: e.matmul(rlb_ps[:], ones_f[0:1, 0:64], rl_s[0:1, :],
                                                start=True, stop=True, skip_group_check=True),
                       deps=[trl], tok=True)
            tbe = P.op("dv", lambda e: e.tensor_copy(out=rlb_s[:], in_=rlb_ps[:]), deps=[tbm])
            tev = P.op("dv", lambda e, h=h, qb=qb: e.tensor_tensor(
                out=attnT[(h % 2) * 64:(h % 2) * 64 + 64,
                          (h // 2) * TQ + qb * 512:(h // 2) * TQ + (qb + 1) * 512],
                in0=av_ps[0:64, :], in1=rlb_s[:], op=ALU.mult), deps=[tbe])
            attn_evac.append(tev)
        head_rel[h] = attn_evac[-1]
        if h % 4 == 3:
            vg_rel[g] = attn_evac[-1]
        if h + 2 < NH:
            load_head(h + 2)
        if h % 4 == 3 and g + 2 < 4:
            load_vg(g + 2)
    if "d_attn" in dbg_out:
        ST.append(P.dma("sy", lambda e: e.dma_start(out=dbg_out["d_attn"], in_=attnT[:]),
                        deps=attn_evac))
    if upto < 5:
        blk = es.enter_context(nc.Block())
        P.emit(blk)
        P.tail_wait(blk, ST)
        return nc, es

    ph4.close()
    ph6s = contextlib.ExitStack()
    x2_s = ph6s.enter_context(nc.sbuf_tensor("x2_s", [128, 2 * D], F32))
    h2_stg = ph6s.enter_context(nc.sbuf_tensor("h2_stg", [128, 2 * D], BF16))
    xn2buf = ph6s.enter_context(nc.sbuf_tensor("xn2buf", [128, 2 * D], F32))
    stat2 = ph6s.enter_context(nc.sbuf_tensor("stat2", [128, 8], F32))
    ph6 = contextlib.ExitStack()
    wo_ps = [[ph6.enter_context(nc.psum_tensor(f"wo_ps{p}_{j}", [128, 512], F32))
              for j in range(2)] for p in range(2)]
    x_rel = dict(relL)
    relWO = {}
    relH2 = {}
    relX2S = {}
    h2_toks = []
    x2_toks = []
    for tc2 in range(16):
        par = tc2 % 2
        xb = xbuf[:, par * D:(par + 1) * D]
        tl = P.dma("sy", lambda e, tc2=tc2, xb=xb: e.dma_start(
            out=xb, in_=x_kv[tc2 * 128:(tc2 + 1) * 128, :]), deps=[x_rel.get(par)])
        tdone = []
        for half in range(2):
            for ac_ in range(8):
                P.op("pe", lambda e, tc2=tc2, ac_=ac_, par=par, half=half: e.matmul(
                    wo_ps[par][half][:],
                    attnT[:, ac_ * TQ + tc2 * 128:ac_ * TQ + (tc2 + 1) * 128],
                    wo_s[:, ac_ * D + half * 512:ac_ * D + (half + 1) * 512],
                    start=(ac_ == 0), stop=False, skip_group_check=True),
                    deps=[attn_evac[-1], relWO.get((par, half))] + fold_done[-8:])
            tm = P.op("pe", lambda e, par=par, half=half: e.matmul(
                wo_ps[par][half][:], ones_b[0:1, 0:128],
                gabo_b[0:1, half * 512:(half + 1) * 512], start=False, stop=True,
                skip_group_check=True), deps=[tgabo], tok=True)
            tdone.append(tm)
        if "d_wops" in dbg_out:
            for half in range(2):
                tdbg = P.op("dv", lambda e, par=par, half=half: e.tensor_copy(
                    out=xn2buf[:, half * 512:(half + 1) * 512], in_=wo_ps[par][half][:]),
                    deps=[tdone[half]])
            ST.append(P.dma("sy", lambda e, tc2=tc2: e.dma_start(
                out=dbg_out["d_wops"][tc2 * 128:(tc2 + 1) * 128, :],
                in_=xn2buf[:, 0:D]), deps=[tdbg]))
        tx2 = None
        for half in range(2):
            tx2 = P.op("dv", lambda e, tc2=tc2, par=par, half=half, xb=xb: e.tensor_tensor(
                out=x2_s[:, par * D + half * 512:par * D + (half + 1) * 512],
                in0=wo_ps[par][half][:], in1=xb[:, half * 512:(half + 1) * 512], op=ALU.add),
                deps=[tdone[half], tl] + (relX2S.get(par) or []))
            relWO[(par, half)] = tx2
        x_rel[par] = tx2
        tso = P.dma("sy", lambda e, tc2=tc2, par=par: e.dma_start(
            out=x2_dram[tc2 * 128:(tc2 + 1) * 128, :], in_=x2_s[:, par * D:(par + 1) * D]),
            deps=[tx2])
        x2_toks.append(tso)
        x2t = x2_s[:, par * D:(par + 1) * D]
        xc2 = xn2buf[:, par * D:(par + 1) * D]
        st = stat2[:, par * 4:par * 4 + 4]
        P.op("dv", lambda e, x2t=x2t, st=st: e.reduce_sum(out=st[:, 0:1], in_=x2t, axis=AX.X),
             deps=[tx2])
        P.op("dv", lambda e, st=st: e.tensor_scalar(
            out=st[:, 1:2], in0=st[:, 0:1], scalar1=1.0 / D, scalar2=None, op0=ALU.mult))
        txc = P.op("dv", lambda e, x2t=x2t, xc2=xc2, st=st: e.tensor_scalar(
            out=xc2, in0=x2t, scalar1=st[:, 1:2], scalar2=None, op0=ALU.subtract))
        relX2S[par] = [txc, tso]
        P.op("ac", lambda e, xc2=xc2, st=st: e.activation(
            out=sqscr[:], in_=xc2, func=AF.Square, accum_out=st[:, 2:3]), deps=[txc])
        P.op("ac", lambda e, st=st: e.activation(
            out=st[:, 3:4], in_=st[:, 2:3], func=AF.Ln, scale=1.0 / D,
            bias=epscol[:, 0:1]), deps=[teps])
        trho = P.op("ac", lambda e, st=st: e.activation(
            out=st[:, 3:4], in_=st[:, 3:4], func=AF.Exp, scale=-0.5))
        P.op("dv", lambda e, xc2=xc2, st=st: e.tensor_scalar(
            out=xc2, in0=xc2, scalar1=st[:, 3:4], scalar2=None, op0=ALU.mult), deps=[trho])
        P.op("dv", lambda e, xc2=xc2: e.tensor_tensor(
            out=xc2, in0=xc2, in1=cb1B[:], op=ALU.mult), deps=[tiles_done])
        h2t = h2_stg[:, par * D:(par + 1) * D]
        th2 = P.op("dv", lambda e, xc2=xc2, h2t=h2t: e.tensor_tensor(
            out=h2t, in0=xc2, in1=sbB[:], op=ALU.add), deps=[relH2.get(par)])
        ts1 = P.dma("sy", lambda e, tc2=tc2, h2t=h2t: e.dma_start(
            out=h2_dram[tc2 * 128:(tc2 + 1) * 128, :], in_=h2t), deps=[th2])
        relH2[par] = ts1
        h2_toks.append(ts1)
    if upto < 6:
        blk = es.enter_context(nc.Block())
        P.emit(blk)
        P.tail_wait(blk, ST)
        return nc, es

    RG = [list(range(8))]
    t_ag_h2 = P.cc(lambda e: e.collective_compute(
        "AllGather", ALU.bypass, replica_groups=RG,
        ins=[h2_dram.bitcast(F32).opt()], outs=[h2_all.bitcast(F32).opt()]),
        deps=h2_toks)

    ph6.close()
    ph7 = contextlib.ExitStack()
    h2T = ph7.enter_context(nc.sbuf_tensor("h2T", [128, 8 * TQ], BF16))
    lgT_ps = ph7.enter_context(nc.psum_tensor("lgT_ps", [8, 512], F32))
    lg_ps = ph7.enter_context(nc.psum_tensor("lg_ps", [128, 8], BF16))
    b_ps = ph7.enter_context(nc.psum_tensor("b_ps", [128, 8], F32))
    a_ps = [ph7.enter_context(nc.psum_tensor(f"a_ps{j}", [128, 8], F32)) for j in range(2)]
    sg_ps = ph7.enter_context(nc.psum_tensor("sg_ps", [128, 16], F32))
    lgT_s = ph7.enter_context(nc.sbuf_tensor("lgT_s", [8, TQ], BF16))
    lgt = ph7.enter_context(nc.sbuf_tensor("lgt", [128, 16 * 8], F32))
    scr8 = ph7.enter_context(nc.sbuf_tensor("scr8", [128, 4 * 8], F32))
    colscr = ph7.enter_context(nc.sbuf_tensor("colscr", [128, 8], F32))
    arun = ph7.enter_context(nc.sbuf_tensor("arun", [128, 2 * 8], F32))
    cntrow = ph7.enter_context(nc.sbuf_tensor("cntrow", [1, 16], F32))
    segrow = ph7.enter_context(nc.sbuf_tensor("segrow", [1, 16], F32))
    segB = ph7.enter_context(nc.sbuf_tensor("segB", [128, 16], F32))
    ticol = ph7.enter_context(nc.sbuf_tensor("ticol", [128, 32], I32))

    h2T_tok = [P.dma("sy", lambda e, dcx=dcx: e.dma_start(
        out=h2T[:, dcx * TQ:(dcx + 1) * TQ],
        in_=h2_dram[:, dcx * 128:(dcx + 1) * 128], transpose=True), deps=h2_toks)
        for dcx in range(8)]
    lgT_evac = []
    for qb in range(4):
        tm = None
        for dc in range(8):
            tm = P.op("pe", lambda e, qb=qb, dc=dc: e.matmul(
                lgT_ps[:], wg_b[:, dc * 8:(dc + 1) * 8],
                h2T[:, dc * TQ + qb * 512:dc * TQ + (qb + 1) * 512],
                start=(dc == 0), stop=(dc == 7)),
                deps=[h2T_tok[dc]] + (lgT_evac[-1:] if lgT_evac else []), tok=(dc == 7))
        lgT_evac.append(P.op("dv", lambda e, qb=qb: e.tensor_copy(
            out=lgT_s[:, qb * 512:(qb + 1) * 512], in_=lgT_ps[:]), deps=[tm]))
    lg_toks = []
    prev_lg = None
    for tc3 in range(16):
        tt = P.op("pe", lambda e, tc3=tc3: e.transpose(
            lg_ps[:, 0:8], lgT_s[0:8, tc3 * 128:(tc3 + 1) * 128], id_b[0:8, 0:8]),
            deps=[lgT_evac[-1], prev_lg], tok=True)
        prev_lg = P.op("dv", lambda e, tc3=tc3: e.tensor_copy(
            out=lgt[:, tc3 * 8:(tc3 + 1) * 8], in_=lg_ps[:, 0:8]), deps=[tt])
        lg_toks.append(prev_lg)
    if "d_logits" in dbg_out:
        ST.append(P.dma("sy", lambda e: e.dma_start(
            out=dbg_out["d_logits"].rearrange("(t p) e -> p t e", p=128),
            in_=lgt[:].rearrange("p (t e) -> p t e", e=8)), deps=lg_toks))

    cnt_done = []
    tae_prev = {0: None, 1: None}
    for tc3 in range(16):
        lg = lgt[:, tc3 * 8:(tc3 + 1) * 8]
        P.op("dv", lambda e, lg=lg: e.reduce_max(out=colscr[:, 0:1], in_=lg, axis=AX.X),
             deps=[lg_toks[tc3]])
        P.op("dv", lambda e, lg=lg: e.tensor_scalar(
            out=scr8[:, 0:8], in0=lg, scalar1=colscr[:, 0:1], scalar2=None,
            op0=ALU.is_equal))
        P.op("dv", lambda e: e.tensor_tensor_scan(
            out=scr8[:, 8:16], data0=scr8[:, 0:8], data1=scr8[:, 0:8],
            initial=0.0, op0=ALU.add, op1=ALU.bypass))
        P.op("dv", lambda e: e.tensor_scalar(
            out=scr8[:, 8:16], in0=scr8[:, 8:16], scalar1=1.0, scalar2=None,
            op0=ALU.is_equal))
        tm0 = P.op("dv", lambda e, tc3=tc3: e.tensor_tensor(
            out=m0_s[:, tc3 * 8:(tc3 + 1) * 8], in0=scr8[:, 0:8], in1=scr8[:, 8:16],
            op=ALU.mult))
        P.op("dv", lambda e, tc3=tc3: e.tensor_scalar(
            out=scr8[:, 16:24], in0=m0_s[:, tc3 * 8:(tc3 + 1) * 8], scalar1=-1e30,
            scalar2=None, op0=ALU.mult))
        P.op("dv", lambda e, lg=lg: e.tensor_tensor(
            out=scr8[:, 16:24], in0=lg, in1=scr8[:, 16:24], op=ALU.add))
        P.op("dv", lambda e: e.reduce_max(out=colscr[:, 1:2], in_=scr8[:, 16:24], axis=AX.X))
        P.op("dv", lambda e: e.tensor_scalar(
            out=scr8[:, 24:32], in0=scr8[:, 16:24], scalar1=colscr[:, 1:2], scalar2=None,
            op0=ALU.is_equal))
        P.op("dv", lambda e: e.tensor_tensor_scan(
            out=scr8[:, 16:24], data0=scr8[:, 24:32], data1=scr8[:, 24:32],
            initial=0.0, op0=ALU.add, op1=ALU.bypass))
        P.op("dv", lambda e: e.tensor_scalar(
            out=scr8[:, 16:24], in0=scr8[:, 16:24], scalar1=1.0, scalar2=None,
            op0=ALU.is_equal))
        tm1 = P.op("dv", lambda e, tc3=tc3: e.tensor_tensor(
            out=m1_s[:, tc3 * 8:(tc3 + 1) * 8], in0=scr8[:, 24:32], in1=scr8[:, 16:24],
            op=ALU.mult))
        td01 = P.op("dv", lambda e: e.tensor_tensor(
            out=colscr[:, 2:3], in0=colscr[:, 1:2], in1=colscr[:, 0:1], op=ALU.subtract))
        P.op("dv", lambda e, tc3=tc3: e.tensor_tensor(
            out=scr8[:, 0:8], in0=m0_s[:, tc3 * 8:(tc3 + 1) * 8], in1=iotaE[:], op=ALU.mult))
        P.op("dv", lambda e, tc3=tc3: e.reduce_sum(
            out=rt_cols[:, tc3 * 8 + 0:tc3 * 8 + 1], in_=scr8[:, 0:8], axis=AX.X))
        P.op("dv", lambda e, tc3=tc3: e.tensor_tensor(
            out=scr8[:, 0:8], in0=m1_s[:, tc3 * 8:(tc3 + 1) * 8], in1=iotaE[:], op=ALU.mult))
        P.op("dv", lambda e, tc3=tc3: e.reduce_sum(
            out=rt_cols[:, tc3 * 8 + 1:tc3 * 8 + 2], in_=scr8[:, 0:8], axis=AX.X))
        tx = P.op("ac", lambda e: e.activation(out=colscr[:, 3:4], in_=colscr[:, 2:3],
                                               func=AF.Exp), deps=[td01])
        P.op("dv", lambda e: e.tensor_scalar_add(out=colscr[:, 3:4], in0=colscr[:, 3:4],
                                                 scalar1=1.0), deps=[tx])
        P.op("dv", lambda e, tc3=tc3: e.reciprocal(
            out=rt_cols[:, tc3 * 8 + 2:tc3 * 8 + 3], in_=colscr[:, 3:4]))
        P.op("dv", lambda e, tc3=tc3: e.tensor_scalar(
            out=rt_cols[:, tc3 * 8 + 3:tc3 * 8 + 4],
            in0=rt_cols[:, tc3 * 8 + 2:tc3 * 8 + 3], scalar1=-1.0, scalar2=1.0,
            op0=ALU.mult, op1=ALU.add))
        for (kk, msrc, aps) in ((0, m0_s, a_ps[0]), (1, m1_s, a_ps[1])):
            tmm = P.op("pe", lambda e, tc3=tc3, msrc=msrc: e.matmul(
                b_ps[:], ut_b[:], msrc[:, tc3 * 8:(tc3 + 1) * 8],
                start=True, stop=True, skip_group_check=True),
                deps=[tm0, tm1] + cnt_done[-1:], tok=True)
            if tc3 == 0:
                P.op("dv", lambda e, kk=kk: e.memset(arun[:, kk * 8:(kk + 1) * 8], 0.0))
            tp = P.op("dv", lambda e, kk=kk: e.tensor_tensor(
                out=scr8[:, 0:8], in0=b_ps[:], in1=arun[:, kk * 8:(kk + 1) * 8],
                op=ALU.add), deps=[tmm])
            P.op("dv", lambda e, tc3=tc3, kk=kk, msrc=msrc: e.tensor_tensor(
                out=scr8[:, 8:16], in0=scr8[:, 0:8], in1=msrc[:, tc3 * 8:(tc3 + 1) * 8],
                op=ALU.mult))
            P.op("dv", lambda e, tc3=tc3, kk=kk: e.reduce_sum(
                out=rt_cols[:, tc3 * 8 + 4 + kk:tc3 * 8 + 5 + kk], in_=scr8[:, 8:16],
                axis=AX.X))
            tam = P.op("pe", lambda e, tc3=tc3, msrc=msrc, aps=aps: e.matmul(
                aps[:], all1_b[:], msrc[:, tc3 * 8:(tc3 + 1) * 8],
                start=(tc3 == 0), stop=(tc3 == 15), skip_group_check=True),
                deps=[tp, tae_prev[kk]], tok=True)
            tae = P.op("dv", lambda e, kk=kk, aps=aps: e.tensor_copy(
                out=arun[:, kk * 8:(kk + 1) * 8], in_=aps[:]), deps=[tam])
            tae_prev[kk] = tae
            cnt_done.append(tae)
    tcnt = P.op("dv", lambda e: e.tensor_copy(out=cntrow[0:1, 0:8], in_=a_ps[0][0:1, :]),
                deps=cnt_done)
    tcnt = P.op("dv", lambda e: e.tensor_copy(out=cntrow[0:1, 8:16], in_=a_ps[1][0:1, :]),
                deps=[tcnt])
    tcnt_st = P.dma("sy", lambda e: e.dma_start(out=cnt_loc, in_=cntrow[0:1, :]), deps=[tcnt])
    t_ag_cnt = P.cc(lambda e: e.collective_compute(
        "AllGather", ALU.bypass, replica_groups=RG,
        ins=[cnt_loc.opt()], outs=[cnt_all_d.opt()]),
        deps=[tcnt_st])
    tsg2 = P.op("dv", lambda e: e.tensor_tensor_scan(
        out=segrow[0:1, :], data0=cntrow[0:1, :], data1=cntrow[0:1, :],
        initial=0.0, op0=ALU.add, op1=ALU.bypass), deps=[tcnt])
    tsg2 = P.op("dv", lambda e: e.tensor_tensor(
        out=segrow[0:1, :], in0=segrow[0:1, :], in1=cntrow[0:1, :], op=ALU.subtract))
    tbm2 = P.op("pe", lambda e: e.matmul(sg_ps[:], ones_f[:], segrow[0:1, :],
                                         start=True, stop=True, skip_group_check=True),
                deps=[tsg2], tok=True)
    tsgB = P.op("dv", lambda e: e.tensor_copy(out=segB[:], in_=sg_ps[:]), deps=[tbm2])

    bc_reg = {}
    ls_toks = []
    for tc3 in range(16):
        for kk in range(2):
            msrc = m0_s if kk == 0 else m1_s
            P.op("dv", lambda e, tc3=tc3, kk=kk, msrc=msrc: e.tensor_tensor(
                out=scr8[:, 0:8], in0=msrc[:, tc3 * 8:(tc3 + 1) * 8],
                in1=segB[:, kk * 8:(kk + 1) * 8], op=ALU.mult), deps=[tsgB])
            P.op("dv", lambda e: e.reduce_sum(out=colscr[:, 4:5], in_=scr8[:, 0:8],
                                              axis=AX.X))
            P.op("dv", lambda e, tc3=tc3, kk=kk: e.tensor_tensor(
                out=colscr[:, 5:6], in0=colscr[:, 4:5],
                in1=rt_cols[:, tc3 * 8 + 4 + kk:tc3 * 8 + 5 + kk], op=ALU.add))
            ci = tc3 * 2 + kk
            t4 = P.op("dv", lambda e, ci=ci: e.tensor_copy(out=ticol[:, ci:ci + 1],
                                                           in_=colscr[:, 5:6]))

            def do(e, tc3=tc3, ci=ci):
                if "bc" not in bc_reg:
                    bc_reg["bc"] = e.to_reg(CAP)
                return e.indirect_dma_start(
                    out=lsort_loc, out_offset=bass.IndirectOffsetOnAxis(ticol[:, ci:ci + 1], 0),
                    in_=gtok_f[:, tc3:tc3 + 1], in_offset=None,
                    bounds_check=bc_reg["bc"], oob_is_err=False)
            ls_toks.append(P.dma("gp", do, deps=[t4]))

    ph7.close()
    ph6s.close()
    ph4a.close()
    phA.close()
    barA = P.barrier()
    t_ag_ls = P.cc(lambda e: e.collective_compute(
        "AllGather", ALU.bypass, replica_groups=RG,
        ins=[lsort_loc.rearrange("(a b) o -> a (o b)", b=128).opt()],
        outs=[lsort_all.rearrange("(a b) o -> a (o b)", b=128).opt()]),
        deps=ls_toks)

    P.base_deps = barA
    phB = contextlib.ExitStack()
    sbB_ = lambda n, s, dt=F32: phB.enter_context(nc.sbuf_tensor(n, s, dt))
    cTB = sbB_("cTB", [16, 8])
    c2B = sbB_("c2B", [8, 16])
    s16B = sbB_("s16B", [16, 16])
    exrB = sbB_("exrB", [16, 2])
    exsB = sbB_("exsB", [8, 2 * 16])
    iodB = sbB_("iodB", [8, 1])
    io128 = sbB_("io128", [128, 1])
    b1c = sbB_("b1c", [128, 32])
    b2Bf = sbB_("b2Bf", [128, D])
    hstg = [sbB_(f"hstg{p}", [128, 384], BF16) for p in range(2)]
    ystg = sbB_("ystg", [128, D], BF16)
    w1_s = sbB_("w1_s", [128, 32 * D], BF16)
    w2_s = sbB_("w2_s", [128, 32 * D], BF16)
    dT = sbB_("dT", [128, 8 * CAP], BF16)
    phB2 = contextlib.ExitStack()
    sb2 = lambda n, s, dt=F32: phB2.enter_context(nc.sbuf_tensor(n, s, dt))
    globB = sb2("globB", [16, 8])
    lbase = sb2("lbase", [8, 16])
    srcb = sb2("srcb", [8, 2])
    sbrow = sb2("sbrow", [1, 18])
    srcrow = sb2("srcrow", [1, 16])
    sbB16 = sb2("sbB16", [128, 16])
    diffB = sb2("diffB", [128, 16])
    totB = sb2("totB", [128, 1])
    b1load = sb2("b1load", [32, 128])
    scrB = sb2("scrB", [128, 2 * 16])
    colsB = sb2("colsB", [128, 8])
    ticB = sb2("ticB", [128, 64], I32)
    invc = sb2("invc", [128, 32])
    dispstg = ystg

    phBp = contextlib.ExitStack()
    tB_ps = phBp.enter_context(nc.psum_tensor("tB_ps", [16, 8], F32))
    t1B_ps = phBp.enter_context(nc.psum_tensor("t1B_ps", [128, 16], F32))
    rB_ps = phBp.enter_context(nc.psum_tensor("rB_ps", [1, 8], F32))
    b1B_ps = phBp.enter_context(nc.psum_tensor("b1B_ps", [128, 32], F32))
    bbB_ps = phBp.enter_context(nc.psum_tensor("bbB_ps", [128, 512], F32))

    tloadB = [P.dma("sy", lambda e, a=a, b_=b_: e.dma_start(out=a, in_=b_))
              for a, b_ in [(s16B[:], s16), (exrB[:, 0:1], exr0),
                            (exrB[:, 1:2], exr1), (exsB[:, 0:16], exs0),
                            (exsB[:, 16:32], exs1), (iodB[:], iotad),
                            (io128[:], iota128), (b2Bf[0:1, :], b2r),
                            (b1load[:], b1m)]]
    tc2l = P.dma("sy", lambda e: e.dma_start(out=c2B[:], in_=cnt_all_d),
                 deps=[t_ag_cnt])
    ttr = P.op("pe", lambda e: e.transpose(tB_ps[:], c2B[0:8, 0:16], id_f[0:8, 0:8]),
               deps=[tc2l], tok=True)
    tctl = P.op("dv", lambda e: e.tensor_copy(out=cTB[:], in_=tB_ps[:]), deps=[ttr])
    tloadB += [tc2l, tctl]
    w1tok = [P.dma("gp", lambda e, hc=hc: e.dma_start(
        out=w1_s[:, hc * D:(hc + 1) * D], in_=w1t[hc * 128:(hc + 1) * 128, :]))
        for hc in range(32)]
    w2tok = [P.dma("gp", lambda e, hc=hc: e.dma_start(
        out=w2_s[:, hc * D:(hc + 1) * D], in_=w2m[hc * 128:(hc + 1) * 128, :]))
        for hc in range(32)]

    P.op("dv", lambda e: e.tensor_tensor_scan(
        out=globB[:], data0=cTB[:], data1=cTB[:], initial=0.0, op0=ALU.add,
        op1=ALU.bypass), deps=tloadB)
    P.op("dv", lambda e: e.tensor_tensor(out=globB[:], in0=globB[:], in1=cTB[:],
                                         op=ALU.subtract))
    t = P.op("dv", lambda e: e.tensor_tensor(out=scrB[0:16, 0:1], in0=globB[:, 7:8],
                                             in1=cTB[:, 7:8], op=ALU.add))
    tm = P.op("pe", lambda e: e.matmul(tB_ps[:, 0:1], s16B[:], scrB[0:16, 0:1],
                                       start=True, stop=True), deps=[t, tctl], tok=True)
    tg = P.op("dv", lambda e: e.tensor_scalar(
        out=globB[:], in0=globB[:], scalar1=tB_ps[:, 0:1], scalar2=None, op0=ALU.add),
        deps=[tm])
    tm = P.op("pe", lambda e: e.matmul(tB_ps[0:1, :], exrB[:, 0:1], globB[:],
                                       start=True, stop=True), deps=[tg], tok=True)
    t = P.op("dv", lambda e: e.tensor_copy(out=sbrow[0:1, 0:8], in_=tB_ps[0:1, :]),
             deps=[tm])
    tm = P.op("pe", lambda e: e.matmul(tB_ps[0:1, :], exrB[:, 1:2], globB[:],
                                       start=True, stop=True), deps=[t], tok=True)
    t = P.op("dv", lambda e: e.tensor_copy(out=sbrow[0:1, 8:16], in_=tB_ps[0:1, :]),
             deps=[tm])
    t2 = P.op("dv", lambda e: e.tensor_tensor(out=scrB[0:16, 1:2], in0=globB[:, 7:8],
                                              in1=cTB[:, 7:8], op=ALU.add))
    tm = P.op("pe", lambda e: e.matmul(tB_ps[0:1, 0:1], exrB[:, 1:2], scrB[0:16, 1:2],
                                       start=True, stop=True), deps=[t2], tok=True)
    ttot = P.op("dv", lambda e: e.tensor_copy(out=sbrow[0:1, 16:17], in_=tB_ps[0:1, 0:1]),
                deps=[tm])
    P.op("dv", lambda e: e.tensor_tensor_scan(
        out=lbase[:], data0=c2B[:], data1=c2B[:], initial=0.0, op0=ALU.add,
        op1=ALU.bypass))
    P.op("dv", lambda e: e.tensor_tensor(out=lbase[:], in0=lbase[:], in1=c2B[:],
                                         op=ALU.subtract))
    tsb = None
    for kk in range(2):
        P.op("dv", lambda e, kk=kk: e.tensor_tensor(
            out=scrB[0:8, 0:16], in0=lbase[:], in1=exsB[:, kk * 16:(kk + 1) * 16],
            op=ALU.mult))
        P.op("dv", lambda e, kk=kk: e.reduce_sum(out=srcb[:, kk:kk + 1],
                                                 in_=scrB[0:8, 0:16], axis=AX.X))
        tsb = P.op("dv", lambda e, kk=kk: e.tensor_tensor(
            out=srcb[:, kk:kk + 1], in0=srcb[:, kk:kk + 1], in1=iodB[:], op=ALU.add))
    prev = None
    for kk in range(2):
        tt = P.op("pe", lambda e, kk=kk: e.transpose(
            rB_ps[0:1, :], srcb[:, kk:kk + 1], id_f[0:8, 0:8]),
            deps=[tsb, prev], tok=True)
        prev = P.op("dv", lambda e, kk=kk: e.tensor_copy(
            out=srcrow[0:1, kk * 8:(kk + 1) * 8], in_=rB_ps[0:1, :]), deps=[tt])
    tm = P.op("pe", lambda e: e.matmul(t1B_ps[:], ones_f[:], sbrow[0:1, 0:16],
                                       start=True, stop=True), deps=[ttot, prev], tok=True)
    tSB = P.op("dv", lambda e: e.tensor_copy(out=sbB16[:], in_=t1B_ps[:]), deps=[tm])
    tm = P.op("pe", lambda e: e.matmul(t1B_ps[:], ones_f[:], srcrow[0:1, :],
                                       start=True, stop=True), deps=[tSB], tok=True)
    P.op("dv", lambda e: e.tensor_copy(out=diffB[:], in_=t1B_ps[:]), deps=[tm])
    tDF = P.op("dv", lambda e: e.tensor_tensor(out=diffB[:], in0=diffB[:], in1=sbB16[:],
                                               op=ALU.subtract))
    tm = P.op("pe", lambda e: e.matmul(t1B_ps[:, 0:1], ones_f[:], sbrow[0:1, 16:17],
                                       start=True, stop=True), deps=[tDF], tok=True)
    tTB = P.op("dv", lambda e: e.tensor_copy(out=totB[:], in_=t1B_ps[:, 0:1]), deps=[tm])
    tt = P.op("pe", lambda e: e.transpose(b1B_ps[:], b1load[:], id_f[0:32, 0:32]),
              deps=tloadB + [tTB], tok=True)
    tb1c = P.op("dv", lambda e: e.tensor_copy(out=b1c[:], in_=b1B_ps[:]), deps=[tt])
    b2b_toks = []
    tprev2 = tb1c
    for half in range(2):
        tm = P.op("pe", lambda e, half=half: e.matmul(
            bbB_ps[:], ones_f[:], b2Bf[0:1, half * 512:(half + 1) * 512],
            start=True, stop=True, skip_group_check=True), deps=[tprev2], tok=True)
        tprev2 = P.op("dv", lambda e, half=half: e.tensor_copy(
            out=b2Bf[:, half * 512:(half + 1) * 512], in_=bbB_ps[:]), deps=[tm])
        b2b_toks.append(tprev2)
    tinv0 = P.op("gp", lambda e: e.memset(invc[:], 0.0))

    bcB = {}
    disp_st = []
    relD = {}
    for t_ in range(32):
        P.op("dv", lambda e, t_=t_: e.tensor_scalar(
            out=colsB[:, 0:1], in0=io128[:], scalar1=float(t_ * 128), scalar2=None,
            op0=ALU.add), deps=[tSB, tDF, tTB])
        P.op("dv", lambda e: e.tensor_scalar(
            out=scrB[:, 0:16], in0=sbB16[:], scalar1=colsB[:, 0:1], scalar2=None,
            op0=ALU.is_le))
        P.op("dv", lambda e: e.tensor_tensor(
            out=scrB[:, 16:31], in0=scrB[:, 0:15], in1=scrB[:, 1:16], op=ALU.subtract))
        P.op("dv", lambda e: e.tensor_copy(out=scrB[:, 31:32], in_=scrB[:, 15:16]))
        P.op("dv", lambda e: e.tensor_tensor(
            out=scrB[:, 16:32], in0=scrB[:, 16:32], in1=diffB[:], op=ALU.mult))
        P.op("dv", lambda e: e.reduce_sum(out=colsB[:, 1:2], in_=scrB[:, 16:32], axis=AX.X))
        P.op("dv", lambda e: e.tensor_tensor(
            out=colsB[:, 1:2], in0=colsB[:, 1:2], in1=colsB[:, 0:1], op=ALU.add))
        P.op("dv", lambda e: e.tensor_scalar(
            out=colsB[:, 2:3], in0=colsB[:, 0:1], scalar1=totB[:, 0:1], scalar2=None,
            op0=ALU.is_lt))
        P.op("dv", lambda e: e.tensor_scalar(
            out=colsB[:, 3:4], in0=colsB[:, 2:3], scalar1=-1e9, scalar2=1e9,
            op0=ALU.mult, op1=ALU.add))
        P.op("dv", lambda e: e.tensor_tensor(
            out=colsB[:, 1:2], in0=colsB[:, 1:2], in1=colsB[:, 3:4], op=ALU.add))
        tic = P.op("dv", lambda e, t_=t_: e.tensor_copy(out=ticB[:, t_:t_ + 1],
                                                        in_=colsB[:, 1:2]))

        def g1(e, t_=t_):
            if "r1" not in bcB:
                bcB["r1"] = e.to_reg(8 * CAP)
            return e.indirect_dma_start(
                out=invc[:, t_:t_ + 1], out_offset=None, in_=lsort_all,
                in_offset=bass.IndirectOffsetOnAxis(ticB[:, t_:t_ + 1], 0),
                bounds_check=bcB["r1"], oob_is_err=False)
        tg1 = P.dma("gp", g1, deps=[tic, tinv0, t_ag_ls])
        P.op("dv", lambda e, t_=t_: e.tensor_scalar(
            out=colsB[:, 4:5], in0=invc[:, t_:t_ + 1], scalar1=-1.0, scalar2=None,
            op0=ALU.add), deps=[tg1])
        P.op("dv", lambda e, t_=t_: e.tensor_scalar(
            out=colsB[:, 5:6], in0=invc[:, t_:t_ + 1], scalar1=0.5, scalar2=None,
            op0=ALU.is_ge))
        P.op("dv", lambda e: e.tensor_scalar(
            out=colsB[:, 6:7], in0=colsB[:, 5:6], scalar1=-1e9, scalar2=1e9,
            op0=ALU.mult, op1=ALU.add))
        P.op("dv", lambda e: e.tensor_tensor(
            out=colsB[:, 4:5], in0=colsB[:, 4:5], in1=colsB[:, 6:7], op=ALU.add))
        tic2 = P.op("dv", lambda e, t_=t_: e.tensor_copy(out=ticB[:, 32 + t_:33 + t_],
                                                         in_=colsB[:, 4:5]))
        dstB = dispstg[:, 0:D]
        tms = P.op("gp", lambda e, dstB=dstB: e.memset(dstB, 0.0), deps=[relD.get(0)])

        def g2(e, t_=t_, dstB=dstB):
            if "r2" not in bcB:
                bcB["r2"] = e.to_reg(8 * TQ)
            return e.indirect_dma_start(
                out=dstB, out_offset=None, in_=h2_all,
                in_offset=bass.IndirectOffsetOnAxis(ticB[:, 32 + t_:33 + t_], 0),
                bounds_check=bcB["r2"], oob_is_err=False)
        tg2 = P.dma("gp", g2, deps=[tic2, tms, t_ag_h2])
        ts = P.dma("sy", lambda e, t_=t_, dstB=dstB: e.dma_start(
            out=disp_dram[t_ * 128:(t_ + 1) * 128, :], in_=dstB), deps=[tg2])
        relD[0] = ts
        disp_st.append(ts)

    dT_tok = [P.dma("sy", lambda e, dc=dc: e.dma_start(
        out=dT[:, dc * CAP:(dc + 1) * CAP],
        in_=disp_dram[:, dc * 128:(dc + 1) * 128], transpose=True), deps=disp_st)
        for dc in range(8)]

    phB2.close()
    phBp.close()
    phB4 = contextlib.ExitStack()
    h_ps = [phB4.enter_context(nc.psum_tensor(f"h_ps{p}", [128, 384], F32))
            for p in range(2)]
    y_ps = [phB4.enter_context(nc.psum_tensor(f"y_ps{j}", [128, 1024], F32))
            for j in range(3)]
    SGS = [(i * 384, min(384, CAP - i * 384)) for i in range((CAP + 383) // 384)]
    yst_prev = {}
    yB_toks = []
    gel_prev = {}
    w2_last = {}
    for sgi, (s0, n) in enumerate(SGS):
        nj = (n + 127) // 128
        tmm2 = None
        for hc in range(32):
            hp = hc % 2
            tm1 = None
            for dc in range(8):
                tm1 = P.op("pe", lambda e, hc=hc, dc=dc, s0=s0, n=n, hp=hp: e.matmul(
                    h_ps[hp][:, 0:n], w1_s[:, hc * D + dc * 128:hc * D + (dc + 1) * 128],
                    dT[:, dc * CAP + s0:dc * CAP + s0 + n],
                    start=(dc == 0), stop=(dc == 7), skip_group_check=True),
                    deps=[dT_tok[dc], w1tok[hc], gel_prev.get(hp)], tok=(dc == 7))
            tgel = P.op("ac", lambda e, hc=hc, n=n, hp=hp: e.activation(
                out=hstg[hp][:, 0:n], in_=h_ps[hp][:, 0:n], func=AF.Gelu,
                bias=b1c[:, hc:hc + 1]), deps=[tm1, tb1c, w2_last.get(hp)])
            gel_prev[hp] = tgel
            for j in range(nj):
                w = min(128, n - j * 128)
                for half in range(2):
                    tmm2 = P.op("pe", lambda e, hc=hc, j=j, w=w, half=half, hp=hp: e.matmul(
                        y_ps[j][:, half * 512:(half + 1) * 512],
                        hstg[hp][:, j * 128:j * 128 + w],
                        w2_s[:, hc * D + half * 512:hc * D + (half + 1) * 512],
                        start=(hc == 0), stop=(hc == 31), skip_group_check=True),
                        deps=[tgel, w2tok[hc]] + ([yst_prev.get(j)] if hc == 0 else []),
                        tok=(j == nj - 1 and half == 1))
            w2_last[hp] = tmm2
        tsl = None
        for j in range(nj):
            w = min(128, n - j * 128)
            tev = P.op("dv", lambda e, j=j, w=w: e.tensor_tensor(
                out=ystg[0:w, :], in0=y_ps[j][0:w, :],
                in1=b2Bf[0:w, :], op=ALU.add),
                deps=[tmm2, tsl] + b2b_toks)
            ts = P.dma("sy", lambda e, s0=s0, j=j, w=w: e.dma_start(
                out=y_loc[s0 + j * 128:s0 + j * 128 + w, :],
                in_=ystg[0:w, :]), deps=[tev])
            yst_prev[j] = ts
            tsl = ts
            yB_toks.append(ts)

    t_ag_y = P.cc(lambda e: e.collective_compute(
        "AllGather", ALU.bypass, replica_groups=RG,
        ins=[y_loc.bitcast(F32).opt()], outs=[y_all.bitcast(F32).opt()]),
        deps=yB_toks)

    P.base_deps = P.barrier()
    phB4.close()
    phB.close()
    phC = contextlib.ExitStack()
    sbC = lambda n, s, dt=F32: phC.enter_context(nc.sbuf_tensor(n, s, dt))
    ctC = sbC("ctC", [16, 8])
    c2C = sbC("c2C", [8, 16])
    rkoC = sbC("rkoC", [16, 8])
    globC = sbC("globC", [16, 8])
    offcC = sbC("offcC", [16, 1])
    offrowC = sbC("offrowC", [1, 16])
    offBC = sbC("offBC", [128, 16])
    m0C = sbC("m0C", [128, 8])
    colsC = sbC("colsC", [128, 8])
    ticC = sbC("ticC", [128, 32], I32)
    y0s = sbC("y0s", [128, D], BF16)
    y1s = sbC("y1s", [128, D], BF16)
    accC = sbC("accC", [128, D])
    x2stg = sbC("x2stg", [128, 2 * D])
    ostg = sbC("ostg", [128, 2 * D])
    s16C = sbC("s16C", [16, 16])

    phCp = contextlib.ExitStack()
    tC_ps = phCp.enter_context(nc.psum_tensor("tC_ps", [16, 8], F32))
    rC_ps = phCp.enter_context(nc.psum_tensor("rC_ps", [1, 16], F32))
    bBC_ps = phCp.enter_context(nc.psum_tensor("bBC_ps", [128, 512], F32))

    tloadC = [P.dma("sy", lambda e, a=a, b_=b_: e.dma_start(out=a, in_=b_))
              for a, b_ in [(s16C[:], s16), (rkoC[:], rkoh)]]
    tc2lC = P.dma("sy", lambda e: e.dma_start(out=c2C[:], in_=cnt_all_d),
                  deps=[t_ag_cnt])
    ttrC = P.op("pe", lambda e: e.transpose(tC_ps[:], c2C[0:8, 0:16], id_f[0:8, 0:8]),
                deps=[tc2lC], tok=True)
    tctC = P.op("dv", lambda e: e.tensor_copy(out=ctC[:], in_=tC_ps[:]), deps=[ttrC])
    tloadC += [tc2lC, tctC]
    P.op("dv", lambda e: e.tensor_tensor_scan(
        out=globC[:], data0=ctC[:], data1=ctC[:], initial=0.0, op0=ALU.add,
        op1=ALU.bypass), deps=tloadC)
    P.op("dv", lambda e: e.tensor_tensor(out=globC[:], in0=globC[:], in1=ctC[:],
                                         op=ALU.subtract))
    t = P.op("dv", lambda e: e.tensor_tensor(out=offcC[:], in0=globC[:, 7:8],
                                             in1=ctC[:, 7:8], op=ALU.add))
    tm = P.op("pe", lambda e: e.matmul(tC_ps[:, 0:1], s16C[:], offcC[:],
                                       start=True, stop=True), deps=[t, tctC], tok=True)
    tg = P.op("dv", lambda e: e.tensor_scalar(
        out=globC[:], in0=globC[:], scalar1=tC_ps[:, 0:1], scalar2=None, op0=ALU.add),
        deps=[tm])
    P.op("dv", lambda e: e.tensor_tensor(out=ctC[:], in0=globC[:], in1=rkoC[:],
                                         op=ALU.mult), deps=[tg])
    toff = P.op("dv", lambda e: e.reduce_sum(out=offcC[:], in_=ctC[:], axis=AX.X))
    tt = P.op("pe", lambda e: e.transpose(rC_ps[0:1, :], offcC[:], id_f[0:16, 0:16]),
              deps=[toff], tok=True)
    t = P.op("dv", lambda e: e.tensor_copy(out=offrowC[0:1, :], in_=rC_ps[0:1, :]),
             deps=[tt])
    tm = P.op("pe", lambda e: e.matmul(bBC_ps[:, 0:16], ones_f[:], offrowC[0:1, :],
                                       start=True, stop=True, skip_group_check=True),
              deps=[t], tok=True)
    tOF = P.op("dv", lambda e: e.tensor_copy(out=offBC[:], in_=bBC_ps[:, 0:16]), deps=[tm])
    ty0 = P.op("gp", lambda e: e.memset(y0s[:], 0.0))
    ty1 = P.op("gp", lambda e: e.memset(y1s[:], 0.0))

    bcC = {}
    relX, relO = {}, {}
    rel_y = None
    for tc in range(16):
        par = tc % 2
        tx = P.dma("sy", lambda e, tc=tc, par=par: e.dma_start(
            out=x2stg[:, par * D:(par + 1) * D],
            in_=x2_dram[tc * 128:(tc + 1) * 128, :]), deps=[relX.get(("x", par))])
        rr = rt_cols[:, tc * 8:(tc + 1) * 8]
        wcol = {}
        for kk in range(2):
            ecol = rr[:, kk:kk + 1]
            vcol = rr[:, 2 + kk:3 + kk]
            pcol = rr[:, 4 + kk:5 + kk]
            P.op("dv", lambda e, ecol=ecol: e.tensor_scalar(
                out=m0C[:], in0=iotaE[:], scalar1=ecol, scalar2=None, op0=ALU.is_equal),
                deps=[tOF])
            P.op("dv", lambda e, kk=kk: e.tensor_tensor(
                out=m0C[:], in0=m0C[:], in1=offBC[:, kk * 8:(kk + 1) * 8], op=ALU.mult))
            P.op("dv", lambda e: e.reduce_sum(out=colsC[:, 0:1], in_=m0C[:], axis=AX.X))
            P.op("dv", lambda e, pcol=pcol: e.tensor_tensor(
                out=colsC[:, 0:1], in0=colsC[:, 0:1], in1=pcol, op=ALU.add))
            P.op("dv", lambda e: e.tensor_scalar(
                out=colsC[:, 1:2], in0=colsC[:, 0:1], scalar1=float(CAP), scalar2=None,
                op0=ALU.is_lt))
            tw = P.op("dv", lambda e, vcol=vcol, kk=kk: e.tensor_tensor(
                out=colsC[:, 4 + kk:5 + kk], in0=vcol, in1=colsC[:, 1:2], op=ALU.mult))
            wcol[kk] = tw
            P.op("dv", lambda e, ecol=ecol: e.tensor_scalar(
                out=colsC[:, 2:3], in0=ecol, scalar1=float(CAP), scalar2=None,
                op0=ALU.mult))
            P.op("dv", lambda e: e.tensor_tensor(
                out=colsC[:, 2:3], in0=colsC[:, 2:3], in1=colsC[:, 0:1], op=ALU.add))
            P.op("dv", lambda e: e.tensor_scalar(
                out=colsC[:, 3:4], in0=colsC[:, 1:2], scalar1=-1e9, scalar2=1e9,
                op0=ALU.mult, op1=ALU.add))
            P.op("dv", lambda e: e.tensor_tensor(
                out=colsC[:, 2:3], in0=colsC[:, 2:3], in1=colsC[:, 3:4], op=ALU.add))
            tic = P.op("dv", lambda e, tc=tc, kk=kk: e.tensor_copy(
                out=ticC[:, tc * 2 + kk:tc * 2 + kk + 1], in_=colsC[:, 2:3]))

            def gy(e, tc=tc, kk=kk):
                if "r" not in bcC:
                    bcC["r"] = e.to_reg(8 * CAP)
                return e.indirect_dma_start(
                    out=(y0s if kk == 0 else y1s)[:], out_offset=None, in_=y_all,
                    in_offset=bass.IndirectOffsetOnAxis(
                        ticC[:, tc * 2 + kk:tc * 2 + kk + 1], 0),
                    bounds_check=bcC["r"], oob_is_err=False)
            tgy = P.dma("gp", gy, deps=[tic, ty0, ty1, rel_y, t_ag_y])
            wcol[(kk, "g")] = tgy
        ta = P.op("dv", lambda e: e.tensor_scalar(
            out=accC[:], in0=y0s[:], scalar1=colsC[:, 4:5], scalar2=None, op0=ALU.mult),
            deps=[wcol[(0, "g")], wcol[(1, "g")], wcol[0], wcol[1]])
        ta = P.op("dv", lambda e: e.scalar_tensor_tensor(
            out=accC[:], in0=y1s[:], scalar=colsC[:, 5:6], in1=accC[:],
            op0=ALU.mult, op1=ALU.add))
        rel_y = ta
        to1 = P.op("dv", lambda e: e.tensor_tensor(
            out=accC[:], in0=accC[:], in1=gbB[:], op=ALU.mult))
        to2 = P.op("dv", lambda e, par=par: e.tensor_tensor(
            out=ostg[:, par * D:(par + 1) * D], in0=accC[:],
            in1=x2stg[:, par * D:(par + 1) * D], op=ALU.add),
            deps=[tx, relO.get(par)])
        relX[("x", par)] = to2
        ts = P.dma("sy", lambda e, tc=tc, par=par: e.dma_start(
            out=outo[tc * 128:(tc + 1) * 128, :], in_=ostg[:, par * D:(par + 1) * D]),
            deps=[to2])
        relO[par] = ts
        ST.append(ts)

    blk = es.enter_context(nc.Block())
    P.emit(blk)
    P.tail_wait(blk, ST)
    return nc, es


HOSTC = {}


def host_consts():
    if HOSTC:
        return HOSTC
    import ml_dtypes
    bf = ml_dtypes.bfloat16
    HOSTC["idf32"] = np.eye(128, dtype=np.float32)
    HOSTC["idbf"] = np.eye(128, dtype=np.float32).astype(bf)
    HOSTC["onesf32"] = np.ones((1, 128), np.float32)
    HOSTC["onesbf"] = np.ones((1, 128), bf)
    HOSTC["ones4k_bf"] = np.ones((1, TK), bf)
    HOSTC["ut_bf"] = np.triu(np.ones((128, 128), np.float32), 1).astype(bf)
    HOSTC["all1_bf"] = np.ones((128, 128), bf)
    eqk = np.zeros((128, 8 * 16), np.float32)
    for oc in range(8):
        for p in range(128):
            eqk[p, oc * 16 + 2 * oc + p // 64] = 1.0
    HOSTC["eq_k"] = eqk.astype(bf)
    eqr = np.zeros((128, 4 * 16), np.float32)
    for oc in range(4):
        for p in range(128):
            eqr[p, oc * 16 + 4 * oc + p // 32] = 1.0
    HOSTC["eq_r"] = eqr.astype(bf)
    HOSTC["iotaEi"] = np.tile(np.arange(8).astype(np.float32), (128, 1))
    return HOSTC


def host_inputs_a(inputs, core):
    hc = host_consts()
    b, qh = core // 2, core % 2
    x = np.asarray(inputs["L0_attn_x"][b])
    x_kv = np.concatenate([x[qh * TQ:(qh + 1) * TQ], x[(1 - qh) * TQ:(2 - qh) * TQ]], 0)
    c = np.asarray(inputs["L0_c"][b])
    gtok = (core * TQ + np.arange(TQ).reshape(16, 128).T + 1).astype(np.float32)
    d = {
        "x_kv": np.ascontiguousarray(x_kv),
        "c_colT": np.ascontiguousarray(c.reshape(8, 128).T),
        "wmod": np.asarray(inputs["L0_wmod"]),
        "wq": np.asarray(inputs["L0_attn_Wq"]), "wk": np.asarray(inputs["L0_attn_Wk"]),
        "wv": np.asarray(inputs["L0_attn_Wv"]), "wr": np.asarray(inputs["L0_attn_Wr"]),
        "wo": np.asarray(inputs["L0_attn_Wo"]),
        "bq": np.asarray(inputs["L0_attn_bq"]).reshape(1, -1),
        "bk": np.asarray(inputs["L0_attn_bk"]).reshape(1, -1),
        "bv": np.asarray(inputs["L0_attn_bv"]).reshape(1, -1),
        "br": np.asarray(inputs["L0_attn_br"]).reshape(1, -1),
        "bo": np.asarray(inputs["L0_attn_bo"]).reshape(1, -1),
        "wg_t": np.ascontiguousarray(
            np.asarray(inputs["L0_ffn_wg"]).reshape(8, 128, 8).transpose(1, 0, 2).reshape(128, 64)),
        "gtokp1": np.ascontiguousarray(gtok),
    }
    for k in ("idf32", "idbf", "onesf32", "onesbf", "ones4k_bf", "ut_bf", "all1_bf",
              "eq_k", "eq_r", "iotaEi"):
        d[k] = hc[k]
    return d


def host_inputs(inputs, core):
    d = host_inputs_a(inputs, core)
    w1 = np.asarray(inputs["L0_ffn_w1"][core])
    d["w1t"] = np.ascontiguousarray(
        w1.reshape(8, 128, 32, 128).transpose(2, 1, 0, 3).reshape(32 * 128, 1024))
    d["w2m"] = np.asarray(inputs["L0_ffn_w2"][core])
    d["b1m"] = np.asarray(inputs["L0_ffn_b1"][core]).reshape(32, 128)
    d["b2r"] = np.asarray(inputs["L0_ffn_b2"][core]).reshape(1, -1)
    s16 = np.zeros((16, 16), np.float32)
    s16[np.arange(8), 8 + np.arange(8)] = 1.0
    d["s16"] = s16
    exr0 = np.zeros((16, 1), np.float32); exr0[core, 0] = 1.0
    exr1 = np.zeros((16, 1), np.float32); exr1[8 + core, 0] = 1.0
    exs0 = np.zeros((8, 16), np.float32); exs0[:, core] = 1.0
    exs1 = np.zeros((8, 16), np.float32); exs1[:, 8 + core] = 1.0
    d["exr0"], d["exr1"], d["exs0"], d["exs1"] = exr0, exr1, exs0, exs1
    d["iotad"] = (np.arange(8, dtype=np.float32) * CAP).reshape(8, 1)
    d["iota128"] = np.arange(128, dtype=np.float32).reshape(128, 1)
    rkoh = np.zeros((16, 8), np.float32); rkoh[:, core] = 1.0
    d["rkoh"] = rkoh
    return d


_CACHE = {}


def kernel(**inputs):
    if "F" not in _CACHE:
        _CACHE["F"] = build_fused()
    ncF, _ = _CACHE["F"]
    cores = list(range(8))
    res = run_bass_kernel_spmd(ncF, [host_inputs(inputs, c) for c in cores],
                               core_ids=cores)
    out = np.zeros((4, 4096, 1024), np.float32)
    for c in cores:
        b, qh = c // 2, c % 2
        out[b, qh * 2048:(qh + 1) * 2048] = np.asarray(res.results[c]["outo"])
    return out
